# version 09
# speedup vs baseline: 1.3406x; 1.0127x over previous
import numpy as np
import ml_dtypes

import concourse.bass as bass
import concourse.mybir as mybir
from concourse.bass_utils import run_bass_kernel_spmd
from contextlib import ExitStack

F32 = mybir.dt.float32
BF16 = mybir.dt.bfloat16
F16 = mybir.dt.float16
U16 = mybir.dt.uint16
U8 = mybir.dt.uint8
U32 = mybir.dt.uint32
I32 = mybir.dt.int32
FP8 = mybir.dt.float8e4
AF = mybir.ActivationFunctionType
ALU = mybir.AluOpType
AX = mybir.AxisListType

B, T, C = 4, 2048, 2048
H, D = 16, 128
FI = 5632
NTOK = 1024
TCH = 8
KCH = 16
ICH = 44
GU = 2 * FI
LN_EPS = 1e-5
GN_EPS = 1e-5
RMS_EPS = 1e-6

_STAGE = 3
_DEBUG = False

ENGINES = ("sync", "vector", "scalar", "tensor", "gpsimd")



class Prog:

    def __init__(self):
        self.items = []
        self.streams = set()

    def op(self, eng, fn, reads=(), writes=(), dma=None, inc=None, tag=""):
        if inc is None:
            inc = 16 if dma else 1
        if dma:
            self.streams.add(dma)
        self.items.append(["op", eng, fn, tuple(reads), tuple(writes), inc, tag,
                           None, None, dma])

    def barrier(self):
        self.items.append(["bar"])

    def schedule(self):
        cnt = {}
        writer = {}
        readers = {}
        bar_pend = {e: {} for e in ENGINES}
        for it in self.items:
            if it[0] == "bar":
                snap = dict(cnt)
                for e in ENGINES:
                    for s2, v in snap.items():
                        if s2 != e and v > 0 and v > bar_pend[e].get(s2, -1):
                            bar_pend[e][s2] = v
                continue
            _, eng, fn, reads, writes, inc, tag, _, _, stream = it
            sem = stream if stream else eng
            need = {}

            def add(dep):
                s2, v = dep
                if s2 == eng and s2 == "tensor":
                    return
                if v > need.get(s2, -1):
                    need[s2] = v
            for k in reads:
                w = writer.get(k)
                if w:
                    add(w)
            for k in writes:
                w = writer.get(k)
                if w:
                    add(w)
                for r in readers.get(k, ()):
                    add(r)
            if stream and cnt.get(sem, 0) > 0:
                if cnt[sem] > need.get(sem, -1):
                    need[sem] = cnt[sem]
            for s2, v in bar_pend[eng].items():
                if v > need.get(s2, -1):
                    need[s2] = v
            bar_pend[eng] = {}
            cnt[sem] = cnt.get(sem, 0) + inc
            it[7] = cnt[sem]
            it[8] = need
            for k in writes:
                writer[k] = (sem, cnt[sem])
                readers[k] = []
            for k in reads:
                readers.setdefault(k, []).append((sem, cnt[sem]))
        hi = {}
        for it in self.items:
            if it[0] == "bar":
                continue
            eng, need = it[1], it[8]
            pruned = {}
            for s2, v in need.items():
                if hi.get((eng, s2), -1) >= v:
                    continue
                pruned[s2] = v
                hi[(eng, s2)] = v
            it[8] = pruned

    def emit_stream(self, eng_name, eng_obj, sems):
        for it in self.items:
            if it[0] == "bar" or it[1] != eng_name:
                continue
            for s2, v in it[8].items():
                eng_obj.wait_ge(sems[s2], v)
            sem = it[9] if it[9] else eng_name
            it[2](eng_obj).then_inc(sems[sem], it[5])

    def stats(self):
        n = {e: 0 for e in ENGINES}
        w = {e: 0 for e in ENGINES}
        for it in self.items:
            if it[0] == "bar":
                continue
            n[it[1]] += 1
            w[it[1]] += len(it[8])
        return n, w



def _host_tables(half):
    f = np.float64
    ds = (np.arange(0, C, 2, dtype=f) + 0.4 * C) / (1.4 * C)
    pos = np.arange(T, dtype=f)
    scale = ds[None, :] ** (pos[:, None] / 512.0)
    inv_freq = 1.0 / 10000.0 ** (np.arange(0, C, 2, dtype=f) / C)
    sinus = pos[:, None] * inv_freq[None, :]
    sin = np.repeat(np.sin(sinus) * scale, 2, axis=-1)
    cos = np.repeat(np.cos(sinus) * scale, 2, axis=-1)
    gamma = 1.0 - 2.0 ** (-5.0 - np.arange(H, dtype=f))
    gch = np.repeat(gamma, D)
    t0 = half * NTOK
    tl = np.arange(t0, t0 + NTOK, dtype=f)
    qfac = gch[:, None] ** (tl - t0)[None, :] * D ** -0.5
    kfac = gch[:, None] ** (t0 - tl)[None, :]
    cq = cos[t0:t0 + NTOK].T * qfac
    sq = sin[t0:t0 + NTOK].T * qfac
    ck = cos[t0:t0 + NTOK].T * kfac
    sk = sin[t0:t0 + NTOK].T * kfac
    return tuple(np.ascontiguousarray(a.astype(np.float32)) for a in (cq, sq, ck, sk))


def _host_consts():
    bf = ml_dtypes.bfloat16
    PT = np.zeros((128, 128), dtype=np.float32)
    ii = np.arange(0, 128, 2)
    PT[ii + 1, ii] = -1.0
    PT[ii, ii + 1] = 1.0
    tri = (np.arange(128)[None, :] >= np.arange(128)[:, None]).astype(np.float32)
    sel = np.zeros((128, 64), dtype=np.float32)
    for j in range(8):
        sel[:, j * 8 + j] = 1.0
    brc4 = np.zeros((4, 4 * 128), dtype=np.float32)
    for r in range(4):
        brc4[r, r * 128:(r + 1) * 128] = 1.0
    eye = np.eye(128, dtype=np.float32)
    return (PT.astype(bf), tri.astype(bf), sel.astype(bf), brc4.astype(bf), eye.astype(bf))


GAMMA_P1024 = [float((1.0 - 2.0 ** (-5.0 - h)) ** 1024) for h in range(H)]



def build_program(stage):
    nc = bass.Bass("TRN2", target_bir_lowering=False, debug=False)
    es = ExitStack()

    def din(name, shape, dt):
        return nc.dram_tensor(name, shape, dt, kind="ExternalInput").ap()

    x_in = din("x_own", [NTOK, C], F32)
    w_qkv = din("w_qkv", [C, 3 * C], F32)
    w_g = din("w_g", [C, C], F32)
    w_proj = din("w_proj", [C, C], F32)
    gnw_in = din("gnw", [128, H], F32)
    gnb_in = din("gnb", [128, H], F32)
    gu_pk = din("gu_pk", [GU, C // 8], I32)
    gu_sc = din("gu_sc", [GU, KCH], F16)
    dn_pk = din("dn_pk", [C, FI // 8], I32)
    dn_sc = din("dn_sc", [C, ICH], F16)
    cq_in = din("cq", [C, NTOK], F32)
    sq_in = din("sq", [C, NTOK], F32)
    ck_in = din("ck", [C, NTOK], F32)
    sk_in = din("sk", [C, NTOK], F32)
    pt_in = din("pt", [128, 128], BF16)
    tri_in = din("tri", [128, 128], BF16)
    sel_in = din("sel", [128, 64], BF16)
    eye_in = din("eye", [128, 128], BF16)
    brc4_in = din("brc4", [4, 4 * 128], BF16)
    alpha_in = din("alpha", [128, 1], F32)

    out_d = nc.dram_tensor("out", [NTOK, C], F32, kind="ExternalOutput").ap()

    gu_dec = nc.dram_tensor("gu_dec", [GU, C], BF16).ap()
    dn_dec = nc.dram_tensor("dn_dec", [C, FI], BF16).ap()
    x1_d = nc.dram_tensor("x1_d", [NTOK, C], F32).ap()
    x2_d = nc.dram_tensor("x2_d", [NTOK, C], F32).ap()
    wqkv_bf = nc.dram_tensor("wqkv_bf", [C, 3 * C], BF16).ap()
    wg_bf = nc.dram_tensor("wg_bf", [C, C], BF16).ap()
    vk_d = nc.dram_tensor("vk_d", [2 * 128, NTOK], BF16).ap()
    og_d = nc.dram_tensor("og_d", [C, NTOK], BF16).ap()
    strow_d = nc.dram_tensor("strow_d", [8, NTOK], F32).ap()
    wproj_bf = nc.dram_tensor("wproj_bf", [C, C], BF16).ap()
    guT_d = nc.dram_tensor("guT_d", [C, GU], BF16).ap()
    dnT_d = nc.dram_tensor("dnT_d", [FI, C], BF16).ap()
    cc_i = [nc.dram_tensor(f"cc_i{g}", [4 * D, D], F32).ap() for g in range(4)]
    cc_o = [nc.dram_tensor(f"cc_o{g}", [2 * 4 * D, D], F32).ap() for g in range(4)]

    P = Prog()

    with es:
        def sb(name, shape, dt):
            return es.enter_context(nc.sbuf_tensor(name, shape, dt))

        sems = {e: es.enter_context(nc.semaphore(f"se_{e}")) for e in ENGINES}

        xlnT = sb("xlnT", [128, KCH * NTOK], BF16)
        arA = sb("arA", [128, ICH * NTOK], BF16)
        arB = sb("arB", [128, 27296], BF16)
        consts = sb("consts", [128, 448], BF16)
        brc4_s = sb("brc4_s", [4, 4 * 128], BF16)
        alpha = sb("alpha_s", [128, 1], F32)
        gnwb = sb("gnwb", [128, 2 * H], F32)
        x_ch = sb("x_ch", [128, C], F32)
        xln_tm = sb("xln_tm", [128, C], BF16)
        sg_f = sb("sg_f", [128, NTOK], F32)
        st_a = sb("st_a", [128, 8], F32)
        st_b = sb("st_b", [128, 8], F32)
        st_c = sb("st_c", [128, 8], F32)
        st_n = sb("st_n", [128, 8], F32)

        p2a = es.enter_context(nc.psum_tensor("p2a", [128, 2048], F32))
        p2b = es.enter_context(nc.psum_tensor("p2b", [128, 2048], F32))

        pt_s = consts[:, 0:128]
        tri_s = consts[:, 128:256]
        sel_s = consts[:, 256:320]
        eye_s = consts[:, 320:448]

        for ap_dst, ap_src, key in ((consts[:, 0:128], pt_in, "c_pt"),
                                    (consts[:, 128:256], tri_in, "c_tri"),
                                    (consts[:, 256:320], sel_in, "c_sel"),
                                    (consts[:, 320:448], eye_in, "c_eye"),
                                    (brc4_s[:], brc4_in, "c_b4"),
                                    (alpha[:], alpha_in, "c_al"),
                                    (gnwb[:, 0:H], gnw_in, "c_gw"),
                                    (gnwb[:, H:2 * H], gnb_in, "c_gb")):
            P.op("sync", (lambda d, s: lambda e: e.dma_start(out=d, in_=s))(ap_dst, ap_src),
                 writes=(key,), dma="qconst", tag="const")


        def newton_rsqrt(dst, src, key, tag):
            di = dst.bitcast(U32)
            si = src.bitcast(U32)
            t = st_n[:, 0:dst.shape[1]]
            P.op("vector", lambda e: e.tensor_scalar(out=di, in0=si, scalar1=1, scalar2=None,
                                                     op0=ALU.logical_shift_right),
                 reads=(key,), writes=(key,), tag=tag)
            P.op("vector", lambda e: e.tensor_scalar(out=di, in0=di, scalar1=-1, scalar2=0x5F3759DF,
                                                     op0=ALU.mult, op1=ALU.add),
                 reads=(key,), writes=(key,), tag=tag)
            for _ in range(3):
                P.op("vector", lambda e: e.tensor_tensor(out=t, in0=dst, in1=dst, op=ALU.mult),
                     reads=(key,), writes=("st_n",), tag=tag)
                P.op("vector", lambda e: e.tensor_tensor(out=t, in0=t, in1=src, op=ALU.mult),
                     reads=("st_n", key), writes=("st_n",), tag=tag)
                P.op("vector", lambda e: e.tensor_scalar(out=t, in0=t, scalar1=-0.5, scalar2=1.5,
                                                         op0=ALU.mult, op1=ALU.add),
                     reads=("st_n",), writes=("st_n",), tag=tag)
                P.op("vector", lambda e: e.tensor_tensor(out=dst, in0=dst, in1=t, op=ALU.mult),
                     reads=(key, "st_n"), writes=(key,), tag=tag)

        def ln_apply(src_ap, src_key, eps, tag):
            skeys = src_key if isinstance(src_key, tuple) else (src_key,)
            src_key = skeys[0]
            su = st_a[:, 0:1]; mu = st_a[:, 1:2]; ssq = st_a[:, 2:3]
            var = st_b[:, 0:1]; rst = st_b[:, 1:2]; nmr = st_b[:, 2:3]
            t0 = st_c[:, 0:1]
            P.op("vector", lambda e: e.reduce_sum(out=su, in_=src_ap, axis=AX.X),
                 reads=skeys, writes=("st_a",), tag=tag)
            P.op("scalar", lambda e: e.activation(out=xln_tm[:], in_=src_ap,
                                                  func=AF.Square, accum_out=ssq),
                 reads=skeys, writes=("xln_tm", "st_a"), tag=tag)
            P.op("vector", lambda e: e.tensor_scalar(out=mu, in0=su, scalar1=1.0 / C,
                                                     scalar2=None, op0=ALU.mult),
                 reads=("st_a",), writes=("st_a",), tag=tag)
            P.op("vector", lambda e: e.tensor_tensor(out=t0, in0=mu, in1=mu, op=ALU.mult),
                 reads=("st_a",), writes=("st_c",), tag=tag)
            P.op("vector", lambda e: e.tensor_scalar(out=var, in0=ssq, scalar1=1.0 / C,
                                                     scalar2=eps, op0=ALU.mult, op1=ALU.add),
                 reads=("st_a",), writes=("st_b",), tag=tag)
            P.op("vector", lambda e: e.tensor_tensor(out=var, in0=var, in1=t0, op=ALU.subtract),
                 reads=("st_b", "st_c"), writes=("st_b",), tag=tag)
            newton_rsqrt(rst, var, "st_b", tag)
            P.op("vector", lambda e: e.tensor_scalar(out=nmr, in0=mu, scalar1=-1.0, scalar2=rst,
                                                     op0=ALU.mult, op1=ALU.mult),
                 reads=("st_a", "st_b"), writes=("st_b",), tag=tag)
            P.op("scalar", lambda e: e.activation(out=xln_tm[:], in_=src_ap, func=AF.Identity,
                                                  bias=nmr, scale=rst),
                 reads=skeys + ("st_b",), writes=("xln_tm",), tag=tag)

        def xlnT_build(tc, psum, pkeys, tag):
            def mm_tr(e):
                ins = None
                for kc in range(KCH):
                    ins = e.matmul(out=psum[:, kc * 128:(kc + 1) * 128],
                                   lhsT=xln_tm[:, kc * 128:(kc + 1) * 128],
                                   rhs=eye_s, start=True, stop=True)
                return ins
            P.op("tensor", mm_tr, reads=("xln_tm", "c_eye") + pkeys,
                 writes=pkeys, tag=tag)
            for kc in range(KCH):
                P.op("scalar", (lambda kc: lambda e: e.activation(
                    out=xlnT[:, kc * NTOK + tc * 128: kc * NTOK + (tc + 1) * 128],
                    in_=psum[:, kc * 128:(kc + 1) * 128], func=AF.Identity))(kc),
                    reads=pkeys, writes=(f"xlnT{kc}",), tag=tag)

        GB_O = 0
        DB_O = 16384
        TAB_O = 20480
        BK_O = 22016
        BK_SZ = 2640
        assert BK_O + 2 * BK_SZ <= 27296

        def bview(off, n, dt=BF16):
            sl = arB[:, off:off + n]
            if dt == BF16:
                return sl
            return sl.bitcast(dt)

        x_chb = x_ch[:].bitcast(BF16)

        def decode_tile(kind, args, tag, bank, phase="all"):
            if kind == "gu":
                ti = args
                n32, ngrp = 256, 16
                pk_src = gu_pk[ti * 128:(ti + 1) * 128, :]
                sc_src = gu_sc[ti * 128:(ti + 1) * 128, :]
                dec_dst = gu_dec[ti * 128:(ti + 1) * 128, :]
                fkey = f"decg{ti}"
            else:
                ti, qt = args
                n32, ngrp = 176, 11
                pk_src = dn_pk[ti * 128:(ti + 1) * 128, qt * n32:(qt + 1) * n32]
                sc_src = dn_sc[ti * 128:(ti + 1) * 128, qt * ngrp:(qt + 1) * ngrp]
                dec_dst = dn_dec[ti * 128:(ti + 1) * 128, qt * 1408:(qt + 1) * 1408]
                fkey = f"decd{ti}_{qt}"
            nW = n32 * 2
            nib = n32 * 8
            BK = BK_O + bank * BK_SZ
            tt = bview(TAB_O + 0 * 512, nW, U16)
            ta = bview(TAB_O + 1 * 512, nW, U16)
            tb = bview(TAB_O + 2 * 512, nW, U16)
            tds = [bview(BK + 592 + j * 512, nW, U16) for j in range(4)]
            w0 = bview(TAB_O + 0 * 512, nW, U16)
            w1 = bview(TAB_O + 1 * 512, nW, U16)
            E0 = bview(BK + 592, 2 * nW, U32)
            E1 = bview(BK + 592 + 2 * 512, 2 * nW, U32)
            nat32 = bview(TAB_O, 2 * nW, U32)
            tmp_nat = bview(BK + 592, nib)
            fin_nat = x_chb[:, bank * 2048:bank * 2048 + nib]
            scf = bview(BK + 512, 2 * ngrp, F32)
            sch = bview(BK + 512 + 64, ngrp, F16)
            pk = bview(BK, 2 * n32, I32)
            wu = pk.bitcast(U16)
            kp, ks, kc_, kt, kab = f"dk{bank}_pk", f"dk{bank}_sch", f"dk{bank}_sc", "dk_t", "dk_a"
            kd = [f"dk{bank}_d{j}" for j in range(4)]
            kdt = tuple(kd)
            kn, kf, kq = f"dk{bank}_nt", f"dk{bank}_F", f"dkQ{bank}"

            if phase in ("pre", "all"):
                P.op("sync", (lambda pk, pk_src: lambda e: e.dma_start(out=pk, in_=pk_src))(pk, pk_src),
                     reads=(kp,), writes=(kp,), dma=f"qpk{bank}", tag=tag)
                P.op("scalar", (lambda sch, sc_src: lambda e: e.dma_start(out=sch, in_=sc_src))(sch, sc_src),
                     reads=(ks,), writes=(ks,), dma=f"qsch{bank}", tag=tag)
                if phase == "pre":
                    return
            P.op("scalar", (lambda scf, sch, ngrp: lambda e: e.activation(
                out=scf[:, 0:ngrp], in_=sch, func=AF.Identity))(scf, sch, ngrp),
                reads=(ks,), writes=(kc_,), tag=tag)
            for j in range(4):
                td = tds[j]
                P.op("vector", (lambda j, tt, wu: lambda e: e.tensor_scalar(
                    out=tt, in0=wu, scalar1=4 * j, scalar2=7,
                    op0=ALU.logical_shift_right, op1=ALU.bitwise_and))(j, tt, wu),
                    reads=(kp,), writes=("dk_t",), tag=tag)
                P.op("vector", (lambda ta, tt: lambda e: e.tensor_scalar(
                    out=ta, in0=tt, scalar1=4, scalar2=48,
                    op0=ALU.mult, op1=ALU.add))(ta, tt),
                    reads=("dk_t",), writes=("dk_a",), tag=tag)
                P.op("vector", (lambda tb, tt: lambda e: e.tensor_scalar(
                    out=tb, in0=tt, scalar1=48, scalar2=None, op0=ALU.mult))(tb, tt),
                    reads=("dk_t",), writes=("dk_b",), tag=tag)
                P.op("vector", (lambda td, ta, tb: lambda e: e.tensor_tensor(
                    out=td, in0=ta, in1=tb, op=ALU.min))(td, ta, tb),
                    reads=("dk_a", "dk_b"), writes=(kd[j],), tag=tag)
                P.op("vector", (lambda j, tb, wu: lambda e: e.tensor_scalar(
                    out=tb, in0=wu,
                    scalar1=(4 if j == 0 else 4 * j - 4), scalar2=128,
                    op0=(ALU.logical_shift_left if j == 0 else ALU.logical_shift_right),
                    op1=ALU.bitwise_and))(j, tb, wu),
                    reads=(kp,), writes=("dk_b",), tag=tag)
                P.op("vector", (lambda td, tb: lambda e: e.tensor_tensor(
                    out=td, in0=td, in1=tb, op=ALU.add))(td, tb),
                    reads=(kd[j], "dk_b"), writes=(kd[j],), tag=tag)
            P.op("vector", (lambda tb, tds: lambda e: e.tensor_scalar(
                out=tb, in0=tds[1], scalar1=8, scalar2=None,
                op0=ALU.logical_shift_left))(tb, tds),
                reads=(kd[1], "dk_b"), writes=("dk_b",), tag=tag)
            P.op("vector", (lambda w0, tb, tds: lambda e: e.tensor_tensor(
                out=w0, in0=tb, in1=tds[0], op=ALU.bitwise_or))(w0, tb, tds),
                reads=("dk_b", kd[0], "dk_t"), writes=("dk_t",), tag=tag)
            P.op("vector", (lambda tb, tds: lambda e: e.tensor_scalar(
                out=tb, in0=tds[3], scalar1=8, scalar2=None,
                op0=ALU.logical_shift_left))(tb, tds),
                reads=(kd[3], "dk_b"), writes=("dk_b",), tag=tag)
            P.op("vector", (lambda w1, tb, tds: lambda e: e.tensor_tensor(
                out=w1, in0=tb, in1=tds[2], op=ALU.bitwise_or))(w1, tb, tds),
                reads=("dk_b", kd[2], "dk_a"), writes=("dk_a",), tag=tag)
            P.op("vector", (lambda E0, w0, nW: lambda e: e.tensor_copy(
                out=E0[:, 0:nW], in_=w0))(E0, w0, nW),
                reads=("dk_t", kd[0]), writes=(kd[0], kd[1]), tag=tag)
            P.op("vector", (lambda E1, w1, nW: lambda e: e.tensor_copy(
                out=E1[:, 0:nW], in_=w1))(E1, w1, nW),
                reads=("dk_a", kd[2]), writes=(kd[2], kd[3]), tag=tag)
            P.op("vector", (lambda E1, nW: lambda e: e.tensor_scalar(
                out=E1[:, 0:nW], in0=E1[:, 0:nW], scalar1=16, scalar2=None,
                op0=ALU.logical_shift_left))(E1, nW),
                reads=(kd[2], kd[3]), writes=(kd[2], kd[3]), tag=tag)
            P.op("vector", (lambda nat32, E0, E1, nW: lambda e: e.tensor_tensor(
                out=nat32[:, 0:nW], in0=E1[:, 0:nW], in1=E0[:, 0:nW],
                op=ALU.bitwise_or))(nat32, E0, E1, nW),
                reads=kdt + ("dk_t", "dk_a"),
                writes=("dk_t", "dk_a"), tag=tag)
            fp8all = nat32[:, 0:nW].bitcast(FP8)
            P.op("vector", (lambda tmp_nat, fp8all, nib: lambda e: e.tensor_copy(
                out=tmp_nat[:, 0:nib], in_=fp8all))(tmp_nat, fp8all, nib),
                reads=("dk_t", "dk_a") + kdt,
                writes=(kn,) + kdt, tag=tag)
            for g in range(ngrp):
                P.op("scalar", (lambda g, fin_nat, tmp_nat, scf: lambda e: e.activation(
                    out=fin_nat[:, g * 128:(g + 1) * 128],
                    in_=tmp_nat[:, g * 128:(g + 1) * 128],
                    func=AF.Identity, scale=scf[:, g:g + 1]))(g, fin_nat, tmp_nat, scf),
                    reads=(kn, kc_) + kdt,
                    writes=(kf,), tag=tag)
            if kind == "gu":
                up = ti >= 44
                g2 = (ti - 44) // 2 if up else ti // 2
                par = g2 % 2
                ic01 = ti % 2
                def mm_tr(e, fin_nat=fin_nat):
                    ins = None
                    for kc in range(KCH):
                        ins = e.matmul(out=p2b[:, kc * 128:(kc + 1) * 128],
                                       lhsT=fin_nat[:, kc * 128:(kc + 1) * 128],
                                       rhs=eye_s, start=True, stop=True)
                    return ins
                P.op("tensor", mm_tr, reads=(kf, "c_eye", "psB"), writes=("psB",),
                     tag=tag + "T")
                gdst = arB[:, GB_O + par * 8192 + (4096 if up else 0):
                           GB_O + par * 8192 + (4096 if up else 0) + 4096]
                gdst3 = gdst.rearrange("p (kc c) -> p kc c", c=256)[:, :, ic01 * 128:(ic01 + 1) * 128]
                psrc3 = p2b[:, 0:2048].rearrange("p (kc c) -> p kc c", c=128)
                gk = ("gu" if up else "gb")
                P.op("scalar", (lambda gdst3, psrc3: lambda e: e.activation(
                    out=gdst3, in_=psrc3, func=AF.Identity))(gdst3, psrc3),
                    reads=("psB",) + tuple(f"{gk}{par}_{kc}" for kc in range(KCH)),
                    writes=("psB",) + tuple(f"{gk}{par}_{kc}" for kc in range(KCH)),
                    tag=tag + "E")
            P.op("gpsimd", (lambda dec_dst, fin_nat, nib: lambda e: e.dma_start(
                out=dec_dst, in_=fin_nat[:, 0:nib]))(dec_dst, fin_nat, nib),
                reads=(kf,), writes=(fkey, kf, kq), dma=f"qdec{bank}", tag=tag)

        _pc_tiles = []
        for kc in range(KCH):
            for j in range(24):
                _pc_tiles.append((w_qkv[kc * 128:(kc + 1) * 128, j * 256:(j + 1) * 256],
                                  wqkv_bf[kc * 128:(kc + 1) * 128, j * 256:(j + 1) * 256]))
        for kc in range(KCH):
            for j in range(8):
                _pc_tiles.append((w_g[kc * 128:(kc + 1) * 128, j * 256:(j + 1) * 256],
                                  wg_bf[kc * 128:(kc + 1) * 128, j * 256:(j + 1) * 256]))
        for kc in range(KCH):
            for j in range(8):
                _pc_tiles.append((w_proj[kc * 128:(kc + 1) * 128, j * 256:(j + 1) * 256],
                                  wproj_bf[kc * 128:(kc + 1) * 128, j * 256:(j + 1) * 256]))
        _pc_state = [0]
        _pc_in = [arA[:, 43008 + b * 512:43008 + (b + 1) * 512].bitcast(F32)
                  for b in range(2)]
        _pc_out = [arA[:, 44032 + b * 256:44032 + (b + 1) * 256] for b in range(2)]

        def emit_prologue_chunk(n=34):
            i0 = _pc_state[0]
            for i in range(i0, min(i0 + n, len(_pc_tiles))):
                src, dst = _pc_tiles[i]
                b = i % 2
                P.op("sync", (lambda src, b: lambda e: e.dma_start(
                    out=_pc_in[b], in_=src))(src, b),
                    reads=(f"pcin{b}",), writes=(f"pcin{b}",), dma=f"qpc{b}", tag=f"pc{i}l")
                P.op("scalar", (lambda b: lambda e: e.activation(
                    out=_pc_out[b], in_=_pc_in[b], func=AF.Identity))(b),
                    reads=(f"pcin{b}",), writes=(f"pcout{b}",), tag=f"pc{i}c")
                P.op("gpsimd", (lambda dst, b: lambda e: e.dma_start(
                    out=dst, in_=_pc_out[b]))(dst, b),
                    reads=(f"pcout{b}",), writes=(f"pcout{b}", f"pcw{b}"),
                    dma=f"qpcw{b}", tag=f"pc{i}w")
            done_before = _pc_state[0] >= len(_pc_tiles)
            _pc_state[0] = min(i0 + n, len(_pc_tiles))
            if _pc_state[0] >= len(_pc_tiles) and not done_before:
                P.op("gpsimd", lambda e: e.engine_nop(),
                     reads=("pcw0", "pcw1"), writes=("h42", "h43", "wbf_all"),
                     tag="pcfin")


        def emit_ffn(name, decode, epi_cb):
            LOOKAHEAD = 1
            dq = []
            _dn = 0
            for g in range(22):
                for t in (2 * g, 2 * g + 1):
                    dq.append((g, "gu", t, f"{name}G{t}"))
                    dq.append((g, "gu", 44 + t, f"{name}U{t}"))
                want = (g + 1) * 64 // 22
                while _dn < want:
                    t, q = divmod(_dn, 4)
                    dq.append((g, "dn", (t, q), f"{name}D{t}_{q}"))
                    _dn += 1
            dq_pos = [0]

            def emit_decode_upto(gmax):
                while dq_pos[0] < len(dq) and dq[dq_pos[0]][0] <= gmax:
                    i = dq_pos[0]
                    if i == 0:
                        decode_tile(dq[0][1], dq[0][2], dq[0][3], 0, phase="pre")
                    if i + 1 < len(dq):
                        _, k1, a1, t1 = dq[i + 1]
                        decode_tile(k1, a1, t1, (i + 1) % 2, phase="pre")
                    _, kind, args, tag = dq[i]
                    decode_tile(kind, args, tag, i % 2, phase="body")
                    dq_pos[0] += 1

            for g2 in range(22):
                if decode:
                    emit_decode_upto(g2 + LOOKAHEAD)
                    emit_prologue_chunk()
                par = g2 % 2
                for kc in range(KCH):
                    dg = bview(GB_O + par * 8192 + kc * 256, 256)
                    du = bview(GB_O + par * 8192 + 4096 + kc * 256, 256)
                    if decode:
                        P.op("gpsimd", (lambda dg, kc, g2: lambda e: e.dma_start(
                            out=guT_d[kc * 128:(kc + 1) * 128, g2 * 256:(g2 + 1) * 256],
                            in_=dg))(dg, kc, g2),
                            reads=(f"gb{par}_{kc}",), writes=(f"guTw{g2}_{kc}",),
                            dma=f"qgw{kc % 4}", tag=f"{name}gw{g2}")
                        P.op("gpsimd", (lambda du, kc, g2: lambda e: e.dma_start(
                            out=guT_d[kc * 128:(kc + 1) * 128, FI + g2 * 256:FI + (g2 + 1) * 256],
                            in_=du))(du, kc, g2),
                            reads=(f"gu{par}_{kc}",), writes=(f"guTw{g2}_{kc}u",),
                            dma=f"qgw{(kc + 1) % 4}", tag=f"{name}uw{g2}")
                    else:
                        P.op(("scalar", "gpsimd")[kc % 2], (lambda dg, kc, g2: lambda e: e.dma_start(
                            out=dg, in_=guT_d[kc * 128:(kc + 1) * 128, g2 * 256:(g2 + 1) * 256]))(dg, kc, g2),
                            reads=(f"gb{par}_{kc}",),
                            writes=(f"gb{par}_{kc}",), dma=f"qgt{kc % 8}", tag=f"{name}gt{g2}")
                        P.op(("gpsimd", "scalar")[kc % 2], (lambda du, kc, g2: lambda e: e.dma_start(
                            out=du, in_=guT_d[kc * 128:(kc + 1) * 128, FI + g2 * 256:FI + (g2 + 1) * 256]))(du, kc, g2),
                            reads=(f"gu{par}_{kc}",),
                            writes=(f"gu{par}_{kc}",), dma=f"qut{kc % 8}", tag=f"{name}ut{g2}")
                for ic01 in (0, 1):
                    ic = 2 * g2 + ic01
                    pg = p2a[:, 0:1024]
                    pu = p2a[:, 1024:2048]
                    pgk, puk = "psg", "psu"

                    def mk_mm(par, ic01, which, pdst):
                        def f(e):
                            for kc in range(KCH):
                                base = GB_O + par * 8192 + which * 4096 + kc * 256 + ic01 * 128
                                for hn in range(2):
                                    ins = e.matmul(out=pdst[:, hn * 512:(hn + 1) * 512],
                                                   lhsT=arB[:, base:base + 128],
                                                   rhs=xlnT[:, kc * NTOK + hn * 512:kc * NTOK + (hn + 1) * 512],
                                                   start=(kc == 0), stop=(kc == KCH - 1))
                            return ins
                        return f
                    P.op("tensor", mk_mm(par, ic01, 0, pg),
                         reads=tuple(f"gb{par}_{kc}" for kc in range(KCH))
                               + tuple(f"xlnT{kc}" for kc in range(KCH)) + (pgk,),
                         writes=(pgk,), tag=f"{name}mg{ic}")
                    P.op("tensor", mk_mm(par, ic01, 1, pu),
                         reads=tuple(f"gu{par}_{kc}" for kc in range(KCH))
                               + tuple(f"xlnT{kc}" for kc in range(KCH)) + (puk,),
                         writes=(puk,), tag=f"{name}mu{ic}")
                    P.op("scalar", (lambda pg: lambda e: e.activation(
                        out=sg_f[:], in_=pg, func=AF.Silu))(pg),
                        reads=(pgk,), writes=("sg_f",), tag=f"{name}si{ic}")
                    P.op("vector", (lambda ic, pu: lambda e: e.tensor_tensor(
                        out=arA[:, ic * 1024:(ic + 1) * 1024],
                        in0=sg_f[:], in1=pu, op=ALU.mult))(ic, pu),
                        reads=("sg_f", puk), writes=(f"h{ic}",), tag=f"{name}h{ic}")

            if decode:
                P.op("gpsimd", lambda e: e.engine_nop(),
                     reads=("dkQ0", "dkQ1"), writes=("x_ch",), tag=f"{name}xfree")

            P.barrier()

            for tp in range(TCH // 2):
                for ic in range(ICH):
                    dsl = (ic % 2) * 2048
                    db = bview(DB_O + dsl, 2048)
                    if decode and tp == 0:
                        P.op("sync", (lambda db, ic: lambda e: e.dma_start(
                            out=db, in_=dn_dec[:, ic * 128:(ic + 1) * 128], transpose=True))(db, ic),
                            reads=tuple(f"decd{t}_{ic // 11}" for t in range(KCH))
                                  + (f"db{ic % 2}",),
                            writes=(f"db{ic % 2}",), dma=f"qdb{ic % 4}", tag=f"{name}dt{tp}_{ic}")
                        P.op("gpsimd", (lambda db, ic: lambda e: e.dma_start(
                            out=dnT_d[ic * 128:(ic + 1) * 128, :], in_=db))(db, ic),
                            reads=(f"db{ic % 2}",), writes=(f"dnTw{ic}",),
                            dma=f"qdw{ic % 4}", tag=f"{name}dw{ic}")
                    else:
                        P.op(("scalar", "gpsimd")[ic % 2], (lambda db, ic: lambda e: e.dma_start(
                            out=db, in_=dnT_d[ic * 128:(ic + 1) * 128, :]))(db, ic),
                            reads=((f"dnTw{ic}",) if decode else ()) + (f"db{ic % 2}",),
                            writes=(f"db{ic % 2}",), dma=f"qdb{ic % 4}", tag=f"{name}dt{tp}_{ic}")

                    def mk_dn(db, ic, tp):
                        def f(e):
                            for half in (0, 1):
                                pd_ = p2a if half == 0 else p2b
                                tc_ = 2 * tp + half
                                lt = arA[:, ic * 1024 + tc_ * 128: ic * 1024 + tc_ * 128 + 128]
                                for hn in range(4):
                                    ins = e.matmul(out=pd_[:, hn * 512:(hn + 1) * 512],
                                                   lhsT=lt, rhs=db[:, hn * 512:(hn + 1) * 512],
                                                   start=(ic == 0), stop=(ic == ICH - 1))
                            return ins
                        return f
                    P.op("tensor", mk_dn(db, ic, tp),
                         reads=(f"db{ic % 2}", f"h{ic}", "psd0", "psd1"),
                         writes=("psd0", "psd1"), tag=f"{name}md{tp}_{ic}")
                epi_cb(2 * tp, p2a, "psd0")
                epi_cb(2 * tp + 1, p2b, "psd1")

        def rms_prep(pd, pdk, tag):
            ssq = st_a[:, 4:5]; ms = st_b[:, 4:5]; rst = st_b[:, 5:6]; hr = st_b[:, 6:7]
            P.op("scalar", (lambda pd: lambda e: e.activation(
                out=xln_tm[:], in_=pd[:], func=AF.Square, accum_out=ssq))(pd),
                reads=(pdk,), writes=("xln_tm", "st_a"), tag=tag)
            P.op("vector", lambda e: e.tensor_scalar(out=ms, in0=ssq, scalar1=1.0 / C,
                                                     scalar2=RMS_EPS, op0=ALU.mult, op1=ALU.add),
                 reads=("st_a",), writes=("st_b",), tag=tag)
            newton_rsqrt(rst, ms, "st_b", tag)
            P.op("vector", lambda e: e.tensor_scalar(out=hr, in0=rst, scalar1=0.5,
                                                     scalar2=None, op0=ALU.mult),
                 reads=("st_b",), writes=("st_b",), tag=tag)
            return hr

        def ffn1_epi(tc, pd, pdk):
            hr = rms_prep(pd, pdk, f"f1r{tc}")
            P.op("sync", (lambda tc: lambda e: e.dma_start(
                out=x_ch[:], in_=x_in[tc * 128:(tc + 1) * 128, :]))(tc),
                reads=("x_ch",), writes=("x_ch",), dma="qx", tag=f"f1el{tc}")
            P.op("vector", (lambda pd: lambda e: e.scalar_tensor_tensor(
                out=x_ch[:], in0=pd[:], scalar=hr, in1=x_ch[:],
                op0=ALU.mult, op1=ALU.add))(pd),
                reads=(pdk, "st_b", "x_ch"), writes=("x_ch", pdk), tag=f"f1ex{tc}")
            P.op("gpsimd", (lambda tc: lambda e: e.dma_start(
                out=x1_d[tc * 128:(tc + 1) * 128, :], in_=x_ch[:]))(tc),
                reads=("x_ch",), writes=(f"x1w{tc}",), dma="qx1", tag=f"f1ew{tc}")
            if stage == 1:
                P.op("gpsimd", (lambda tc: lambda e: e.dma_start(
                    out=out_d[tc * 128:(tc + 1) * 128, :], in_=x_ch[:]))(tc),
                    reads=("x_ch",), writes=(f"outw{tc}",), dma="qout", tag=f"f1eo{tc}")
            else:
                ln_apply(x_ch[:], "x_ch", LN_EPS, f"ln2c{tc}")
                xlnT_build(tc, pd, (pdk,), f"ln2t{tc}")

        def ffn2_epi(tc, pd, pdk):
            hr = rms_prep(pd, pdk, f"f2r{tc}")
            P.op("sync", (lambda tc: lambda e: e.dma_start(
                out=x_ch[:], in_=x2_d[tc * 128:(tc + 1) * 128, :]))(tc),
                reads=("x_ch",), writes=("x_ch",), dma="qx", tag=f"f2el{tc}")
            P.op("vector", (lambda pd: lambda e: e.scalar_tensor_tensor(
                out=x_ch[:], in0=pd[:], scalar=hr, in1=x_ch[:],
                op0=ALU.mult, op1=ALU.add))(pd),
                reads=(pdk, "st_b", "x_ch"), writes=("x_ch", pdk), tag=f"f2ex{tc}")
            P.op("gpsimd", (lambda tc: lambda e: e.dma_start(
                out=out_d[tc * 128:(tc + 1) * 128, :], in_=x_ch[:]))(tc),
                reads=("x_ch",), writes=(f"outw{tc}",), dma="qout", tag=f"f2ew{tc}")

        if stage == 0:
            import os as _os
            _skip_dn = _os.environ.get("SKIP_DN", "0") == "1"
            dn_cnt0 = [0]
            bk0 = [0]
            for g2 in range(22):
                for t in (2 * g2, 2 * g2 + 1):
                    decode_tile("gu", t, f"zG{t}", bk0[0] % 2); bk0[0] += 1
                    decode_tile("gu", 44 + t, f"zU{t}", bk0[0] % 2); bk0[0] += 1
                want = 0 if _skip_dn else (g2 + 1) * 64 // 22
                while dn_cnt0[0] < want:
                    t, q = divmod(dn_cnt0[0], 4)
                    decode_tile("dn", (t, q), f"zD{t}_{q}", bk0[0] % 2); bk0[0] += 1
                    dn_cnt0[0] += 1
            for tc in range(TCH):
                P.op("sync", (lambda tc: lambda e: e.dma_start(
                    out=x_ch[:], in_=x_in[tc * 128:(tc + 1) * 128, :]))(tc),
                    reads=("x_ch",), writes=("x_ch",), dma="qx", tag=f"z{tc}")
                P.op("gpsimd", (lambda tc: lambda e: e.dma_start(
                    out=out_d[tc * 128:(tc + 1) * 128, :], in_=x_ch[:]))(tc),
                    reads=("x_ch",), writes=(f"outw{tc}",), dma="qout", tag=f"zo{tc}")
        dbstg = bview(DB_O, 4096, F32)
        for tc in range(TCH) if stage >= 1 else ():
            P.op("sync", (lambda tc: lambda e: e.dma_start(
                out=dbstg, in_=x_in[tc * 128:(tc + 1) * 128, :]))(tc),
                reads=("db0", "db1"), writes=("db0", "db1"), dma="qx", tag=f"ln1l{tc}")
            ln_apply(dbstg, ("db0", "db1"), LN_EPS, f"ln1c{tc}")
            xlnT_build(tc, (p2a, p2b)[tc % 2],
                       (("psg", "psu"), ("psB",))[tc % 2], f"ln1t{tc}")
        if stage >= 1:
            emit_ffn("f1", True, ffn1_epi)


        def emit_attention():
            QG_O, KG_O, VT_O, KT_O = 0, 8192, 16384, 24576
            YS_O, MS_O, SS_O, YSQ_O = 26624, 34816, 38912, 39936
            YN_O = 40960
            WQ_O = 0
            WSTG_O = 8192
            TB_O = 12288
            AT_O = 16384
            STF_O = 18432
            RG_O = 20480
            SF_O = 24576
            RB_O = 25600
            assert RB_O + 1024 <= 26720

            def aA(off, n, dt=BF16):
                sl = arA[:, off:off + n]
                return sl if dt == BF16 else sl.bitcast(dt)

            pQ = p2a[:, 0:1024]
            pRot = p2a[:, 1024:2048]
            pY = p2b[:, 0:1024]
            pT = p2b[:, 1024:2048]

            def mm_span(e, psum_ap, c0, c1, lhsT_ap, rhs_ap, rhs_off, start, stop):
                c = c0
                ins = None
                while c < c1:
                    ce = min(c1, (c // 512 + 1) * 512)
                    ins = e.matmul(out=psum_ap[:, c:ce], lhsT=lhsT_ap,
                                   rhs=rhs_ap[:, c - rhs_off:ce - rhs_off],
                                   start=start, stop=stop)
                    c = ce
                return ins

            def phase_A(g):
                par = g % 2
                for m_name, m_base in (("v", 2 * C), ("k", C), ("q", 0)):
                    for kc in range(KCH):
                        dst = bview(WQ_O + kc * 512, 512)
                        src = wqkv_bf[kc * 128:(kc + 1) * 128,
                                      m_base + g * 512:m_base + (g + 1) * 512]
                        P.op("sync", (lambda dst, src: lambda e: e.dma_start(
                            out=dst, in_=src))(dst, src),
                            reads=("wbf_all", f"wq{kc}"), writes=(f"wq{kc}",),
                            dma=f"qwq{kc % 4}", tag=f"at{g}wl")
                    for hl in range(4):
                        h = 4 * g + hl
                        def mm_qkv(e, hl=hl):
                            ins = None
                            for kc in range(KCH):
                                lt = bview(WQ_O + kc * 512 + hl * 128, 128)
                                for sub in range(2):
                                    ins = e.matmul(out=pQ[:, sub * 512:(sub + 1) * 512],
                                                   lhsT=lt,
                                                   rhs=xlnT[:, kc * NTOK + sub * 512:kc * NTOK + (sub + 1) * 512],
                                                   start=(kc == 0), stop=(kc == KCH - 1))
                            return ins
                        P.op("tensor", mm_qkv,
                             reads=tuple(f"wq{kc}" for kc in range(KCH))
                                   + tuple(f"xlnT{kc}" for kc in range(KCH)) + ("psQ",),
                             writes=("psQ",), tag=f"at{g}m{m_name}{hl}")
                        if m_name == "v":
                            P.op("scalar", lambda e: e.activation(
                                out=xln_tm[:, 0:1024], in_=pQ, func=AF.Identity),
                                reads=("psQ", "vstg"), writes=("vstg", "psQ"), tag=f"at{g}ve{hl}")
                            P.op("gpsimd", lambda e: e.dma_start(
                                out=vk_d[0:128, :], in_=xln_tm[:, 0:1024]),
                                reads=("vstg",), writes=("vkd_v",), dma="qvko", tag=f"at{g}vo{hl}")
                            for j in range(TCH):
                                dst = aA(VT_O + par * 4096 + hl * 1024 + j * 128, 128)
                                P.op("sync", (lambda dst, j: lambda e: e.dma_start(
                                    out=dst, in_=vk_d[0:128, j * 128:(j + 1) * 128],
                                    transpose=True))(dst, j),
                                    reads=("vkd_v", f"vt{par}_{hl}_{j}"),
                                    writes=(f"vt{par}_{hl}_{j}",),
                                    dma=f"qvt{j % 2}", tag=f"at{g}vt{hl}")
                        else:
                            tb_c = bview(TB_O, 2048, F32)
                            tb_s = bview(TB_O + 2048, 2048, F32)
                            c_src = (cq_in if m_name == "q" else ck_in)
                            s_src = (sq_in if m_name == "q" else sk_in)
                            P.op("sync", (lambda c_src, h, tb_c: lambda e: e.dma_start(
                                out=tb_c[:, 0:1024], in_=c_src[h * 128:(h + 1) * 128, :]))(c_src, h, tb_c),
                                reads=("tbc",), writes=("tbc",), dma="qtabc", tag=f"at{g}tc")
                            P.op("sync", (lambda s_src, h, tb_s: lambda e: e.dma_start(
                                out=tb_s[:, 0:1024], in_=s_src[h * 128:(h + 1) * 128, :]))(s_src, h, tb_s),
                                reads=("tbs",), writes=("tbs",), dma="qtabs", tag=f"at{g}ts")
                            P.op("scalar", lambda e: e.activation(
                                out=xln_tm[:, 0:1024], in_=pQ, func=AF.Identity),
                                reads=("psQ", "qraw"), writes=("qraw",), tag=f"at{g}xe{m_name}{hl}")
                            def mm_rot(e):
                                e.matmul(out=pRot[:, 0:512], lhsT=pt_s, rhs=xln_tm[:, 0:512],
                                         start=True, stop=True)
                                return e.matmul(out=pRot[:, 512:1024], lhsT=pt_s,
                                                rhs=xln_tm[:, 512:1024], start=True, stop=True)
                            P.op("tensor", mm_rot,
                                 reads=("qraw", "c_pt", "psR"), writes=("psR",),
                                 tag=f"at{g}xr{m_name}{hl}")
                            xc0 = x_ch[:, 0:1024]
                            xc1 = x_ch[:, 1024:2048]
                            P.op("vector", (lambda tb_c: lambda e: e.tensor_tensor(
                                out=xc0, in0=xln_tm[:, 0:1024], in1=tb_c[:, 0:1024], op=ALU.mult))(tb_c),
                                reads=("qraw", "tbc", "xc0"), writes=("xc0", "psQ"), tag=f"at{g}x1{m_name}{hl}")
                            P.op("vector", (lambda tb_s: lambda e: e.tensor_tensor(
                                out=xc1, in0=pRot[:, 0:1024], in1=tb_s[:, 0:1024], op=ALU.mult))(tb_s),
                                reads=("psR", "tbs", "xc1"), writes=("xc1", "psR"), tag=f"at{g}x2{m_name}{hl}")
                            dstq = aA((QG_O if m_name == "q" else KG_O) + par * 4096 + hl * 1024, 1024)
                            P.op("vector", (lambda dstq: lambda e: e.tensor_tensor(
                                out=dstq, in0=xc0, in1=xc1, op=ALU.add))(dstq),
                                reads=("xc0", "xc1", f"{m_name}hat{par}_{hl}"),
                                writes=(f"{m_name}hat{par}_{hl}",),
                                tag=f"at{g}x3{m_name}{hl}")
                            if m_name == "k":
                                P.op("gpsimd", (lambda dstq: lambda e: e.dma_start(
                                    out=vk_d[128:256, :], in_=dstq))(dstq),
                                    reads=(f"khat{par}_{hl}",), writes=("vkd_k",), dma="qvko",
                                    tag=f"at{g}ko{hl}")
                                for j in range(TCH):
                                    dkt = aA(KT_O + par * 1024 + j * 128, 128)
                                    P.op("sync", (lambda dkt, j: lambda e: e.dma_start(
                                        out=dkt, in_=vk_d[128:256, j * 128:(j + 1) * 128],
                                        transpose=True))(dkt, j),
                                        reads=("vkd_k", f"kt{par}_{j}"), writes=(f"kt{par}_{j}",),
                                        dma=f"qkt{j % 2}", tag=f"at{g}kt{hl}")
                                def mm_M(e, par=par, hl=hl):
                                    ins = None
                                    for j in range(TCH):
                                        ins = e.matmul(
                                            out=pRot[:, 0:128],
                                            lhsT=arA[:, KT_O + par * 1024 + j * 128:KT_O + par * 1024 + (j + 1) * 128],
                                            rhs=arA[:, VT_O + par * 4096 + hl * 1024 + j * 128:VT_O + par * 4096 + hl * 1024 + (j + 1) * 128],
                                            start=(j == 0), stop=(j == TCH - 1))
                                    return ins
                                P.op("tensor", mm_M,
                                     reads=tuple(f"kt{par}_{j}" for j in range(TCH))
                                           + tuple(f"vt{par}_{hl}_{j}" for j in range(TCH)) + ("psR",),
                                     writes=("psR",), tag=f"at{g}M{hl}")
                                msl = aA(MS_O + (4 * g + hl) * 256, 256, F32)
                                P.op("vector", (lambda msl, h: lambda e: e.tensor_scalar(
                                    out=msl, in0=pRot[:, 0:128], scalar1=GAMMA_P1024[h],
                                    scalar2=None, op0=ALU.mult))(msl, h),
                                    reads=("psR", f"ms{h}"), writes=(f"ms{h}", "psR"), tag=f"at{g}Me{hl}")

            def phase_B(g):
                msl_all = arA[:, MS_O + 4 * g * 256: MS_O + (4 * g + 4) * 256].bitcast(F32)
                P.op("gpsimd", (lambda msl_all, g: lambda e: e.dma_start(
                    out=cc_i[g].rearrange("(hl dk) dv -> dk hl dv", dk=128),
                    in_=msl_all.rearrange("dk (hl dv) -> dk hl dv", hl=4)))(msl_all, g),
                    reads=tuple(f"ms{4 * g + hl}" for hl in range(4)),
                    writes=(f"cci{g}",), dma="qcc", tag=f"cc{g}o")
                P.op("gpsimd", (lambda g: lambda e: e.collective_compute(
                    "AllGather", ALU.bypass,
                    replica_groups=[[0, 1], [2, 3], [4, 5], [6, 7]],
                    ins=[cc_i[g].opt()], outs=[cc_o[g].opt()]))(g),
                    reads=(f"cci{g}",), writes=(f"cco{g}",), inc=1, tag=f"cc{g}c")
                sf = bview(SF_O, 1024, F32)
                P.op("sync", (lambda sf, g: lambda e: e.dma_start(
                    out=sf.rearrange("dk (hl dv) -> dk hl dv", hl=4),
                    in_=cc_o[g][0:512, :].rearrange("(hl dk) dv -> dk hl dv", dk=128)))(sf, g),
                    reads=(f"cco{g}", "sfst"), writes=("sfst",), dma="qsf", tag=f"cc{g}l")
                ssl = aA(SS_O + (g % 2) * 512, 512)
                P.op("vector", (lambda ssl, sf: lambda e: e.tensor_scalar(
                    out=ssl, in0=sf, scalar1=alpha[:, 0:1], scalar2=None,
                    op0=ALU.mult))(ssl, sf),
                    reads=("sfst", "c_al", f"shat{g % 2}"), writes=(f"shat{g % 2}",), tag=f"cc{g}s")

            def phase_C(g):
                par = g % 2
                stf = bview(STF_O, 2048, F32)[0:8, :]
                P.op("vector", (lambda stf: lambda e: e.memset(stf, 0.0))(stf),
                     reads=("stf",), writes=("stf",), tag=f"at{g}z")
                for hl in range(4):
                    qh = aA(QG_O + par * 4096 + hl * 1024, 1024)
                    kh = aA(KG_O + par * 4096 + hl * 1024, 1024)
                    ysb = aA(YS_O + par * 4096 + hl * 1024, 1024)
                    ssl_ap = aA(SS_O + par * 512 + hl * 128, 128)
                    P.op("tensor", (lambda ssl_ap, qh: lambda e: mm_span(
                        e, pY, 0, 1024, ssl_ap, qh, 0, True, False))(ssl_ap, qh),
                        reads=(f"shat{par}", f"qhat{par}_{hl}", "psY"),
                        writes=("psY",), tag=f"at{g}cx{hl}")
                    for j in range(TCH):
                        Nj = 1024 - 128 * j
                        atb = bview(AT_O + (j % 2) * 1024, 1024)
                        P.op("tensor", (lambda kh, qh, j, Nj: lambda e: mm_span(
                            e, pT, 0, Nj,
                            kh[:, j * 128:(j + 1) * 128],
                            qh, -128 * j, True, True))(kh, qh, j, Nj),
                            reads=(f"khat{par}_{hl}", f"qhat{par}_{hl}", "psT"),
                            writes=("psT",), tag=f"at{g}sc{hl}_{j}")
                        P.op("scalar", (lambda atb, Nj: lambda e: e.activation(
                            out=atb[:, 0:Nj], in_=pT[:, 0:Nj], func=AF.Identity))(atb, Nj),
                            reads=("psT", f"atb{j % 2}"), writes=(f"atb{j % 2}", "psT"),
                            tag=f"at{g}er{hl}_{j}")
                        P.op("vector", (lambda atb: lambda e: e.tensor_tensor(
                            out=atb[:, 0:128], in0=atb[:, 0:128], in1=tri_s, op=ALU.mult))(atb),
                            reads=(f"atb{j % 2}", "c_tri"), writes=(f"atb{j % 2}",),
                            tag=f"at{g}ed{hl}_{j}")
                        P.op("tensor", (lambda atb, j, par, hl: lambda e: mm_span(
                            e, pY, 128 * j, 1024,
                            arA[:, VT_O + par * 4096 + hl * 1024 + j * 128:VT_O + par * 4096 + hl * 1024 + (j + 1) * 128],
                            atb, 128 * j, False, (j == TCH - 1)))(atb, j, par, hl),
                            reads=(f"atb{j % 2}", f"vt{par}_{hl}_{j}", "psY"),
                            writes=("psY",), tag=f"at{g}av{hl}_{j}")
                    P.op("scalar", (lambda ysb: lambda e: e.activation(
                        out=ysb, in_=pY, func=AF.Identity))(ysb),
                        reads=("psY", f"ysb{par}_{hl}"), writes=(f"ysb{par}_{hl}", "psY"),
                        tag=f"at{g}ye{hl}")
                    ysq = aA(YSQ_O, 1024)
                    P.op("scalar", (lambda ysq, ysb: lambda e: e.activation(
                        out=ysq, in_=ysb, func=AF.Square))(ysq, ysb),
                        reads=(f"ysb{par}_{hl}", "ysq"), writes=("ysq",), tag=f"at{g}yq{hl}")
                    def mm_st(e, hl=hl, ysb=ysb):
                        ysq_ = aA(YSQ_O, 1024)
                        ins = None
                        for sub in range(2):
                            ins = e.matmul(out=p2b[0:8, 1024 + sub * 512:1024 + (sub + 1) * 512],
                                           lhsT=sel_s[:, hl * 8:(hl + 1) * 8],
                                           rhs=ysb[:, sub * 512:(sub + 1) * 512],
                                           start=True, stop=False)
                        for sub in range(2):
                            ins = e.matmul(out=p2b[0:8, 1024 + sub * 512:1024 + (sub + 1) * 512],
                                           lhsT=sel_s[:, (4 + hl) * 8:(5 + hl) * 8],
                                           rhs=ysq_[:, sub * 512:(sub + 1) * 512],
                                           start=False, stop=True)
                        return ins
                    P.op("tensor", mm_st,
                         reads=(f"ysb{par}_{hl}", "ysq", "c_sel", "psT"),
                         writes=("psT",), tag=f"at{g}st{hl}")
                    P.op("vector", (lambda stf: lambda e: e.tensor_tensor(
                        out=stf, in0=stf, in1=p2b[0:8, 1024:2048], op=ALU.add))(stf),
                        reads=("stf", "psT"), writes=("stf", "psT"),
                        tag=f"at{g}sa{hl}")

            def phase_D(g):
                par = g % 2
                stf = bview(STF_O, 2048, F32)[0:8, :]
                RGv = bview(RG_O, 4096, F32)
                ex24 = RGv[0:4, 0:1024]
                varc = RGv[0:4, 1024:2048]
                mu = stf[0:4, :]
                t4 = arA[:, 41984:44032].bitcast(F32)[0:4, :]
                Rb = bview(RB_O, 1024)[0:4, :]
                Mb = arA[:, 44032:45056][0:4, :]
                P.op("vector", lambda e: e.tensor_scalar(out=stf, in0=stf, scalar1=1.0 / D,
                                                         scalar2=None, op0=ALU.mult),
                     reads=("stf",), writes=("stf",), tag=f"gn{g}")
                P.op("gpsimd", lambda e: e.dma_start(out=strow_d, in_=stf),
                     reads=("stf", "strd"), writes=("strd",), dma="qst", tag=f"gn{g}")
                P.op("sync", lambda e: e.dma_start(out=ex24, in_=strow_d[4:8, :]),
                     reads=("strd", "rg"), writes=("rg",), dma="qst2", tag=f"gn{g}")
                P.op("vector", lambda e: e.tensor_tensor(out=t4, in0=mu, in1=mu, op=ALU.mult),
                     reads=("stf", "t4"), writes=("t4",), tag=f"gn{g}")
                P.op("vector", lambda e: e.tensor_tensor(out=ex24, in0=ex24, in1=t4, op=ALU.subtract),
                     reads=("rg", "t4"), writes=("rg",), tag=f"gn{g}")
                P.op("vector", lambda e: e.tensor_scalar(out=ex24, in0=ex24, scalar1=1.0,
                                                         scalar2=GN_EPS, op0=ALU.mult, op1=ALU.add),
                     reads=("rg",), writes=("rg",), tag=f"gn{g}")
                P.op("vector", lambda e: e.tensor_copy(out=varc, in_=ex24),
                     reads=("rg",), writes=("rg",), tag=f"gn{g}")
                di = ex24.bitcast(U32)
                P.op("vector", lambda e: e.tensor_scalar(out=di, in0=varc.bitcast(U32), scalar1=1,
                                                         scalar2=None, op0=ALU.logical_shift_right),
                     reads=("rg",), writes=("rg",), tag=f"gn{g}")
                P.op("vector", lambda e: e.tensor_scalar(out=di, in0=di, scalar1=-1, scalar2=0x5F3759DF,
                                                         op0=ALU.mult, op1=ALU.add),
                     reads=("rg",), writes=("rg",), tag=f"gn{g}")
                for _ in range(3):
                    P.op("vector", lambda e: e.tensor_tensor(out=t4, in0=ex24, in1=ex24, op=ALU.mult),
                         reads=("rg", "t4"), writes=("t4",), tag=f"gn{g}")
                    P.op("vector", lambda e: e.tensor_tensor(out=t4, in0=t4, in1=varc, op=ALU.mult),
                         reads=("t4", "rg"), writes=("t4",), tag=f"gn{g}")
                    P.op("vector", lambda e: e.tensor_scalar(out=t4, in0=t4, scalar1=-0.5, scalar2=1.5,
                                                             op0=ALU.mult, op1=ALU.add),
                         reads=("t4",), writes=("t4",), tag=f"gn{g}")
                    P.op("vector", lambda e: e.tensor_tensor(out=ex24, in0=ex24, in1=t4, op=ALU.mult),
                         reads=("rg", "t4"), writes=("rg",), tag=f"gn{g}")
                P.op("vector", lambda e: e.tensor_tensor(out=mu, in0=mu, in1=ex24, op=ALU.mult),
                     reads=("stf", "rg"), writes=("stf",), tag=f"gn{g}")
                P.op("vector", lambda e: e.tensor_copy(out=Rb, in_=ex24),
                     reads=("rg", "rb"), writes=("rb",), tag=f"gn{g}")
                P.op("vector", lambda e: e.tensor_copy(out=Mb, in_=mu),
                     reads=("stf", "mb"), writes=("mb",), tag=f"gn{g}")
                for hl in range(4):
                    h = 4 * g + hl
                    ysb = aA(YS_O + par * 4096 + hl * 1024, 1024)
                    def mm_b(e, hl, rhs_tile, pdst):
                        ins = None
                        for sub in range(2):
                            ins = e.matmul(out=pdst[:, sub * 512:(sub + 1) * 512],
                                           lhsT=brc4_s[:, hl * 128:(hl + 1) * 128],
                                           rhs=rhs_tile[:, sub * 512:(sub + 1) * 512],
                                           start=True, stop=True)
                        return ins
                    P.op("tensor", (lambda hl: lambda e: mm_b(e, hl, Rb, pT))(hl),
                         reads=("rb", "c_b4", "psT"), writes=("psT",), tag=f"gn{g}bR{hl}")
                    P.op("tensor", (lambda hl: lambda e: mm_b(e, hl, Mb, pY))(hl),
                         reads=("mb", "c_b4", "psY"), writes=("psY",), tag=f"gn{g}bM{hl}")
                    xc0 = x_ch[:, 0:1024]
                    xc1 = x_ch[:, 1024:2048]
                    P.op("vector", (lambda ysb: lambda e: e.tensor_tensor(
                        out=xc0, in0=ysb, in1=pT, op=ALU.mult))(ysb),
                        reads=(f"ysb{par}_{hl}", "psT", "xc0"), writes=("xc0", "psT"),
                        tag=f"gn{g}a1{hl}")
                    P.op("vector", lambda e: e.tensor_tensor(
                        out=xc1, in0=xc0, in1=pY, op=ALU.subtract),
                        reads=("xc0", "psY", "xc1"), writes=("xc1", "psY"),
                        tag=f"gn{g}a2{hl}")
                    yn = aA(YN_O, 1024)
                    P.op("scalar", (lambda h, yn: lambda e: e.activation(
                        out=yn, in_=xc1, func=AF.Identity,
                        bias=gnwb[:, H + h:H + h + 1], scale=gnwb[:, h:h + 1]))(h, yn),
                        reads=("xc1", "c_gw", "c_gb", "yn"), writes=("yn",), tag=f"gn{g}a3{hl}")
                    for kc in range(KCH):
                        dst = bview(WQ_O + kc * 512, 128)
                        src = wg_bf[kc * 128:(kc + 1) * 128, h * 128:(h + 1) * 128]
                        P.op("sync", (lambda dst, src: lambda e: e.dma_start(
                            out=dst, in_=src))(dst, src),
                            reads=("wbf_all", f"wq{kc}"), writes=(f"wq{kc}",),
                            dma=f"qwq{kc % 4}", tag=f"gn{g}gl{hl}")
                    def mm_g(e):
                        ins = None
                        for kc in range(KCH):
                            for sub in range(2):
                                ins = e.matmul(out=pQ[:, sub * 512:(sub + 1) * 512],
                                               lhsT=bview(WQ_O + kc * 512, 128),
                                               rhs=xlnT[:, kc * NTOK + sub * 512:kc * NTOK + (sub + 1) * 512],
                                               start=(kc == 0), stop=(kc == KCH - 1))
                        return ins
                    P.op("tensor", mm_g,
                         reads=tuple(f"wq{kc}" for kc in range(KCH))
                               + tuple(f"xlnT{kc}" for kc in range(KCH)) + ("psQ",),
                         writes=("psQ",), tag=f"gn{g}mg{hl}")
                    P.op("scalar", lambda e: e.activation(out=sg_f[:], in_=pQ, func=AF.Silu),
                         reads=("psQ", "sg_f"), writes=("sg_f", "psQ"), tag=f"gn{g}si{hl}")
                    ogb = aA(YSQ_O, 1024)
                    P.op("vector", (lambda ogb, yn: lambda e: e.tensor_tensor(
                        out=ogb, in0=sg_f[:], in1=yn, op=ALU.mult))(ogb, yn),
                        reads=("sg_f", "yn", "ysq"), writes=("ysq",), tag=f"gn{g}og{hl}")
                    P.op("gpsimd", (lambda ogb, h: lambda e: e.dma_start(
                        out=og_d[h * 128:(h + 1) * 128, :], in_=ogb))(ogb, h),
                        reads=("ysq",), writes=(f"ogw{h}", "ysq"), dma="qog", tag=f"gn{g}ow{hl}")

            phase_A(0); phase_B(0)
            phase_A(1)
            phase_C(0); phase_D(0)
            phase_B(1)
            phase_A(2)
            phase_C(1); phase_D(1)
            phase_B(2)
            phase_A(3)
            phase_C(2); phase_D(2)
            phase_B(3)
            phase_C(3); phase_D(3)

            P.barrier()
            OGT_O = 0
            WP_O = 8192
            WPB_O = 12288
            for tp in range(4):
                for kc in range(KCH):
                    ogt = bview(OGT_O + (tp % 2) * 4096 + kc * 256, 256)
                    P.op("sync", (lambda ogt, kc, tp: lambda e: e.dma_start(
                        out=ogt, in_=og_d[kc * 128:(kc + 1) * 128, tp * 256:(tp + 1) * 256]))(ogt, kc, tp),
                        reads=tuple(f"ogw{h}" for h in range(H)) + (f"ogt{tp % 2}_{kc}",),
                        writes=(f"ogt{tp % 2}_{kc}",), dma=f"qot{kc % 4}", tag=f"pj{tp}o")
                for kc in range(KCH):
                    wpb = bview(WPB_O + (kc % 2) * 2048, 2048)
                    P.op("sync", (lambda wpb, kc: lambda e: e.dma_start(
                        out=wpb, in_=wproj_bf[kc * 128:(kc + 1) * 128, :]))(wpb, kc),
                        reads=("wbf_all", f"wpb{kc % 2}"), writes=(f"wpb{kc % 2}",),
                        dma=("qwp", "qwp2")[kc % 2], tag=f"pj{tp}l{kc}")
                    def mm_p(e, kc=kc, tp=tp, wpb=wpb):
                        ins = None
                        for half in (0, 1):
                            pdst = p2a if half == 0 else p2b
                            for sub in range(4):
                                ins = e.matmul(
                                    out=pdst[:, sub * 512:(sub + 1) * 512],
                                    lhsT=bview(OGT_O + (tp % 2) * 4096 + kc * 256 + half * 128, 128),
                                    rhs=wpb[:, sub * 512:(sub + 1) * 512],
                                    start=(kc == 0), stop=(kc == KCH - 1))
                        return ins
                    P.op("tensor", mm_p,
                         reads=(f"ogt{tp % 2}_{kc}", f"wpb{kc % 2}", "psP0", "psP1"),
                         writes=("psP0", "psP1"), tag=f"pj{tp}m{kc}")
                for half in (0, 1):
                    tc = 2 * tp + half
                    pdst, pk_ = (p2a, "psP0") if half == 0 else (p2b, "psP1")
                    P.op("sync", (lambda tc: lambda e: e.dma_start(
                        out=x_ch[:], in_=x1_d[tc * 128:(tc + 1) * 128, :]))(tc),
                        reads=("x_ch",), writes=("x_ch",), dma="qx", tag=f"pj{tp}x{half}")
                    P.op("vector", (lambda pdst: lambda e: e.tensor_tensor(
                        out=x_ch[:], in0=x_ch[:], in1=pdst[:, 0:2048], op=ALU.add))(pdst),
                        reads=(pk_, "x_ch"), writes=("x_ch", pk_), tag=f"pj{tp}a{half}")
                    P.op("gpsimd", (lambda tc: lambda e: e.dma_start(
                        out=x2_d[tc * 128:(tc + 1) * 128, :], in_=x_ch[:]))(tc),
                        reads=("x_ch",), writes=(f"x2w{tc}",), dma="qx1", tag=f"pj{tp}w{half}")
                    if stage == 2:
                        P.op("gpsimd", (lambda tc: lambda e: e.dma_start(
                            out=out_d[tc * 128:(tc + 1) * 128, :], in_=x_ch[:]))(tc),
                            reads=("x_ch",), writes=(f"outw{tc}",), dma="qout", tag=f"pj{tp}q{half}")
                    else:
                        ln_apply(x_ch[:], "x_ch", LN_EPS, f"ln3c{tc}")
                        xlnT_build(tc, pdst, (pk_,), f"ln3t{tc}")

        if stage >= 2:
            P.barrier()
            emit_attention()


        if stage >= 3:
            P.barrier()
            emit_ffn("f2", False, ffn2_epi)

        if _DEBUG:
            d_xln = nc.dram_tensor("d_xln", [NTOK, C], BF16, kind="ExternalOutput").ap()
            d_w = nc.dram_tensor("d_w", [256, C], BF16, kind="ExternalOutput").ap()
            d_h = nc.dram_tensor("d_h", [128, 2048], BF16, kind="ExternalOutput").ap()
            P.barrier()
            P.op("gpsimd", lambda e: e.dma_start(out=d_xln, in_=xln_d), writes=("dbg1",), dma="qdbg")
            P.op("gpsimd", lambda e: e.dma_start(out=d_w[0:128, :], in_=gu_dec[128:256, :]), writes=("dbg2",), dma="qdbg")
            P.op("gpsimd", lambda e: e.dma_start(out=d_w[128:256, :], in_=gu_dec[FI + 128:FI + 256, :]), writes=("dbg3",), dma="qdbg")
            P.op("gpsimd", lambda e: e.dma_start(out=d_h, in_=arA[:, 0:2048]), reads=("h0", "h1"), writes=("dbg4",), dma="qdbg")
        P.barrier()
        P.op("gpsimd", lambda e: e.engine_nop(),
             reads=tuple(f"outw{tc}" for tc in range(TCH)) +
                   (("dbg1", "dbg2", "dbg3", "dbg4") if _DEBUG else ()),
             writes=(), tag="fin")

        for st_name in sorted(P.streams):
            sems[st_name] = es.enter_context(nc.semaphore(f"sd_{st_name}"))
        P.schedule()
        n, w = P.stats()
        build_program.last_stats = (n, w)

        with nc.Block() as block:
            @block.sync
            def _(eng):
                P.emit_stream("sync", eng, sems)

            @block.vector
            def _(eng):
                P.emit_stream("vector", eng, sems)

            @block.scalar
            def _(eng):
                P.emit_stream("scalar", eng, sems)

            @block.tensor
            def _(eng):
                P.emit_stream("tensor", eng, sems)

            @block.gpsimd
            def _(eng):
                P.emit_stream("gpsimd", eng, sems)

    return nc





def kernel(L0_attn_x, L0_attn_w_qkv, L0_attn_w_g, L0_attn_w_proj,
           L0_attn_gn_weight, L0_attn_gn_bias,
           L0_ffn_gate_up_packed, L0_ffn_gate_up_scales,
           L0_ffn_down_packed, L0_ffn_down_scales):
    x = np.asarray(L0_attn_x, dtype=np.float32)
    wqkv = np.ascontiguousarray(np.asarray(L0_attn_w_qkv, dtype=np.float32))
    wg = np.ascontiguousarray(np.asarray(L0_attn_w_g, dtype=np.float32))
    wproj = np.ascontiguousarray(np.asarray(L0_attn_w_proj, dtype=np.float32))
    gnw = np.ascontiguousarray(np.asarray(L0_attn_gn_weight, np.float32).reshape(H, 128).T)
    gnb = np.ascontiguousarray(np.asarray(L0_attn_gn_bias, np.float32).reshape(H, 128).T)
    gu_pk = np.ascontiguousarray(np.asarray(L0_ffn_gate_up_packed, np.int32))
    gu_sc = np.ascontiguousarray(np.asarray(L0_ffn_gate_up_scales).T)
    dn_pk = np.ascontiguousarray(np.asarray(L0_ffn_down_packed, np.int32))
    dn_sc = np.ascontiguousarray(np.asarray(L0_ffn_down_scales).T)
    PT, tri, sel, brc4, eye = _host_consts()
    tabs = {0: _host_tables(0), 1: _host_tables(1)}

    nc = build_program(_STAGE)
    in_maps = []
    for c in range(8):
        b, half = c // 2, c % 2
        cq, sq, ck, sk = tabs[half]
        in_maps.append({
            "x_own": np.ascontiguousarray(x[b, half * NTOK:(half + 1) * NTOK, :]),
            "w_qkv": wqkv, "w_g": wg, "w_proj": wproj,
            "gnw": gnw, "gnb": gnb,
            "gu_pk": gu_pk, "gu_sc": gu_sc, "dn_pk": dn_pk, "dn_sc": dn_sc,
            "cq": cq, "sq": sq, "ck": ck, "sk": sk,
            "pt": PT, "tri": tri, "sel": sel, "eye": eye, "brc4": brc4,
            "alpha": np.full((128, 1), float(half), np.float32),
        })
    res = run_bass_kernel_spmd(nc, in_maps, core_ids=list(range(8)))
    out = np.empty((B, T, C), np.float32)
    for c in range(8):
        b, half = c // 2, c % 2
        out[b, half * NTOK:(half + 1) * NTOK, :] = res.results[c]["out"]
    return out


# version 10
# speedup vs baseline: 1.4139x; 1.0546x over previous
import numpy as np
import ml_dtypes

import concourse.bass as bass
import concourse.mybir as mybir
from concourse.bass_utils import run_bass_kernel_spmd
from contextlib import ExitStack

F32 = mybir.dt.float32
BF16 = mybir.dt.bfloat16
F16 = mybir.dt.float16
U16 = mybir.dt.uint16
U8 = mybir.dt.uint8
U32 = mybir.dt.uint32
I32 = mybir.dt.int32
FP8 = mybir.dt.float8e4
AF = mybir.ActivationFunctionType
ALU = mybir.AluOpType
AX = mybir.AxisListType

B, T, C = 4, 2048, 2048
H, D = 16, 128
FI = 5632
NTOK = 1024
TCH = 8
KCH = 16
ICH = 44
GU = 2 * FI
LN_EPS = 1e-5
GN_EPS = 1e-5
RMS_EPS = 1e-6

_STAGE = 3
_DEBUG = False

ENGINES = ("sync", "vector", "scalar", "tensor", "gpsimd")



class Prog:

    def __init__(self):
        self.items = []
        self.streams = set()

    def op(self, eng, fn, reads=(), writes=(), dma=None, inc=None, tag=""):
        if inc is None:
            inc = 16 if dma else 1
        if dma:
            self.streams.add(dma)
        self.items.append(["op", eng, fn, tuple(reads), tuple(writes), inc, tag,
                           None, None, dma])

    def barrier(self):
        self.items.append(["bar"])

    def schedule(self):
        cnt = {}
        writer = {}
        readers = {}
        bar_pend = {e: {} for e in ENGINES}
        for it in self.items:
            if it[0] == "bar":
                snap = dict(cnt)
                for e in ENGINES:
                    for s2, v in snap.items():
                        if s2 != e and v > 0 and v > bar_pend[e].get(s2, -1):
                            bar_pend[e][s2] = v
                continue
            _, eng, fn, reads, writes, inc, tag, _, _, stream = it
            sem = stream if stream else eng
            need = {}

            def add(dep):
                s2, v = dep
                if s2 == eng and s2 == "tensor":
                    return
                if v > need.get(s2, -1):
                    need[s2] = v
            for k in reads:
                w = writer.get(k)
                if w:
                    add(w)
            for k in writes:
                w = writer.get(k)
                if w:
                    add(w)
                for r in readers.get(k, ()):
                    add(r)
            if stream and cnt.get(sem, 0) > 0:
                if cnt[sem] > need.get(sem, -1):
                    need[sem] = cnt[sem]
            for s2, v in bar_pend[eng].items():
                if v > need.get(s2, -1):
                    need[s2] = v
            bar_pend[eng] = {}
            cnt[sem] = cnt.get(sem, 0) + inc
            it[7] = cnt[sem]
            it[8] = need
            for k in writes:
                writer[k] = (sem, cnt[sem])
                readers[k] = []
            for k in reads:
                readers.setdefault(k, []).append((sem, cnt[sem]))
        hi = {}
        for it in self.items:
            if it[0] == "bar":
                continue
            eng, need = it[1], it[8]
            pruned = {}
            for s2, v in need.items():
                if hi.get((eng, s2), -1) >= v:
                    continue
                pruned[s2] = v
                hi[(eng, s2)] = v
            it[8] = pruned

    def emit_stream(self, eng_name, eng_obj, sems):
        for it in self.items:
            if it[0] == "bar" or it[1] != eng_name:
                continue
            for s2, v in it[8].items():
                eng_obj.wait_ge(sems[s2], v)
            sem = it[9] if it[9] else eng_name
            it[2](eng_obj).then_inc(sems[sem], it[5])

    def stats(self):
        n = {e: 0 for e in ENGINES}
        w = {e: 0 for e in ENGINES}
        for it in self.items:
            if it[0] == "bar":
                continue
            n[it[1]] += 1
            w[it[1]] += len(it[8])
        return n, w



def _host_tables(half):
    f = np.float64
    ds = (np.arange(0, C, 2, dtype=f) + 0.4 * C) / (1.4 * C)
    pos = np.arange(T, dtype=f)
    scale = ds[None, :] ** (pos[:, None] / 512.0)
    inv_freq = 1.0 / 10000.0 ** (np.arange(0, C, 2, dtype=f) / C)
    sinus = pos[:, None] * inv_freq[None, :]
    sin = np.repeat(np.sin(sinus) * scale, 2, axis=-1)
    cos = np.repeat(np.cos(sinus) * scale, 2, axis=-1)
    gamma = 1.0 - 2.0 ** (-5.0 - np.arange(H, dtype=f))
    gch = np.repeat(gamma, D)
    t0 = half * NTOK
    tl = np.arange(t0, t0 + NTOK, dtype=f)
    qfac = gch[:, None] ** (tl - t0)[None, :] * D ** -0.5
    kfac = gch[:, None] ** (t0 - tl)[None, :]
    cq = cos[t0:t0 + NTOK].T * qfac
    sq = sin[t0:t0 + NTOK].T * qfac
    ck = cos[t0:t0 + NTOK].T * kfac
    sk = sin[t0:t0 + NTOK].T * kfac
    return tuple(np.ascontiguousarray(a.astype(np.float32)) for a in (cq, sq, ck, sk))


def _host_consts():
    bf = ml_dtypes.bfloat16
    PT = np.zeros((128, 128), dtype=np.float32)
    ii = np.arange(0, 128, 2)
    PT[ii + 1, ii] = -1.0
    PT[ii, ii + 1] = 1.0
    tri = (np.arange(128)[None, :] >= np.arange(128)[:, None]).astype(np.float32)
    sel = np.zeros((128, 64), dtype=np.float32)
    for j in range(8):
        sel[:, j * 8 + j] = 1.0
    brc4 = np.zeros((4, 4 * 128), dtype=np.float32)
    for r in range(4):
        brc4[r, r * 128:(r + 1) * 128] = 1.0
    eye = np.eye(128, dtype=np.float32)
    return (PT.astype(bf), tri.astype(bf), sel.astype(bf), brc4.astype(bf), eye.astype(bf))


GAMMA_P1024 = [float((1.0 - 2.0 ** (-5.0 - h)) ** 1024) for h in range(H)]



def build_program(stage):
    nc = bass.Bass("TRN2", target_bir_lowering=False, debug=False)
    es = ExitStack()

    def din(name, shape, dt):
        return nc.dram_tensor(name, shape, dt, kind="ExternalInput").ap()

    x_in = din("x_own", [NTOK, C], F32)
    w_qkv = din("w_qkv", [C, 3 * C], F32)
    w_g = din("w_g", [C, C], F32)
    w_proj = din("w_proj", [C, C], F32)
    gnw_in = din("gnw", [128, H], F32)
    gnb_in = din("gnb", [128, H], F32)
    gu_pk = din("gu_pk", [GU, C // 8], I32)
    gu_sc = din("gu_sc", [GU, KCH], F16)
    dn_pk = din("dn_pk", [C, FI // 8], I32)
    dn_sc = din("dn_sc", [C, ICH], F16)
    cq_in = din("cq", [C, NTOK], F32)
    sq_in = din("sq", [C, NTOK], F32)
    ck_in = din("ck", [C, NTOK], F32)
    sk_in = din("sk", [C, NTOK], F32)
    pt_in = din("pt", [128, 128], BF16)
    tri_in = din("tri", [128, 128], BF16)
    sel_in = din("sel", [128, 64], BF16)
    eye_in = din("eye", [128, 128], BF16)
    brc4_in = din("brc4", [4, 4 * 128], BF16)
    alpha_in = din("alpha", [128, 1], F32)

    out_d = nc.dram_tensor("out", [NTOK, C], F32, kind="ExternalOutput").ap()

    gu_dec = nc.dram_tensor("gu_dec", [GU, C], BF16).ap()
    dn_dec = nc.dram_tensor("dn_dec", [C, FI], BF16).ap()
    x1_d = nc.dram_tensor("x1_d", [NTOK, C], F32).ap()
    x2_d = nc.dram_tensor("x2_d", [NTOK, C], F32).ap()
    wqkv_bf = nc.dram_tensor("wqkv_bf", [C, 3 * C], BF16).ap()
    wg_bf = nc.dram_tensor("wg_bf", [C, C], BF16).ap()
    vk_d = nc.dram_tensor("vk_d", [2 * 128, NTOK], BF16).ap()
    og_d = nc.dram_tensor("og_d", [C, NTOK], BF16).ap()
    strow_d = nc.dram_tensor("strow_d", [8, NTOK], F32).ap()
    wproj_bf = nc.dram_tensor("wproj_bf", [C, C], BF16).ap()
    guT_d = nc.dram_tensor("guT_d", [C, GU], BF16).ap()
    dnT_d = nc.dram_tensor("dnT_d", [FI, C], BF16).ap()
    cc_i = [nc.dram_tensor(f"cc_i{g}", [4 * D, D], F32).ap() for g in range(4)]
    cc_o = [nc.dram_tensor(f"cc_o{g}", [2 * 4 * D, D], F32).ap() for g in range(4)]

    P = Prog()

    with es:
        def sb(name, shape, dt):
            return es.enter_context(nc.sbuf_tensor(name, shape, dt))

        sems = {e: es.enter_context(nc.semaphore(f"se_{e}")) for e in ENGINES}

        xlnT = sb("xlnT", [128, KCH * NTOK], BF16)
        arA = sb("arA", [128, ICH * NTOK], BF16)
        arB = sb("arB", [128, 27296], BF16)
        consts = sb("consts", [128, 448], BF16)
        brc4_s = sb("brc4_s", [4, 4 * 128], BF16)
        alpha = sb("alpha_s", [128, 1], F32)
        gnwb = sb("gnwb", [128, 2 * H], F32)
        x_ch = sb("x_ch", [128, C], F32)
        xln_tm = sb("xln_tm", [128, C], BF16)
        sg_f = sb("sg_f", [128, NTOK], F32)
        st_a = sb("st_a", [128, 8], F32)
        st_b = sb("st_b", [128, 8], F32)
        st_c = sb("st_c", [128, 8], F32)
        st_n = sb("st_n", [128, 8], F32)

        p2a = es.enter_context(nc.psum_tensor("p2a", [128, 2048], F32))
        p2b = es.enter_context(nc.psum_tensor("p2b", [128, 2048], F32))

        pt_s = consts[:, 0:128]
        tri_s = consts[:, 128:256]
        sel_s = consts[:, 256:320]
        eye_s = consts[:, 320:448]

        for ap_dst, ap_src, key in ((consts[:, 0:128], pt_in, "c_pt"),
                                    (consts[:, 128:256], tri_in, "c_tri"),
                                    (consts[:, 256:320], sel_in, "c_sel"),
                                    (consts[:, 320:448], eye_in, "c_eye"),
                                    (brc4_s[:], brc4_in, "c_b4"),
                                    (alpha[:], alpha_in, "c_al"),
                                    (gnwb[:, 0:H], gnw_in, "c_gw"),
                                    (gnwb[:, H:2 * H], gnb_in, "c_gb")):
            P.op("sync", (lambda d, s: lambda e: e.dma_start(out=d, in_=s))(ap_dst, ap_src),
                 writes=(key,), dma="qconst", tag="const")


        def newton_rsqrt(dst, src, key, tag):
            di = dst.bitcast(U32)
            si = src.bitcast(U32)
            t = st_n[:, 0:dst.shape[1]]
            P.op("vector", lambda e: e.tensor_scalar(out=di, in0=si, scalar1=1, scalar2=None,
                                                     op0=ALU.logical_shift_right),
                 reads=(key,), writes=(key,), tag=tag)
            P.op("vector", lambda e: e.tensor_scalar(out=di, in0=di, scalar1=-1, scalar2=0x5F3759DF,
                                                     op0=ALU.mult, op1=ALU.add),
                 reads=(key,), writes=(key,), tag=tag)
            for _ in range(3):
                P.op("vector", lambda e: e.tensor_tensor(out=t, in0=dst, in1=dst, op=ALU.mult),
                     reads=(key,), writes=("st_n",), tag=tag)
                P.op("vector", lambda e: e.tensor_tensor(out=t, in0=t, in1=src, op=ALU.mult),
                     reads=("st_n", key), writes=("st_n",), tag=tag)
                P.op("vector", lambda e: e.tensor_scalar(out=t, in0=t, scalar1=-0.5, scalar2=1.5,
                                                         op0=ALU.mult, op1=ALU.add),
                     reads=("st_n",), writes=("st_n",), tag=tag)
                P.op("vector", lambda e: e.tensor_tensor(out=dst, in0=dst, in1=t, op=ALU.mult),
                     reads=(key, "st_n"), writes=(key,), tag=tag)

        def ln_apply(src_ap, src_key, eps, tag):
            skeys = src_key if isinstance(src_key, tuple) else (src_key,)
            src_key = skeys[0]
            su = st_a[:, 0:1]; mu = st_a[:, 1:2]; ssq = st_a[:, 2:3]
            var = st_b[:, 0:1]; rst = st_b[:, 1:2]; nmr = st_b[:, 2:3]
            t0 = st_c[:, 0:1]
            P.op("vector", lambda e: e.reduce_sum(out=su, in_=src_ap, axis=AX.X),
                 reads=skeys, writes=("st_a",), tag=tag)
            P.op("scalar", lambda e: e.activation(out=xln_tm[:], in_=src_ap,
                                                  func=AF.Square, accum_out=ssq),
                 reads=skeys, writes=("xln_tm", "st_a"), tag=tag)
            P.op("vector", lambda e: e.tensor_scalar(out=mu, in0=su, scalar1=1.0 / C,
                                                     scalar2=None, op0=ALU.mult),
                 reads=("st_a",), writes=("st_a",), tag=tag)
            P.op("vector", lambda e: e.tensor_tensor(out=t0, in0=mu, in1=mu, op=ALU.mult),
                 reads=("st_a",), writes=("st_c",), tag=tag)
            P.op("vector", lambda e: e.tensor_scalar(out=var, in0=ssq, scalar1=1.0 / C,
                                                     scalar2=eps, op0=ALU.mult, op1=ALU.add),
                 reads=("st_a",), writes=("st_b",), tag=tag)
            P.op("vector", lambda e: e.tensor_tensor(out=var, in0=var, in1=t0, op=ALU.subtract),
                 reads=("st_b", "st_c"), writes=("st_b",), tag=tag)
            newton_rsqrt(rst, var, "st_b", tag)
            P.op("vector", lambda e: e.tensor_scalar(out=nmr, in0=mu, scalar1=-1.0, scalar2=rst,
                                                     op0=ALU.mult, op1=ALU.mult),
                 reads=("st_a", "st_b"), writes=("st_b",), tag=tag)
            P.op("scalar", lambda e: e.activation(out=xln_tm[:], in_=src_ap, func=AF.Identity,
                                                  bias=nmr, scale=rst),
                 reads=skeys + ("st_b",), writes=("xln_tm",), tag=tag)

        def xlnT_build(tc, psum, pkeys, tag):
            def mm_tr(e):
                ins = None
                for kc in range(KCH):
                    ins = e.matmul(out=psum[:, kc * 128:(kc + 1) * 128],
                                   lhsT=xln_tm[:, kc * 128:(kc + 1) * 128],
                                   rhs=eye_s, start=True, stop=True)
                return ins
            P.op("tensor", mm_tr, reads=("xln_tm", "c_eye") + pkeys,
                 writes=pkeys, tag=tag)
            for kc in range(KCH):
                P.op("scalar", (lambda kc: lambda e: e.activation(
                    out=xlnT[:, kc * NTOK + tc * 128: kc * NTOK + (tc + 1) * 128],
                    in_=psum[:, kc * 128:(kc + 1) * 128], func=AF.Identity))(kc),
                    reads=pkeys, writes=(f"xlnT{kc}",), tag=tag)

        GB_O = 0
        DB_O = 16384
        TAB_O = 20480
        BK_O = 22016
        BK_SZ = 2640
        assert BK_O + 2 * BK_SZ <= 27296

        def bview(off, n, dt=BF16):
            sl = arB[:, off:off + n]
            if dt == BF16:
                return sl
            return sl.bitcast(dt)

        x_chb = x_ch[:].bitcast(BF16)

        def decode_tile(kind, args, tag, bank, phase="all"):
            if kind == "gu":
                ti = args
                n32, ngrp = 256, 16
                pk_src = gu_pk[ti * 128:(ti + 1) * 128, :]
                sc_src = gu_sc[ti * 128:(ti + 1) * 128, :]
                dec_dst = gu_dec[ti * 128:(ti + 1) * 128, :]
                fkey = f"decg{ti}"
            else:
                ti, qt = args
                n32, ngrp = 176, 11
                pk_src = dn_pk[ti * 128:(ti + 1) * 128, qt * n32:(qt + 1) * n32]
                sc_src = dn_sc[ti * 128:(ti + 1) * 128, qt * ngrp:(qt + 1) * ngrp]
                dec_dst = dn_dec[ti * 128:(ti + 1) * 128, qt * 1408:(qt + 1) * 1408]
                fkey = f"decd{ti}_{qt}"
            nW = n32 * 2
            nib = n32 * 8
            BK = BK_O + bank * BK_SZ
            tt = bview(TAB_O + 0 * 512, nW, U16)
            ta = bview(TAB_O + 1 * 512, nW, U16)
            tb = bview(TAB_O + 2 * 512, nW, U16)
            tds = [bview(BK + 592 + j * 512, nW, U16) for j in range(4)]
            w0 = bview(TAB_O + 0 * 512, nW, U16)
            w1 = bview(TAB_O + 1 * 512, nW, U16)
            E0 = bview(BK + 592, 2 * nW, U32)
            E1 = bview(BK + 592 + 2 * 512, 2 * nW, U32)
            nat32 = bview(TAB_O, 2 * nW, U32)
            tmp_nat = bview(BK + 592, nib)
            fin_nat = x_chb[:, bank * 2048:bank * 2048 + nib]
            scf = bview(BK + 512, 2 * ngrp, F32)
            sch = bview(BK + 512 + 64, ngrp, F16)
            pk = bview(BK, 2 * n32, I32)
            wu = pk.bitcast(U16)
            kp, ks, kc_, kt, kab = f"dk{bank}_pk", f"dk{bank}_sch", f"dk{bank}_sc", "dk_t", "dk_a"
            kd = [f"dk{bank}_d{j}" for j in range(4)]
            kdt = tuple(kd)
            kn, kf, kq = f"dk{bank}_nt", f"dk{bank}_F", f"dkQ{bank}"

            if phase in ("pre", "all"):
                P.op("sync", (lambda pk, pk_src: lambda e: e.dma_start(out=pk, in_=pk_src))(pk, pk_src),
                     reads=(kp,), writes=(kp,), dma=f"qpk{bank}", tag=tag)
                P.op("scalar", (lambda sch, sc_src: lambda e: e.dma_start(out=sch, in_=sc_src))(sch, sc_src),
                     reads=(ks,), writes=(ks,), dma=f"qsch{bank}", tag=tag)
                if phase == "pre":
                    return
            P.op("scalar", (lambda scf, sch, ngrp: lambda e: e.activation(
                out=scf[:, 0:ngrp], in_=sch, func=AF.Identity))(scf, sch, ngrp),
                reads=(ks,), writes=(kc_,), tag=tag)
            for j in range(4):
                td = tds[j]
                P.op("vector", (lambda j, tt, wu: lambda e: e.tensor_scalar(
                    out=tt, in0=wu, scalar1=4 * j, scalar2=7,
                    op0=ALU.logical_shift_right, op1=ALU.bitwise_and))(j, tt, wu),
                    reads=(kp,), writes=("dk_t",), tag=tag)
                P.op("vector", (lambda ta, tt: lambda e: e.tensor_scalar(
                    out=ta, in0=tt, scalar1=4, scalar2=48,
                    op0=ALU.mult, op1=ALU.add))(ta, tt),
                    reads=("dk_t",), writes=("dk_a",), tag=tag)
                P.op("vector", (lambda tb, tt: lambda e: e.tensor_scalar(
                    out=tb, in0=tt, scalar1=48, scalar2=None, op0=ALU.mult))(tb, tt),
                    reads=("dk_t",), writes=("dk_b",), tag=tag)
                P.op("vector", (lambda td, ta, tb: lambda e: e.tensor_tensor(
                    out=td, in0=ta, in1=tb, op=ALU.min))(td, ta, tb),
                    reads=("dk_a", "dk_b"), writes=(kd[j],), tag=tag)
                P.op("vector", (lambda j, tb, wu: lambda e: e.tensor_scalar(
                    out=tb, in0=wu,
                    scalar1=(4 if j == 0 else 4 * j - 4), scalar2=128,
                    op0=(ALU.logical_shift_left if j == 0 else ALU.logical_shift_right),
                    op1=ALU.bitwise_and))(j, tb, wu),
                    reads=(kp,), writes=("dk_b",), tag=tag)
                P.op("vector", (lambda td, tb: lambda e: e.tensor_tensor(
                    out=td, in0=td, in1=tb, op=ALU.add))(td, tb),
                    reads=(kd[j], "dk_b"), writes=(kd[j],), tag=tag)
            P.op("vector", (lambda tb, tds: lambda e: e.tensor_scalar(
                out=tb, in0=tds[1], scalar1=8, scalar2=None,
                op0=ALU.logical_shift_left))(tb, tds),
                reads=(kd[1], "dk_b"), writes=("dk_b",), tag=tag)
            P.op("vector", (lambda w0, tb, tds: lambda e: e.tensor_tensor(
                out=w0, in0=tb, in1=tds[0], op=ALU.bitwise_or))(w0, tb, tds),
                reads=("dk_b", kd[0], "dk_t"), writes=("dk_t",), tag=tag)
            P.op("vector", (lambda tb, tds: lambda e: e.tensor_scalar(
                out=tb, in0=tds[3], scalar1=8, scalar2=None,
                op0=ALU.logical_shift_left))(tb, tds),
                reads=(kd[3], "dk_b"), writes=("dk_b",), tag=tag)
            P.op("vector", (lambda w1, tb, tds: lambda e: e.tensor_tensor(
                out=w1, in0=tb, in1=tds[2], op=ALU.bitwise_or))(w1, tb, tds),
                reads=("dk_b", kd[2], "dk_a"), writes=("dk_a",), tag=tag)
            P.op("vector", (lambda E0, w0, nW: lambda e: e.tensor_copy(
                out=E0[:, 0:nW], in_=w0))(E0, w0, nW),
                reads=("dk_t", kd[0]), writes=(kd[0], kd[1]), tag=tag)
            P.op("vector", (lambda E1, w1, nW: lambda e: e.tensor_copy(
                out=E1[:, 0:nW], in_=w1))(E1, w1, nW),
                reads=("dk_a", kd[2]), writes=(kd[2], kd[3]), tag=tag)
            P.op("vector", (lambda E1, nW: lambda e: e.tensor_scalar(
                out=E1[:, 0:nW], in0=E1[:, 0:nW], scalar1=16, scalar2=None,
                op0=ALU.logical_shift_left))(E1, nW),
                reads=(kd[2], kd[3]), writes=(kd[2], kd[3]), tag=tag)
            P.op("vector", (lambda nat32, E0, E1, nW: lambda e: e.tensor_tensor(
                out=nat32[:, 0:nW], in0=E1[:, 0:nW], in1=E0[:, 0:nW],
                op=ALU.bitwise_or))(nat32, E0, E1, nW),
                reads=kdt + ("dk_t", "dk_a"),
                writes=("dk_t", "dk_a"), tag=tag)
            fp8all = nat32[:, 0:nW].bitcast(FP8)
            P.op("vector", (lambda tmp_nat, fp8all, nib: lambda e: e.tensor_copy(
                out=tmp_nat[:, 0:nib], in_=fp8all))(tmp_nat, fp8all, nib),
                reads=("dk_t", "dk_a") + kdt,
                writes=(kn,) + kdt, tag=tag)
            for g in range(ngrp):
                P.op("scalar", (lambda g, fin_nat, tmp_nat, scf: lambda e: e.activation(
                    out=fin_nat[:, g * 128:(g + 1) * 128],
                    in_=tmp_nat[:, g * 128:(g + 1) * 128],
                    func=AF.Identity, scale=scf[:, g:g + 1]))(g, fin_nat, tmp_nat, scf),
                    reads=(kn, kc_) + kdt,
                    writes=(kf,), tag=tag)
            if kind == "gu":
                up = ti >= 44
                g2 = (ti - 44) // 2 if up else ti // 2
                par = g2 % 2
                ic01 = ti % 2
                def mm_tr(e, fin_nat=fin_nat):
                    ins = None
                    for kc in range(KCH):
                        ins = e.matmul(out=p2b[:, kc * 128:(kc + 1) * 128],
                                       lhsT=fin_nat[:, kc * 128:(kc + 1) * 128],
                                       rhs=eye_s, start=True, stop=True)
                    return ins
                P.op("tensor", mm_tr, reads=(kf, "c_eye", "psB"), writes=("psB",),
                     tag=tag + "T")
                gdst = arB[:, GB_O + par * 8192 + (4096 if up else 0):
                           GB_O + par * 8192 + (4096 if up else 0) + 4096]
                gdst3 = gdst.rearrange("p (kc c) -> p kc c", c=256)[:, :, ic01 * 128:(ic01 + 1) * 128]
                psrc3 = p2b[:, 0:2048].rearrange("p (kc c) -> p kc c", c=128)
                gk = ("gu" if up else "gb")
                P.op("scalar", (lambda gdst3, psrc3: lambda e: e.activation(
                    out=gdst3, in_=psrc3, func=AF.Identity))(gdst3, psrc3),
                    reads=("psB",) + tuple(f"{gk}{par}_{kc}" for kc in range(KCH)),
                    writes=("psB",) + tuple(f"{gk}{par}_{kc}" for kc in range(KCH)),
                    tag=tag + "E")
            P.op("gpsimd", (lambda dec_dst, fin_nat, nib: lambda e: e.dma_start(
                out=dec_dst, in_=fin_nat[:, 0:nib]))(dec_dst, fin_nat, nib),
                reads=(kf,), writes=(fkey, kf, kq), dma=f"qdec{bank}", tag=tag)

        _pc_tiles = []
        for kc in range(KCH):
            for j in range(24):
                _pc_tiles.append((w_qkv[kc * 128:(kc + 1) * 128, j * 256:(j + 1) * 256],
                                  wqkv_bf[kc * 128:(kc + 1) * 128, j * 256:(j + 1) * 256]))
        for kc in range(KCH):
            for j in range(8):
                _pc_tiles.append((w_g[kc * 128:(kc + 1) * 128, j * 256:(j + 1) * 256],
                                  wg_bf[kc * 128:(kc + 1) * 128, j * 256:(j + 1) * 256]))
        for kc in range(KCH):
            for j in range(8):
                _pc_tiles.append((w_proj[kc * 128:(kc + 1) * 128, j * 256:(j + 1) * 256],
                                  wproj_bf[kc * 128:(kc + 1) * 128, j * 256:(j + 1) * 256]))
        _pc_state = [0]
        _pc_in = [arA[:, 43008 + b * 512:43008 + (b + 1) * 512].bitcast(F32)
                  for b in range(2)]
        _pc_out = [arA[:, 44032 + b * 256:44032 + (b + 1) * 256] for b in range(2)]

        def emit_prologue_chunk(n=34):
            i0 = _pc_state[0]
            for i in range(i0, min(i0 + n, len(_pc_tiles))):
                src, dst = _pc_tiles[i]
                b = i % 2
                P.op("sync", (lambda src, b: lambda e: e.dma_start(
                    out=_pc_in[b], in_=src))(src, b),
                    reads=(f"pcin{b}",), writes=(f"pcin{b}",), dma=f"qpc{b}", tag=f"pc{i}l")
                P.op("scalar", (lambda b: lambda e: e.activation(
                    out=_pc_out[b], in_=_pc_in[b], func=AF.Identity))(b),
                    reads=(f"pcin{b}",), writes=(f"pcout{b}",), tag=f"pc{i}c")
                P.op("gpsimd", (lambda dst, b: lambda e: e.dma_start(
                    out=dst, in_=_pc_out[b]))(dst, b),
                    reads=(f"pcout{b}",), writes=(f"pcout{b}", f"pcw{b}"),
                    dma=f"qpcw{b}", tag=f"pc{i}w")
            done_before = _pc_state[0] >= len(_pc_tiles)
            _pc_state[0] = min(i0 + n, len(_pc_tiles))
            if _pc_state[0] >= len(_pc_tiles) and not done_before:
                P.op("gpsimd", lambda e: e.engine_nop(),
                     reads=("pcw0", "pcw1"), writes=("h42", "h43", "wbf_all"),
                     tag="pcfin")


        def emit_ffn(name, decode, epi_cb):
            LOOKAHEAD = 1
            dq = []
            _dn = 0
            for g in range(22):
                for t in (2 * g, 2 * g + 1):
                    dq.append((g, "gu", t, f"{name}G{t}"))
                    dq.append((g, "gu", 44 + t, f"{name}U{t}"))
                want = (g + 1) * 64 // 22
                while _dn < want:
                    t, q = divmod(_dn, 4)
                    dq.append((g, "dn", (t, q), f"{name}D{t}_{q}"))
                    _dn += 1
            dq_pos = [0]

            def emit_decode_upto(gmax):
                while dq_pos[0] < len(dq) and dq[dq_pos[0]][0] <= gmax:
                    i = dq_pos[0]
                    if i == 0:
                        decode_tile(dq[0][1], dq[0][2], dq[0][3], 0, phase="pre")
                    if i + 1 < len(dq):
                        _, k1, a1, t1 = dq[i + 1]
                        decode_tile(k1, a1, t1, (i + 1) % 2, phase="pre")
                    _, kind, args, tag = dq[i]
                    decode_tile(kind, args, tag, i % 2, phase="body")
                    dq_pos[0] += 1

            for g2 in range(22):
                if decode:
                    emit_decode_upto(g2 + LOOKAHEAD)
                    emit_prologue_chunk()
                par = g2 % 2
                for kc in range(KCH):
                    dg = bview(GB_O + par * 8192 + kc * 256, 256)
                    du = bview(GB_O + par * 8192 + 4096 + kc * 256, 256)
                    if decode:
                        P.op("gpsimd", (lambda dg, kc, g2: lambda e: e.dma_start(
                            out=guT_d[kc * 128:(kc + 1) * 128, g2 * 256:(g2 + 1) * 256],
                            in_=dg))(dg, kc, g2),
                            reads=(f"gb{par}_{kc}",), writes=(f"guTw{g2}_{kc}",),
                            dma=f"qgw{kc % 4}", tag=f"{name}gw{g2}")
                        P.op("gpsimd", (lambda du, kc, g2: lambda e: e.dma_start(
                            out=guT_d[kc * 128:(kc + 1) * 128, FI + g2 * 256:FI + (g2 + 1) * 256],
                            in_=du))(du, kc, g2),
                            reads=(f"gu{par}_{kc}",), writes=(f"guTw{g2}_{kc}u",),
                            dma=f"qgw{(kc + 1) % 4}", tag=f"{name}uw{g2}")
                    else:
                        P.op(("scalar", "gpsimd")[kc % 2], (lambda dg, kc, g2: lambda e: e.dma_start(
                            out=dg, in_=guT_d[kc * 128:(kc + 1) * 128, g2 * 256:(g2 + 1) * 256]))(dg, kc, g2),
                            reads=(f"gb{par}_{kc}",),
                            writes=(f"gb{par}_{kc}",), dma=f"qgt{kc % 8}", tag=f"{name}gt{g2}")
                        P.op(("gpsimd", "scalar")[kc % 2], (lambda du, kc, g2: lambda e: e.dma_start(
                            out=du, in_=guT_d[kc * 128:(kc + 1) * 128, FI + g2 * 256:FI + (g2 + 1) * 256]))(du, kc, g2),
                            reads=(f"gu{par}_{kc}",),
                            writes=(f"gu{par}_{kc}",), dma=f"qut{kc % 8}", tag=f"{name}ut{g2}")
                for ic01 in (0, 1):
                    ic = 2 * g2 + ic01
                    pg = p2a[:, 0:1024]
                    pu = p2a[:, 1024:2048]
                    pgk, puk = "psg", "psu"

                    def mk_mm(par, ic01, which, pdst):
                        def f(e):
                            for kc in range(KCH):
                                base = GB_O + par * 8192 + which * 4096 + kc * 256 + ic01 * 128
                                for hn in range(2):
                                    ins = e.matmul(out=pdst[:, hn * 512:(hn + 1) * 512],
                                                   lhsT=arB[:, base:base + 128],
                                                   rhs=xlnT[:, kc * NTOK + hn * 512:kc * NTOK + (hn + 1) * 512],
                                                   start=(kc == 0), stop=(kc == KCH - 1))
                            return ins
                        return f
                    P.op("tensor", mk_mm(par, ic01, 0, pg),
                         reads=tuple(f"gb{par}_{kc}" for kc in range(KCH))
                               + tuple(f"xlnT{kc}" for kc in range(KCH)) + (pgk,),
                         writes=(pgk,), tag=f"{name}mg{ic}")
                    P.op("tensor", mk_mm(par, ic01, 1, pu),
                         reads=tuple(f"gu{par}_{kc}" for kc in range(KCH))
                               + tuple(f"xlnT{kc}" for kc in range(KCH)) + (puk,),
                         writes=(puk,), tag=f"{name}mu{ic}")
                    P.op("scalar", (lambda pg: lambda e: e.activation(
                        out=sg_f[:], in_=pg, func=AF.Silu))(pg),
                        reads=(pgk,), writes=("sg_f",), tag=f"{name}si{ic}")
                    P.op("vector", (lambda ic, pu: lambda e: e.tensor_tensor(
                        out=arA[:, ic * 1024:(ic + 1) * 1024],
                        in0=sg_f[:], in1=pu, op=ALU.mult))(ic, pu),
                        reads=("sg_f", puk), writes=(f"h{ic}",), tag=f"{name}h{ic}")

            if decode:
                P.op("gpsimd", lambda e: e.engine_nop(),
                     reads=("dkQ0", "dkQ1"), writes=("x_ch",), tag=f"{name}xfree")

            P.barrier()

            for tp in range(TCH // 2):
                for ic in range(ICH):
                    dsl = (ic % 2) * 2048
                    db = bview(DB_O + dsl, 2048)
                    if decode and tp == 0:
                        P.op("sync", (lambda db, ic: lambda e: e.dma_start(
                            out=db, in_=dn_dec[:, ic * 128:(ic + 1) * 128], transpose=True))(db, ic),
                            reads=tuple(f"decd{t}_{ic // 11}" for t in range(KCH))
                                  + (f"db{ic % 2}",),
                            writes=(f"db{ic % 2}",), dma=f"qdb{ic % 4}", tag=f"{name}dt{tp}_{ic}")
                        P.op("gpsimd", (lambda db, ic: lambda e: e.dma_start(
                            out=dnT_d[ic * 128:(ic + 1) * 128, :], in_=db))(db, ic),
                            reads=(f"db{ic % 2}",), writes=(f"dnTw{ic}",),
                            dma=f"qdw{ic % 4}", tag=f"{name}dw{ic}")
                    else:
                        P.op(("scalar", "gpsimd")[ic % 2], (lambda db, ic: lambda e: e.dma_start(
                            out=db, in_=dnT_d[ic * 128:(ic + 1) * 128, :]))(db, ic),
                            reads=((f"dnTw{ic}",) if decode else ()) + (f"db{ic % 2}",),
                            writes=(f"db{ic % 2}",), dma=f"qdb{ic % 4}", tag=f"{name}dt{tp}_{ic}")

                    def mk_dn(db, ic, tp):
                        def f(e):
                            for half in (0, 1):
                                pd_ = p2a if half == 0 else p2b
                                tc_ = 2 * tp + half
                                lt = arA[:, ic * 1024 + tc_ * 128: ic * 1024 + tc_ * 128 + 128]
                                for hn in range(4):
                                    ins = e.matmul(out=pd_[:, hn * 512:(hn + 1) * 512],
                                                   lhsT=lt, rhs=db[:, hn * 512:(hn + 1) * 512],
                                                   start=(ic == 0), stop=(ic == ICH - 1))
                            return ins
                        return f
                    P.op("tensor", mk_dn(db, ic, tp),
                         reads=(f"db{ic % 2}", f"h{ic}", "psd0", "psd1"),
                         writes=("psd0", "psd1"), tag=f"{name}md{tp}_{ic}")
                epi_cb(2 * tp, p2a, "psd0")
                epi_cb(2 * tp + 1, p2b, "psd1")

        def rms_prep(pd, pdk, tag):
            ssq = st_a[:, 4:5]; ms = st_b[:, 4:5]; rst = st_b[:, 5:6]; hr = st_b[:, 6:7]
            P.op("scalar", (lambda pd: lambda e: e.activation(
                out=xln_tm[:], in_=pd[:], func=AF.Square, accum_out=ssq))(pd),
                reads=(pdk,), writes=("xln_tm", "st_a"), tag=tag)
            P.op("vector", lambda e: e.tensor_scalar(out=ms, in0=ssq, scalar1=1.0 / C,
                                                     scalar2=RMS_EPS, op0=ALU.mult, op1=ALU.add),
                 reads=("st_a",), writes=("st_b",), tag=tag)
            newton_rsqrt(rst, ms, "st_b", tag)
            P.op("vector", lambda e: e.tensor_scalar(out=hr, in0=rst, scalar1=0.5,
                                                     scalar2=None, op0=ALU.mult),
                 reads=("st_b",), writes=("st_b",), tag=tag)
            return hr

        def ffn1_epi(tc, pd, pdk):
            hr = rms_prep(pd, pdk, f"f1r{tc}")
            P.op("sync", (lambda tc: lambda e: e.dma_start(
                out=x_ch[:], in_=x_in[tc * 128:(tc + 1) * 128, :]))(tc),
                reads=("x_ch",), writes=("x_ch",), dma="qx", tag=f"f1el{tc}")
            P.op("vector", (lambda pd: lambda e: e.scalar_tensor_tensor(
                out=x_ch[:], in0=pd[:], scalar=hr, in1=x_ch[:],
                op0=ALU.mult, op1=ALU.add))(pd),
                reads=(pdk, "st_b", "x_ch"), writes=("x_ch", pdk), tag=f"f1ex{tc}")
            P.op("gpsimd", (lambda tc: lambda e: e.dma_start(
                out=x1_d[tc * 128:(tc + 1) * 128, :], in_=x_ch[:]))(tc),
                reads=("x_ch",), writes=(f"x1w{tc}",), dma="qx1", tag=f"f1ew{tc}")
            if stage == 1:
                P.op("gpsimd", (lambda tc: lambda e: e.dma_start(
                    out=out_d[tc * 128:(tc + 1) * 128, :], in_=x_ch[:]))(tc),
                    reads=("x_ch",), writes=(f"outw{tc}",), dma="qout", tag=f"f1eo{tc}")
            else:
                ln_apply(x_ch[:], "x_ch", LN_EPS, f"ln2c{tc}")
                xlnT_build(tc, pd, (pdk,), f"ln2t{tc}")

        def ffn2_epi(tc, pd, pdk):
            hr = rms_prep(pd, pdk, f"f2r{tc}")
            P.op("sync", (lambda tc: lambda e: e.dma_start(
                out=x_ch[:], in_=x2_d[tc * 128:(tc + 1) * 128, :]))(tc),
                reads=("x_ch",), writes=("x_ch",), dma="qx", tag=f"f2el{tc}")
            P.op("vector", (lambda pd: lambda e: e.scalar_tensor_tensor(
                out=x_ch[:], in0=pd[:], scalar=hr, in1=x_ch[:],
                op0=ALU.mult, op1=ALU.add))(pd),
                reads=(pdk, "st_b", "x_ch"), writes=("x_ch", pdk), tag=f"f2ex{tc}")
            P.op("gpsimd", (lambda tc: lambda e: e.dma_start(
                out=out_d[tc * 128:(tc + 1) * 128, :], in_=x_ch[:]))(tc),
                reads=("x_ch",), writes=(f"outw{tc}",), dma="qout", tag=f"f2ew{tc}")

        if stage == 0:
            import os as _os
            _skip_dn = _os.environ.get("SKIP_DN", "0") == "1"
            dn_cnt0 = [0]
            bk0 = [0]
            for g2 in range(22):
                for t in (2 * g2, 2 * g2 + 1):
                    decode_tile("gu", t, f"zG{t}", bk0[0] % 2); bk0[0] += 1
                    decode_tile("gu", 44 + t, f"zU{t}", bk0[0] % 2); bk0[0] += 1
                want = 0 if _skip_dn else (g2 + 1) * 64 // 22
                while dn_cnt0[0] < want:
                    t, q = divmod(dn_cnt0[0], 4)
                    decode_tile("dn", (t, q), f"zD{t}_{q}", bk0[0] % 2); bk0[0] += 1
                    dn_cnt0[0] += 1
            for tc in range(TCH):
                P.op("sync", (lambda tc: lambda e: e.dma_start(
                    out=x_ch[:], in_=x_in[tc * 128:(tc + 1) * 128, :]))(tc),
                    reads=("x_ch",), writes=("x_ch",), dma="qx", tag=f"z{tc}")
                P.op("gpsimd", (lambda tc: lambda e: e.dma_start(
                    out=out_d[tc * 128:(tc + 1) * 128, :], in_=x_ch[:]))(tc),
                    reads=("x_ch",), writes=(f"outw{tc}",), dma="qout", tag=f"zo{tc}")
        dbstg = bview(DB_O, 4096, F32)
        for tc in range(TCH) if stage >= 1 else ():
            P.op("sync", (lambda tc: lambda e: e.dma_start(
                out=dbstg, in_=x_in[tc * 128:(tc + 1) * 128, :]))(tc),
                reads=("db0", "db1"), writes=("db0", "db1"), dma="qx", tag=f"ln1l{tc}")
            ln_apply(dbstg, ("db0", "db1"), LN_EPS, f"ln1c{tc}")
            xlnT_build(tc, (p2a, p2b)[tc % 2],
                       (("psg", "psu"), ("psB",))[tc % 2], f"ln1t{tc}")
        if stage >= 1:
            emit_ffn("f1", True, ffn1_epi)


        def emit_attention():
            QG_O, KG_O, VT_O, KT_O = 0, 8192, 16384, 24576
            YS_O, MS_O, SS_O, YSQ_O = 26624, 34816, 38912, 39936
            YN_O = 40960
            WQ_O = 0
            WSTG_O = 8192
            TB_O = 12288
            AT_O = 16384
            STF_O = 18432
            RG_O = 20480
            SF_O = 24576
            RB_O = 25600
            assert RB_O + 1024 <= 26720

            def aA(off, n, dt=BF16):
                sl = arA[:, off:off + n]
                return sl if dt == BF16 else sl.bitcast(dt)

            pQ = p2a[:, 0:1024]
            pRot = p2a[:, 1024:2048]
            pY = p2b[:, 0:1024]
            pT = p2b[:, 1024:2048]

            def mm_span(e, psum_ap, c0, c1, lhsT_ap, rhs_ap, rhs_off, start, stop):
                c = c0
                ins = None
                while c < c1:
                    ce = min(c1, (c // 512 + 1) * 512)
                    ins = e.matmul(out=psum_ap[:, c:ce], lhsT=lhsT_ap,
                                   rhs=rhs_ap[:, c - rhs_off:ce - rhs_off],
                                   start=start, stop=stop)
                    c = ce
                return ins

            def phase_A(g):
                par = g % 2
                for m_name, m_base in (("v", 2 * C), ("k", C), ("q", 0)):
                    for kc in range(KCH):
                        dst = bview(WQ_O + kc * 512, 512)
                        src = wqkv_bf[kc * 128:(kc + 1) * 128,
                                      m_base + g * 512:m_base + (g + 1) * 512]
                        P.op("sync", (lambda dst, src: lambda e: e.dma_start(
                            out=dst, in_=src))(dst, src),
                            reads=("wbf_all", f"wq{kc}"), writes=(f"wq{kc}",),
                            dma=f"qwq{kc % 4}", tag=f"at{g}wl")
                    for hl in range(4):
                        h = 4 * g + hl
                        def mm_qkv(e, hl=hl):
                            ins = None
                            for kc in range(KCH):
                                lt = bview(WQ_O + kc * 512 + hl * 128, 128)
                                for sub in range(2):
                                    ins = e.matmul(out=pQ[:, sub * 512:(sub + 1) * 512],
                                                   lhsT=lt,
                                                   rhs=xlnT[:, kc * NTOK + sub * 512:kc * NTOK + (sub + 1) * 512],
                                                   start=(kc == 0), stop=(kc == KCH - 1))
                            return ins
                        P.op("tensor", mm_qkv,
                             reads=tuple(f"wq{kc}" for kc in range(KCH))
                                   + tuple(f"xlnT{kc}" for kc in range(KCH)) + ("psQ",),
                             writes=("psQ",), tag=f"at{g}m{m_name}{hl}")
                        if m_name == "v":
                            P.op("scalar", lambda e: e.activation(
                                out=xln_tm[:, 0:1024], in_=pQ, func=AF.Identity),
                                reads=("psQ", "vstg"), writes=("vstg", "psQ"), tag=f"at{g}ve{hl}")
                            def mm_vt(e):
                                ins = None
                                for j in range(TCH):
                                    ins = e.matmul(out=pRot[:, j * 128:(j + 1) * 128],
                                                   lhsT=xln_tm[:, j * 128:(j + 1) * 128],
                                                   rhs=eye_s, start=True, stop=True)
                                return ins
                            P.op("tensor", mm_vt, reads=("vstg", "c_eye", "psR"),
                                 writes=("psR",), tag=f"at{g}vT{hl}")
                            vt_dst = aA(VT_O + par * 4096 + hl * 1024, 1024)
                            P.op("scalar", (lambda vt_dst: lambda e: e.activation(
                                out=vt_dst, in_=pRot, func=AF.Identity))(vt_dst),
                                reads=("psR",) + tuple(f"vt{par}_{hl}_{j}" for j in range(TCH)),
                                writes=("psR",) + tuple(f"vt{par}_{hl}_{j}" for j in range(TCH)),
                                tag=f"at{g}vE{hl}")
                        else:
                            tb_c = bview(TB_O, 2048, F32)
                            tb_s = bview(TB_O + 2048, 2048, F32)
                            c_src = (cq_in if m_name == "q" else ck_in)
                            s_src = (sq_in if m_name == "q" else sk_in)
                            P.op("sync", (lambda c_src, h, tb_c: lambda e: e.dma_start(
                                out=tb_c[:, 0:1024], in_=c_src[h * 128:(h + 1) * 128, :]))(c_src, h, tb_c),
                                reads=("tbc",), writes=("tbc",), dma="qtabc", tag=f"at{g}tc")
                            P.op("sync", (lambda s_src, h, tb_s: lambda e: e.dma_start(
                                out=tb_s[:, 0:1024], in_=s_src[h * 128:(h + 1) * 128, :]))(s_src, h, tb_s),
                                reads=("tbs",), writes=("tbs",), dma="qtabs", tag=f"at{g}ts")
                            P.op("scalar", lambda e: e.activation(
                                out=xln_tm[:, 0:1024], in_=pQ, func=AF.Identity),
                                reads=("psQ", "qraw"), writes=("qraw",), tag=f"at{g}xe{m_name}{hl}")
                            def mm_rot(e):
                                e.matmul(out=pRot[:, 0:512], lhsT=pt_s, rhs=xln_tm[:, 0:512],
                                         start=True, stop=True)
                                return e.matmul(out=pRot[:, 512:1024], lhsT=pt_s,
                                                rhs=xln_tm[:, 512:1024], start=True, stop=True)
                            P.op("tensor", mm_rot,
                                 reads=("qraw", "c_pt", "psR"), writes=("psR",),
                                 tag=f"at{g}xr{m_name}{hl}")
                            xc0 = x_ch[:, 0:1024]
                            xc1 = x_ch[:, 1024:2048]
                            P.op("vector", (lambda tb_c: lambda e: e.tensor_tensor(
                                out=xc0, in0=xln_tm[:, 0:1024], in1=tb_c[:, 0:1024], op=ALU.mult))(tb_c),
                                reads=("qraw", "tbc", "xc0"), writes=("xc0", "psQ"), tag=f"at{g}x1{m_name}{hl}")
                            P.op("vector", (lambda tb_s: lambda e: e.tensor_tensor(
                                out=xc1, in0=pRot[:, 0:1024], in1=tb_s[:, 0:1024], op=ALU.mult))(tb_s),
                                reads=("psR", "tbs", "xc1"), writes=("xc1", "psR"), tag=f"at{g}x2{m_name}{hl}")
                            dstq = aA((QG_O if m_name == "q" else KG_O) + par * 4096 + hl * 1024, 1024)
                            P.op("vector", (lambda dstq: lambda e: e.tensor_tensor(
                                out=dstq, in0=xc0, in1=xc1, op=ALU.add))(dstq),
                                reads=("xc0", "xc1", f"{m_name}hat{par}_{hl}"),
                                writes=(f"{m_name}hat{par}_{hl}",),
                                tag=f"at{g}x3{m_name}{hl}")
                            if m_name == "k":
                                def mm_kt(e, dstq=dstq):
                                    ins = None
                                    for j in range(TCH):
                                        ins = e.matmul(out=pQ[:, j * 128:(j + 1) * 128],
                                                       lhsT=dstq[:, j * 128:(j + 1) * 128],
                                                       rhs=eye_s, start=True, stop=True)
                                    return ins
                                P.op("tensor", mm_kt,
                                     reads=(f"khat{par}_{hl}", "c_eye", "psQ"),
                                     writes=("psQ",), tag=f"at{g}kT{hl}")
                                kt_dst = aA(KT_O + par * 1024, 1024)
                                P.op("scalar", (lambda kt_dst: lambda e: e.activation(
                                    out=kt_dst, in_=pQ, func=AF.Identity))(kt_dst),
                                    reads=("psQ",) + tuple(f"kt{par}_{j}" for j in range(TCH)),
                                    writes=("psQ",) + tuple(f"kt{par}_{j}" for j in range(TCH)),
                                    tag=f"at{g}kE{hl}")
                                def mm_M(e, par=par, hl=hl):
                                    ins = None
                                    for j in range(TCH):
                                        ins = e.matmul(
                                            out=pRot[:, 0:128],
                                            lhsT=arA[:, KT_O + par * 1024 + j * 128:KT_O + par * 1024 + (j + 1) * 128],
                                            rhs=arA[:, VT_O + par * 4096 + hl * 1024 + j * 128:VT_O + par * 4096 + hl * 1024 + (j + 1) * 128],
                                            start=(j == 0), stop=(j == TCH - 1))
                                    return ins
                                P.op("tensor", mm_M,
                                     reads=tuple(f"kt{par}_{j}" for j in range(TCH))
                                           + tuple(f"vt{par}_{hl}_{j}" for j in range(TCH)) + ("psR",),
                                     writes=("psR",), tag=f"at{g}M{hl}")
                                msl = aA(MS_O + (4 * g + hl) * 256, 256, F32)
                                P.op("vector", (lambda msl, h: lambda e: e.tensor_scalar(
                                    out=msl, in0=pRot[:, 0:128], scalar1=GAMMA_P1024[h],
                                    scalar2=None, op0=ALU.mult))(msl, h),
                                    reads=("psR", f"ms{h}"), writes=(f"ms{h}", "psR"), tag=f"at{g}Me{hl}")

            def phase_B(g):
                msl_all = arA[:, MS_O + 4 * g * 256: MS_O + (4 * g + 4) * 256].bitcast(F32)
                P.op("gpsimd", (lambda msl_all, g: lambda e: e.dma_start(
                    out=cc_i[g].rearrange("(hl dk) dv -> dk hl dv", dk=128),
                    in_=msl_all.rearrange("dk (hl dv) -> dk hl dv", hl=4)))(msl_all, g),
                    reads=tuple(f"ms{4 * g + hl}" for hl in range(4)),
                    writes=(f"cci{g}",), dma="qcc", tag=f"cc{g}o")
                P.op("gpsimd", (lambda g: lambda e: e.collective_compute(
                    "AllGather", ALU.bypass,
                    replica_groups=[[0, 1], [2, 3], [4, 5], [6, 7]],
                    ins=[cc_i[g].opt()], outs=[cc_o[g].opt()]))(g),
                    reads=(f"cci{g}",), writes=(f"cco{g}",), inc=1, tag=f"cc{g}c")
                sf = bview(SF_O, 1024, F32)
                P.op("sync", (lambda sf, g: lambda e: e.dma_start(
                    out=sf.rearrange("dk (hl dv) -> dk hl dv", hl=4),
                    in_=cc_o[g][0:512, :].rearrange("(hl dk) dv -> dk hl dv", dk=128)))(sf, g),
                    reads=(f"cco{g}", "sfst"), writes=("sfst",), dma="qsf", tag=f"cc{g}l")
                ssl = aA(SS_O + (g % 2) * 512, 512)
                P.op("vector", (lambda ssl, sf: lambda e: e.tensor_scalar(
                    out=ssl, in0=sf, scalar1=alpha[:, 0:1], scalar2=None,
                    op0=ALU.mult))(ssl, sf),
                    reads=("sfst", "c_al", f"shat{g % 2}"), writes=(f"shat{g % 2}",), tag=f"cc{g}s")

            def phase_C(g):
                par = g % 2
                stf = bview(STF_O, 2048, F32)[0:8, :]
                P.op("vector", (lambda stf: lambda e: e.memset(stf, 0.0))(stf),
                     reads=("stf",), writes=("stf",), tag=f"at{g}z")
                for hl in range(4):
                    qh = aA(QG_O + par * 4096 + hl * 1024, 1024)
                    kh = aA(KG_O + par * 4096 + hl * 1024, 1024)
                    ysb = aA(YS_O + par * 4096 + hl * 1024, 1024)
                    ssl_ap = aA(SS_O + par * 512 + hl * 128, 128)
                    P.op("tensor", (lambda ssl_ap, qh: lambda e: mm_span(
                        e, pY, 0, 1024, ssl_ap, qh, 0, True, False))(ssl_ap, qh),
                        reads=(f"shat{par}", f"qhat{par}_{hl}", "psY"),
                        writes=("psY",), tag=f"at{g}cx{hl}")
                    for j in range(TCH):
                        Nj = 1024 - 128 * j
                        atb = bview(AT_O + (j % 2) * 1024, 1024)
                        P.op("tensor", (lambda kh, qh, j, Nj: lambda e: mm_span(
                            e, pT, 0, Nj,
                            kh[:, j * 128:(j + 1) * 128],
                            qh, -128 * j, True, True))(kh, qh, j, Nj),
                            reads=(f"khat{par}_{hl}", f"qhat{par}_{hl}", "psT"),
                            writes=("psT",), tag=f"at{g}sc{hl}_{j}")
                        P.op("scalar", (lambda atb, Nj: lambda e: e.activation(
                            out=atb[:, 0:Nj], in_=pT[:, 0:Nj], func=AF.Identity))(atb, Nj),
                            reads=("psT", f"atb{j % 2}"), writes=(f"atb{j % 2}", "psT"),
                            tag=f"at{g}er{hl}_{j}")
                        P.op("vector", (lambda atb: lambda e: e.tensor_tensor(
                            out=atb[:, 0:128], in0=atb[:, 0:128], in1=tri_s, op=ALU.mult))(atb),
                            reads=(f"atb{j % 2}", "c_tri"), writes=(f"atb{j % 2}",),
                            tag=f"at{g}ed{hl}_{j}")
                        P.op("tensor", (lambda atb, j, par, hl: lambda e: mm_span(
                            e, pY, 128 * j, 1024,
                            arA[:, VT_O + par * 4096 + hl * 1024 + j * 128:VT_O + par * 4096 + hl * 1024 + (j + 1) * 128],
                            atb, 128 * j, False, (j == TCH - 1)))(atb, j, par, hl),
                            reads=(f"atb{j % 2}", f"vt{par}_{hl}_{j}", "psY"),
                            writes=("psY",), tag=f"at{g}av{hl}_{j}")
                    P.op("scalar", (lambda ysb: lambda e: e.activation(
                        out=ysb, in_=pY, func=AF.Identity))(ysb),
                        reads=("psY", f"ysb{par}_{hl}"), writes=(f"ysb{par}_{hl}", "psY"),
                        tag=f"at{g}ye{hl}")
                    ysq = aA(YSQ_O, 1024)
                    P.op("scalar", (lambda ysq, ysb: lambda e: e.activation(
                        out=ysq, in_=ysb, func=AF.Square))(ysq, ysb),
                        reads=(f"ysb{par}_{hl}", "ysq"), writes=("ysq",), tag=f"at{g}yq{hl}")
                    def mm_st(e, hl=hl, ysb=ysb):
                        ysq_ = aA(YSQ_O, 1024)
                        ins = None
                        for sub in range(2):
                            ins = e.matmul(out=p2b[0:8, 1024 + sub * 512:1024 + (sub + 1) * 512],
                                           lhsT=sel_s[:, hl * 8:(hl + 1) * 8],
                                           rhs=ysb[:, sub * 512:(sub + 1) * 512],
                                           start=True, stop=False)
                        for sub in range(2):
                            ins = e.matmul(out=p2b[0:8, 1024 + sub * 512:1024 + (sub + 1) * 512],
                                           lhsT=sel_s[:, (4 + hl) * 8:(5 + hl) * 8],
                                           rhs=ysq_[:, sub * 512:(sub + 1) * 512],
                                           start=False, stop=True)
                        return ins
                    P.op("tensor", mm_st,
                         reads=(f"ysb{par}_{hl}", "ysq", "c_sel", "psT"),
                         writes=("psT",), tag=f"at{g}st{hl}")
                    P.op("vector", (lambda stf: lambda e: e.tensor_tensor(
                        out=stf, in0=stf, in1=p2b[0:8, 1024:2048], op=ALU.add))(stf),
                        reads=("stf", "psT"), writes=("stf", "psT"),
                        tag=f"at{g}sa{hl}")

            def phase_D(g):
                par = g % 2
                stf = bview(STF_O, 2048, F32)[0:8, :]
                RGv = bview(RG_O, 4096, F32)
                ex24 = RGv[0:4, 0:1024]
                varc = RGv[0:4, 1024:2048]
                mu = stf[0:4, :]
                t4 = arA[:, 41984:44032].bitcast(F32)[0:4, :]
                Rb = bview(RB_O, 1024)[0:4, :]
                Mb = arA[:, 44032:45056][0:4, :]
                P.op("vector", lambda e: e.tensor_scalar(out=stf, in0=stf, scalar1=1.0 / D,
                                                         scalar2=None, op0=ALU.mult),
                     reads=("stf",), writes=("stf",), tag=f"gn{g}")
                P.op("gpsimd", lambda e: e.dma_start(out=strow_d, in_=stf),
                     reads=("stf", "strd"), writes=("strd",), dma="qst", tag=f"gn{g}")
                P.op("sync", lambda e: e.dma_start(out=ex24, in_=strow_d[4:8, :]),
                     reads=("strd", "rg"), writes=("rg",), dma="qst2", tag=f"gn{g}")
                P.op("vector", lambda e: e.tensor_tensor(out=t4, in0=mu, in1=mu, op=ALU.mult),
                     reads=("stf", "t4"), writes=("t4",), tag=f"gn{g}")
                P.op("vector", lambda e: e.tensor_tensor(out=ex24, in0=ex24, in1=t4, op=ALU.subtract),
                     reads=("rg", "t4"), writes=("rg",), tag=f"gn{g}")
                P.op("vector", lambda e: e.tensor_scalar(out=ex24, in0=ex24, scalar1=1.0,
                                                         scalar2=GN_EPS, op0=ALU.mult, op1=ALU.add),
                     reads=("rg",), writes=("rg",), tag=f"gn{g}")
                P.op("vector", lambda e: e.tensor_copy(out=varc, in_=ex24),
                     reads=("rg",), writes=("rg",), tag=f"gn{g}")
                di = ex24.bitcast(U32)
                P.op("vector", lambda e: e.tensor_scalar(out=di, in0=varc.bitcast(U32), scalar1=1,
                                                         scalar2=None, op0=ALU.logical_shift_right),
                     reads=("rg",), writes=("rg",), tag=f"gn{g}")
                P.op("vector", lambda e: e.tensor_scalar(out=di, in0=di, scalar1=-1, scalar2=0x5F3759DF,
                                                         op0=ALU.mult, op1=ALU.add),
                     reads=("rg",), writes=("rg",), tag=f"gn{g}")
                for _ in range(3):
                    P.op("vector", lambda e: e.tensor_tensor(out=t4, in0=ex24, in1=ex24, op=ALU.mult),
                         reads=("rg", "t4"), writes=("t4",), tag=f"gn{g}")
                    P.op("vector", lambda e: e.tensor_tensor(out=t4, in0=t4, in1=varc, op=ALU.mult),
                         reads=("t4", "rg"), writes=("t4",), tag=f"gn{g}")
                    P.op("vector", lambda e: e.tensor_scalar(out=t4, in0=t4, scalar1=-0.5, scalar2=1.5,
                                                             op0=ALU.mult, op1=ALU.add),
                         reads=("t4",), writes=("t4",), tag=f"gn{g}")
                    P.op("vector", lambda e: e.tensor_tensor(out=ex24, in0=ex24, in1=t4, op=ALU.mult),
                         reads=("rg", "t4"), writes=("rg",), tag=f"gn{g}")
                P.op("vector", lambda e: e.tensor_tensor(out=mu, in0=mu, in1=ex24, op=ALU.mult),
                     reads=("stf", "rg"), writes=("stf",), tag=f"gn{g}")
                P.op("vector", lambda e: e.tensor_copy(out=Rb, in_=ex24),
                     reads=("rg", "rb"), writes=("rb",), tag=f"gn{g}")
                P.op("vector", lambda e: e.tensor_copy(out=Mb, in_=mu),
                     reads=("stf", "mb"), writes=("mb",), tag=f"gn{g}")
                for hl in range(4):
                    h = 4 * g + hl
                    ysb = aA(YS_O + par * 4096 + hl * 1024, 1024)
                    def mm_b(e, hl, rhs_tile, pdst):
                        ins = None
                        for sub in range(2):
                            ins = e.matmul(out=pdst[:, sub * 512:(sub + 1) * 512],
                                           lhsT=brc4_s[:, hl * 128:(hl + 1) * 128],
                                           rhs=rhs_tile[:, sub * 512:(sub + 1) * 512],
                                           start=True, stop=True)
                        return ins
                    P.op("tensor", (lambda hl: lambda e: mm_b(e, hl, Rb, pT))(hl),
                         reads=("rb", "c_b4", "psT"), writes=("psT",), tag=f"gn{g}bR{hl}")
                    P.op("tensor", (lambda hl: lambda e: mm_b(e, hl, Mb, pY))(hl),
                         reads=("mb", "c_b4", "psY"), writes=("psY",), tag=f"gn{g}bM{hl}")
                    xc0 = x_ch[:, 0:1024]
                    xc1 = x_ch[:, 1024:2048]
                    P.op("vector", (lambda ysb: lambda e: e.tensor_tensor(
                        out=xc0, in0=ysb, in1=pT, op=ALU.mult))(ysb),
                        reads=(f"ysb{par}_{hl}", "psT", "xc0"), writes=("xc0", "psT"),
                        tag=f"gn{g}a1{hl}")
                    P.op("vector", lambda e: e.tensor_tensor(
                        out=xc1, in0=xc0, in1=pY, op=ALU.subtract),
                        reads=("xc0", "psY", "xc1"), writes=("xc1", "psY"),
                        tag=f"gn{g}a2{hl}")
                    yn = aA(YN_O, 1024)
                    P.op("scalar", (lambda h, yn: lambda e: e.activation(
                        out=yn, in_=xc1, func=AF.Identity,
                        bias=gnwb[:, H + h:H + h + 1], scale=gnwb[:, h:h + 1]))(h, yn),
                        reads=("xc1", "c_gw", "c_gb", "yn"), writes=("yn",), tag=f"gn{g}a3{hl}")
                    for kc in range(KCH):
                        dst = bview(WQ_O + kc * 512, 128)
                        src = wg_bf[kc * 128:(kc + 1) * 128, h * 128:(h + 1) * 128]
                        P.op("sync", (lambda dst, src: lambda e: e.dma_start(
                            out=dst, in_=src))(dst, src),
                            reads=("wbf_all", f"wq{kc}"), writes=(f"wq{kc}",),
                            dma=f"qwq{kc % 4}", tag=f"gn{g}gl{hl}")
                    def mm_g(e):
                        ins = None
                        for kc in range(KCH):
                            for sub in range(2):
                                ins = e.matmul(out=pQ[:, sub * 512:(sub + 1) * 512],
                                               lhsT=bview(WQ_O + kc * 512, 128),
                                               rhs=xlnT[:, kc * NTOK + sub * 512:kc * NTOK + (sub + 1) * 512],
                                               start=(kc == 0), stop=(kc == KCH - 1))
                        return ins
                    P.op("tensor", mm_g,
                         reads=tuple(f"wq{kc}" for kc in range(KCH))
                               + tuple(f"xlnT{kc}" for kc in range(KCH)) + ("psQ",),
                         writes=("psQ",), tag=f"gn{g}mg{hl}")
                    P.op("scalar", lambda e: e.activation(out=sg_f[:], in_=pQ, func=AF.Silu),
                         reads=("psQ", "sg_f"), writes=("sg_f", "psQ"), tag=f"gn{g}si{hl}")
                    ogb = aA(YSQ_O, 1024)
                    P.op("vector", (lambda ogb, yn: lambda e: e.tensor_tensor(
                        out=ogb, in0=sg_f[:], in1=yn, op=ALU.mult))(ogb, yn),
                        reads=("sg_f", "yn", "ysq"), writes=("ysq",), tag=f"gn{g}og{hl}")
                    P.op("gpsimd", (lambda ogb, h: lambda e: e.dma_start(
                        out=og_d[h * 128:(h + 1) * 128, :], in_=ogb))(ogb, h),
                        reads=("ysq",), writes=(f"ogw{h}", "ysq"), dma="qog", tag=f"gn{g}ow{hl}")

            phase_A(0); phase_B(0)
            phase_A(1)
            phase_C(0); phase_D(0)
            phase_B(1)
            phase_A(2)
            phase_C(1); phase_D(1)
            phase_B(2)
            phase_A(3)
            phase_C(2); phase_D(2)
            phase_B(3)
            phase_C(3); phase_D(3)

            P.barrier()
            OGT_O = 0
            WP_O = 8192
            WPB_O = 12288
            for tp in range(4):
                for kc in range(KCH):
                    ogt = bview(OGT_O + (tp % 2) * 4096 + kc * 256, 256)
                    P.op("sync", (lambda ogt, kc, tp: lambda e: e.dma_start(
                        out=ogt, in_=og_d[kc * 128:(kc + 1) * 128, tp * 256:(tp + 1) * 256]))(ogt, kc, tp),
                        reads=tuple(f"ogw{h}" for h in range(H)) + (f"ogt{tp % 2}_{kc}",),
                        writes=(f"ogt{tp % 2}_{kc}",), dma=f"qot{kc % 4}", tag=f"pj{tp}o")
                for kc in range(KCH):
                    wpb = bview(WPB_O + (kc % 2) * 2048, 2048)
                    P.op("sync", (lambda wpb, kc: lambda e: e.dma_start(
                        out=wpb, in_=wproj_bf[kc * 128:(kc + 1) * 128, :]))(wpb, kc),
                        reads=("wbf_all", f"wpb{kc % 2}"), writes=(f"wpb{kc % 2}",),
                        dma=("qwp", "qwp2")[kc % 2], tag=f"pj{tp}l{kc}")
                    def mm_p(e, kc=kc, tp=tp, wpb=wpb):
                        ins = None
                        for half in (0, 1):
                            pdst = p2a if half == 0 else p2b
                            for sub in range(4):
                                ins = e.matmul(
                                    out=pdst[:, sub * 512:(sub + 1) * 512],
                                    lhsT=bview(OGT_O + (tp % 2) * 4096 + kc * 256 + half * 128, 128),
                                    rhs=wpb[:, sub * 512:(sub + 1) * 512],
                                    start=(kc == 0), stop=(kc == KCH - 1))
                        return ins
                    P.op("tensor", mm_p,
                         reads=(f"ogt{tp % 2}_{kc}", f"wpb{kc % 2}", "psP0", "psP1"),
                         writes=("psP0", "psP1"), tag=f"pj{tp}m{kc}")
                for half in (0, 1):
                    tc = 2 * tp + half
                    pdst, pk_ = (p2a, "psP0") if half == 0 else (p2b, "psP1")
                    P.op("sync", (lambda tc: lambda e: e.dma_start(
                        out=x_ch[:], in_=x1_d[tc * 128:(tc + 1) * 128, :]))(tc),
                        reads=("x_ch",), writes=("x_ch",), dma="qx", tag=f"pj{tp}x{half}")
                    P.op("vector", (lambda pdst: lambda e: e.tensor_tensor(
                        out=x_ch[:], in0=x_ch[:], in1=pdst[:, 0:2048], op=ALU.add))(pdst),
                        reads=(pk_, "x_ch"), writes=("x_ch", pk_), tag=f"pj{tp}a{half}")
                    P.op("gpsimd", (lambda tc: lambda e: e.dma_start(
                        out=x2_d[tc * 128:(tc + 1) * 128, :], in_=x_ch[:]))(tc),
                        reads=("x_ch",), writes=(f"x2w{tc}",), dma="qx1", tag=f"pj{tp}w{half}")
                    if stage == 2:
                        P.op("gpsimd", (lambda tc: lambda e: e.dma_start(
                            out=out_d[tc * 128:(tc + 1) * 128, :], in_=x_ch[:]))(tc),
                            reads=("x_ch",), writes=(f"outw{tc}",), dma="qout", tag=f"pj{tp}q{half}")
                    else:
                        ln_apply(x_ch[:], "x_ch", LN_EPS, f"ln3c{tc}")
                        xlnT_build(tc, pdst, (pk_,), f"ln3t{tc}")

        if stage >= 2:
            P.barrier()
            emit_attention()


        if stage >= 3:
            P.barrier()
            emit_ffn("f2", False, ffn2_epi)

        if _DEBUG:
            d_xln = nc.dram_tensor("d_xln", [NTOK, C], BF16, kind="ExternalOutput").ap()
            d_w = nc.dram_tensor("d_w", [256, C], BF16, kind="ExternalOutput").ap()
            d_h = nc.dram_tensor("d_h", [128, 2048], BF16, kind="ExternalOutput").ap()
            P.barrier()
            P.op("gpsimd", lambda e: e.dma_start(out=d_xln, in_=xln_d), writes=("dbg1",), dma="qdbg")
            P.op("gpsimd", lambda e: e.dma_start(out=d_w[0:128, :], in_=gu_dec[128:256, :]), writes=("dbg2",), dma="qdbg")
            P.op("gpsimd", lambda e: e.dma_start(out=d_w[128:256, :], in_=gu_dec[FI + 128:FI + 256, :]), writes=("dbg3",), dma="qdbg")
            P.op("gpsimd", lambda e: e.dma_start(out=d_h, in_=arA[:, 0:2048]), reads=("h0", "h1"), writes=("dbg4",), dma="qdbg")
        P.barrier()
        P.op("gpsimd", lambda e: e.engine_nop(),
             reads=tuple(f"outw{tc}" for tc in range(TCH)) +
                   (("dbg1", "dbg2", "dbg3", "dbg4") if _DEBUG else ()),
             writes=(), tag="fin")

        for st_name in sorted(P.streams):
            sems[st_name] = es.enter_context(nc.semaphore(f"sd_{st_name}"))
        P.schedule()
        n, w = P.stats()
        build_program.last_stats = (n, w)

        with nc.Block() as block:
            @block.sync
            def _(eng):
                P.emit_stream("sync", eng, sems)

            @block.vector
            def _(eng):
                P.emit_stream("vector", eng, sems)

            @block.scalar
            def _(eng):
                P.emit_stream("scalar", eng, sems)

            @block.tensor
            def _(eng):
                P.emit_stream("tensor", eng, sems)

            @block.gpsimd
            def _(eng):
                P.emit_stream("gpsimd", eng, sems)

    return nc





def kernel(L0_attn_x, L0_attn_w_qkv, L0_attn_w_g, L0_attn_w_proj,
           L0_attn_gn_weight, L0_attn_gn_bias,
           L0_ffn_gate_up_packed, L0_ffn_gate_up_scales,
           L0_ffn_down_packed, L0_ffn_down_scales):
    x = np.asarray(L0_attn_x, dtype=np.float32)
    wqkv = np.ascontiguousarray(np.asarray(L0_attn_w_qkv, dtype=np.float32))
    wg = np.ascontiguousarray(np.asarray(L0_attn_w_g, dtype=np.float32))
    wproj = np.ascontiguousarray(np.asarray(L0_attn_w_proj, dtype=np.float32))
    gnw = np.ascontiguousarray(np.asarray(L0_attn_gn_weight, np.float32).reshape(H, 128).T)
    gnb = np.ascontiguousarray(np.asarray(L0_attn_gn_bias, np.float32).reshape(H, 128).T)
    gu_pk = np.ascontiguousarray(np.asarray(L0_ffn_gate_up_packed, np.int32))
    gu_sc = np.ascontiguousarray(np.asarray(L0_ffn_gate_up_scales).T)
    dn_pk = np.ascontiguousarray(np.asarray(L0_ffn_down_packed, np.int32))
    dn_sc = np.ascontiguousarray(np.asarray(L0_ffn_down_scales).T)
    PT, tri, sel, brc4, eye = _host_consts()
    tabs = {0: _host_tables(0), 1: _host_tables(1)}

    nc = build_program(_STAGE)
    in_maps = []
    for c in range(8):
        b, half = c // 2, c % 2
        cq, sq, ck, sk = tabs[half]
        in_maps.append({
            "x_own": np.ascontiguousarray(x[b, half * NTOK:(half + 1) * NTOK, :]),
            "w_qkv": wqkv, "w_g": wg, "w_proj": wproj,
            "gnw": gnw, "gnb": gnb,
            "gu_pk": gu_pk, "gu_sc": gu_sc, "dn_pk": dn_pk, "dn_sc": dn_sc,
            "cq": cq, "sq": sq, "ck": ck, "sk": sk,
            "pt": PT, "tri": tri, "sel": sel, "eye": eye, "brc4": brc4,
            "alpha": np.full((128, 1), float(half), np.float32),
        })
    res = run_bass_kernel_spmd(nc, in_maps, core_ids=list(range(8)))
    out = np.empty((B, T, C), np.float32)
    for c in range(8):
        b, half = c // 2, c % 2
        out[b, half * NTOK:(half + 1) * NTOK, :] = res.results[c]["out"]
    return out


# version 12
# speedup vs baseline: 1.6081x; 1.1374x over previous
import numpy as np
import ml_dtypes

import concourse.bass as bass
import concourse.mybir as mybir
from concourse.bass_utils import run_bass_kernel_spmd
from contextlib import ExitStack

F32 = mybir.dt.float32
BF16 = mybir.dt.bfloat16
F16 = mybir.dt.float16
U16 = mybir.dt.uint16
U8 = mybir.dt.uint8
U32 = mybir.dt.uint32
I32 = mybir.dt.int32
FP8 = mybir.dt.float8e4
AF = mybir.ActivationFunctionType
ALU = mybir.AluOpType
AX = mybir.AxisListType

B, T, C = 4, 2048, 2048
H, D = 16, 128
FI = 5632
NTOK = 1024
TCH = 8
KCH = 16
ICH = 44
GU = 2 * FI
LN_EPS = 1e-5
GN_EPS = 1e-5
RMS_EPS = 1e-6

_STAGE = 3
_DEBUG = False

ENGINES = ("sync", "vector", "scalar", "tensor", "gpsimd")



class Prog:

    def __init__(self):
        self.items = []
        self.streams = set()

    def op(self, eng, fn, reads=(), writes=(), dma=None, inc=None, tag=""):
        if inc is None:
            inc = 16 if dma else 1
        if dma:
            self.streams.add(dma)
        self.items.append(["op", eng, fn, tuple(reads), tuple(writes), inc, tag,
                           None, None, dma])

    def barrier(self):
        self.items.append(["bar"])

    def schedule(self):
        cnt = {}
        writer = {}
        readers = {}
        bar_pend = {e: {} for e in ENGINES}
        for it in self.items:
            if it[0] == "bar":
                snap = dict(cnt)
                for e in ENGINES:
                    for s2, v in snap.items():
                        if s2 != e and v > 0 and v > bar_pend[e].get(s2, -1):
                            bar_pend[e][s2] = v
                continue
            _, eng, fn, reads, writes, inc, tag, _, _, stream = it
            sem = stream if stream else eng
            need = {}

            def add(dep):
                s2, v = dep
                if s2 == eng and s2 == "tensor":
                    return
                if v > need.get(s2, -1):
                    need[s2] = v
            for k in reads:
                w = writer.get(k)
                if w:
                    add(w)
            for k in writes:
                w = writer.get(k)
                if w:
                    add(w)
                for r in readers.get(k, ()):
                    add(r)
            if stream and cnt.get(sem, 0) > 0:
                if cnt[sem] > need.get(sem, -1):
                    need[sem] = cnt[sem]
            for s2, v in bar_pend[eng].items():
                if v > need.get(s2, -1):
                    need[s2] = v
            bar_pend[eng] = {}
            cnt[sem] = cnt.get(sem, 0) + inc
            it[7] = cnt[sem]
            it[8] = need
            for k in writes:
                writer[k] = (sem, cnt[sem])
                readers[k] = []
            for k in reads:
                readers.setdefault(k, []).append((sem, cnt[sem]))
        hi = {}
        for it in self.items:
            if it[0] == "bar":
                continue
            eng, need = it[1], it[8]
            pruned = {}
            for s2, v in need.items():
                if hi.get((eng, s2), -1) >= v:
                    continue
                pruned[s2] = v
                hi[(eng, s2)] = v
            it[8] = pruned

    def emit_stream(self, eng_name, eng_obj, sems):
        for it in self.items:
            if it[0] == "bar" or it[1] != eng_name:
                continue
            for s2, v in it[8].items():
                eng_obj.wait_ge(sems[s2], v)
            sem = it[9] if it[9] else eng_name
            it[2](eng_obj).then_inc(sems[sem], it[5])

    def stats(self):
        n = {e: 0 for e in ENGINES}
        w = {e: 0 for e in ENGINES}
        for it in self.items:
            if it[0] == "bar":
                continue
            n[it[1]] += 1
            w[it[1]] += len(it[8])
        return n, w



def _host_tables(half):
    f = np.float64
    ds = (np.arange(0, C, 2, dtype=f) + 0.4 * C) / (1.4 * C)
    pos = np.arange(T, dtype=f)
    scale = ds[None, :] ** (pos[:, None] / 512.0)
    inv_freq = 1.0 / 10000.0 ** (np.arange(0, C, 2, dtype=f) / C)
    sinus = pos[:, None] * inv_freq[None, :]
    sin = np.repeat(np.sin(sinus) * scale, 2, axis=-1)
    cos = np.repeat(np.cos(sinus) * scale, 2, axis=-1)
    gamma = 1.0 - 2.0 ** (-5.0 - np.arange(H, dtype=f))
    gch = np.repeat(gamma, D)
    t0 = half * NTOK
    tl = np.arange(t0, t0 + NTOK, dtype=f)
    qfac = gch[:, None] ** (tl - t0)[None, :] * D ** -0.5
    kfac = gch[:, None] ** (t0 - tl)[None, :]
    cq = cos[t0:t0 + NTOK].T * qfac
    sq = sin[t0:t0 + NTOK].T * qfac
    ck = cos[t0:t0 + NTOK].T * kfac
    sk = sin[t0:t0 + NTOK].T * kfac
    return tuple(np.ascontiguousarray(a.astype(np.float32)) for a in (cq, sq, ck, sk))


def _host_consts():
    bf = ml_dtypes.bfloat16
    PT = np.zeros((128, 128), dtype=np.float32)
    ii = np.arange(0, 128, 2)
    PT[ii + 1, ii] = -1.0
    PT[ii, ii + 1] = 1.0
    tri = (np.arange(128)[None, :] >= np.arange(128)[:, None]).astype(np.float32)
    sel = np.zeros((128, 64), dtype=np.float32)
    for j in range(8):
        sel[:, j * 8 + j] = 1.0
    brc4 = np.zeros((4, 4 * 128), dtype=np.float32)
    for r in range(4):
        brc4[r, r * 128:(r + 1) * 128] = 1.0
    eye = np.eye(128, dtype=np.float32)
    return (PT.astype(bf), tri.astype(bf), sel.astype(bf), brc4.astype(bf), eye.astype(bf))


GAMMA_P1024 = [float((1.0 - 2.0 ** (-5.0 - h)) ** 1024) for h in range(H)]



def build_program(stage):
    nc = bass.Bass("TRN2", target_bir_lowering=False, debug=False)
    es = ExitStack()

    def din(name, shape, dt):
        return nc.dram_tensor(name, shape, dt, kind="ExternalInput").ap()

    x_in = din("x_own", [NTOK, C], F32)
    w_qkv = din("w_qkv", [C, 3 * C], F32)
    w_g = din("w_g", [C, C], F32)
    w_proj = din("w_proj", [C, C], F32)
    gnw_in = din("gnw", [128, H], F32)
    gnb_in = din("gnb", [128, H], F32)
    gu_pk = din("gu_pk", [GU, C // 8], I32)
    gu_sc = din("gu_sc", [GU, KCH], F16)
    dn_pk = din("dn_pk", [C, FI // 8], I32)
    dn_sc = din("dn_sc", [C, ICH], F16)
    cq_in = din("cq", [C, NTOK], F32)
    sq_in = din("sq", [C, NTOK], F32)
    ck_in = din("ck", [C, NTOK], F32)
    sk_in = din("sk", [C, NTOK], F32)
    pt_in = din("pt", [128, 128], BF16)
    tri_in = din("tri", [128, 128], BF16)
    sel_in = din("sel", [128, 64], BF16)
    eye_in = din("eye", [128, 128], BF16)
    brc4_in = din("brc4", [4, 4 * 128], BF16)
    alpha_in = din("alpha", [128, 1], F32)

    out_d = nc.dram_tensor("out", [NTOK, C], F32, kind="ExternalOutput").ap()

    gu_dec = nc.dram_tensor("gu_dec", [GU, C], BF16).ap()
    dn_dec = nc.dram_tensor("dn_dec", [C, FI], BF16).ap()
    x1_d = nc.dram_tensor("x1_d", [NTOK, C], F32).ap()
    x2_d = nc.dram_tensor("x2_d", [NTOK, C], F32).ap()
    vk_d = nc.dram_tensor("vk_d", [2 * 128, NTOK], BF16).ap()
    og_d = nc.dram_tensor("og_d", [C, NTOK], BF16).ap()
    strow_d = nc.dram_tensor("strow_d", [8, NTOK], F32).ap()
    wproj_bf = nc.dram_tensor("wproj_bf", [C, C], BF16).ap()
    guT_d = nc.dram_tensor("guT_d", [C, GU], BF16).ap()
    dnT_d = nc.dram_tensor("dnT_d", [FI, C], BF16).ap()
    cc_i = [nc.dram_tensor(f"cc_i{g}", [4 * D, D], F32).ap() for g in range(4)]
    cc_o = [nc.dram_tensor(f"cc_o{g}", [2 * 4 * D, D], F32).ap() for g in range(4)]

    P = Prog()

    with es:
        def sb(name, shape, dt):
            return es.enter_context(nc.sbuf_tensor(name, shape, dt))

        sems = {e: es.enter_context(nc.semaphore(f"se_{e}")) for e in ENGINES}

        xlnT = sb("xlnT", [128, KCH * NTOK], BF16)
        arA = sb("arA", [128, ICH * NTOK], BF16)
        arB = sb("arB", [128, 27296], BF16)
        consts = sb("consts", [128, 448], BF16)
        brc4_s = sb("brc4_s", [4, 4 * 128], BF16)
        alpha = sb("alpha_s", [128, 1], F32)
        gnwb = sb("gnwb", [128, 2 * H], F32)
        x_ch = sb("x_ch", [128, C], F32)
        xln_tm = sb("xln_tm", [128, C], BF16)
        sg_f = sb("sg_f", [128, NTOK], F32)
        st_a = sb("st_a", [128, 8], F32)
        st_b = sb("st_b", [128, 8], F32)
        st_c = sb("st_c", [128, 8], F32)
        st_n = sb("st_n", [128, 8], F32)

        p2a = es.enter_context(nc.psum_tensor("p2a", [128, 2048], F32))
        p2b = es.enter_context(nc.psum_tensor("p2b", [128, 2048], F32))

        pt_s = consts[:, 0:128]
        tri_s = consts[:, 128:256]
        sel_s = consts[:, 256:320]
        eye_s = consts[:, 320:448]

        for ap_dst, ap_src, key in ((consts[:, 0:128], pt_in, "c_pt"),
                                    (consts[:, 128:256], tri_in, "c_tri"),
                                    (consts[:, 256:320], sel_in, "c_sel"),
                                    (consts[:, 320:448], eye_in, "c_eye"),
                                    (brc4_s[:], brc4_in, "c_b4"),
                                    (alpha[:], alpha_in, "c_al"),
                                    (gnwb[:, 0:H], gnw_in, "c_gw"),
                                    (gnwb[:, H:2 * H], gnb_in, "c_gb")):
            P.op("sync", (lambda d, s: lambda e: e.dma_start(out=d, in_=s))(ap_dst, ap_src),
                 writes=(key,), dma="qconst", tag="const")


        def newton_rsqrt(dst, src, key, tag):
            di = dst.bitcast(U32)
            si = src.bitcast(U32)
            t = st_n[:, 0:dst.shape[1]]
            P.op("vector", lambda e: e.tensor_scalar(out=di, in0=si, scalar1=1, scalar2=None,
                                                     op0=ALU.logical_shift_right),
                 reads=(key,), writes=(key,), tag=tag)
            P.op("vector", lambda e: e.tensor_scalar(out=di, in0=di, scalar1=-1, scalar2=0x5F3759DF,
                                                     op0=ALU.mult, op1=ALU.add),
                 reads=(key,), writes=(key,), tag=tag)
            for _ in range(3):
                P.op("vector", lambda e: e.tensor_tensor(out=t, in0=dst, in1=dst, op=ALU.mult),
                     reads=(key,), writes=("st_n",), tag=tag)
                P.op("vector", lambda e: e.tensor_tensor(out=t, in0=t, in1=src, op=ALU.mult),
                     reads=("st_n", key), writes=("st_n",), tag=tag)
                P.op("vector", lambda e: e.tensor_scalar(out=t, in0=t, scalar1=-0.5, scalar2=1.5,
                                                         op0=ALU.mult, op1=ALU.add),
                     reads=("st_n",), writes=("st_n",), tag=tag)
                P.op("vector", lambda e: e.tensor_tensor(out=dst, in0=dst, in1=t, op=ALU.mult),
                     reads=(key, "st_n"), writes=(key,), tag=tag)

        def ln_apply(src_ap, src_key, eps, tag):
            skeys = src_key if isinstance(src_key, tuple) else (src_key,)
            src_key = skeys[0]
            su = st_a[:, 0:1]; mu = st_a[:, 1:2]; ssq = st_a[:, 2:3]
            var = st_b[:, 0:1]; rst = st_b[:, 1:2]; nmr = st_b[:, 2:3]
            t0 = st_c[:, 0:1]
            P.op("vector", lambda e: e.reduce_sum(out=su, in_=src_ap, axis=AX.X),
                 reads=skeys, writes=("st_a",), tag=tag)
            P.op("scalar", lambda e: e.activation(out=xln_tm[:], in_=src_ap,
                                                  func=AF.Square, accum_out=ssq),
                 reads=skeys, writes=("xln_tm", "st_a"), tag=tag)
            P.op("vector", lambda e: e.tensor_scalar(out=mu, in0=su, scalar1=1.0 / C,
                                                     scalar2=None, op0=ALU.mult),
                 reads=("st_a",), writes=("st_a",), tag=tag)
            P.op("vector", lambda e: e.tensor_tensor(out=t0, in0=mu, in1=mu, op=ALU.mult),
                 reads=("st_a",), writes=("st_c",), tag=tag)
            P.op("vector", lambda e: e.tensor_scalar(out=var, in0=ssq, scalar1=1.0 / C,
                                                     scalar2=eps, op0=ALU.mult, op1=ALU.add),
                 reads=("st_a",), writes=("st_b",), tag=tag)
            P.op("vector", lambda e: e.tensor_tensor(out=var, in0=var, in1=t0, op=ALU.subtract),
                 reads=("st_b", "st_c"), writes=("st_b",), tag=tag)
            newton_rsqrt(rst, var, "st_b", tag)
            P.op("vector", lambda e: e.tensor_scalar(out=nmr, in0=mu, scalar1=-1.0, scalar2=rst,
                                                     op0=ALU.mult, op1=ALU.mult),
                 reads=("st_a", "st_b"), writes=("st_b",), tag=tag)
            P.op("scalar", lambda e: e.activation(out=xln_tm[:], in_=src_ap, func=AF.Identity,
                                                  bias=nmr, scale=rst),
                 reads=skeys + ("st_b",), writes=("xln_tm",), tag=tag)

        def xlnT_build(tc, psum, pkeys, tag):
            def mm_tr(e):
                ins = None
                for kc in range(KCH):
                    ins = e.matmul(out=psum[:, kc * 128:(kc + 1) * 128],
                                   lhsT=xln_tm[:, kc * 128:(kc + 1) * 128],
                                   rhs=eye_s, start=True, stop=True)
                return ins
            P.op("tensor", mm_tr, reads=("xln_tm", "c_eye") + pkeys,
                 writes=pkeys, tag=tag)
            for kc in range(KCH):
                P.op("scalar", (lambda kc: lambda e: e.activation(
                    out=xlnT[:, kc * NTOK + tc * 128: kc * NTOK + (tc + 1) * 128],
                    in_=psum[:, kc * 128:(kc + 1) * 128], func=AF.Identity))(kc),
                    reads=pkeys, writes=(f"xlnT{kc}",), tag=tag)

        GB_O = 0
        DB_O = 16384
        TAB_O = 20480
        BK_O = 22016
        BK_SZ = 2640
        assert BK_O + 2 * BK_SZ <= 27296

        def bview(off, n, dt=BF16):
            sl = arB[:, off:off + n]
            if dt == BF16:
                return sl
            return sl.bitcast(dt)

        x_chb = x_ch[:].bitcast(BF16)

        def decode_tile(kind, args, tag, bank, phase="all"):
            if kind == "gu":
                ti = args
                n32, ngrp = 256, 16
                pk_src = gu_pk[ti * 128:(ti + 1) * 128, :]
                sc_src = gu_sc[ti * 128:(ti + 1) * 128, :]
                dec_dst = gu_dec[ti * 128:(ti + 1) * 128, :]
                fkey = f"decg{ti}"
            else:
                ti, qt = args
                n32, ngrp = 176, 11
                pk_src = dn_pk[ti * 128:(ti + 1) * 128, qt * n32:(qt + 1) * n32]
                sc_src = dn_sc[ti * 128:(ti + 1) * 128, qt * ngrp:(qt + 1) * ngrp]
                dec_dst = dn_dec[ti * 128:(ti + 1) * 128, qt * 1408:(qt + 1) * 1408]
                fkey = f"decd{ti}_{qt}"
            nW = n32 * 2
            nib = n32 * 8
            BK = BK_O + bank * BK_SZ
            tt = bview(TAB_O + 0 * 512, nW, U16)
            ta = bview(TAB_O + 1 * 512, nW, U16)
            tb = bview(TAB_O + 2 * 512, nW, U16)
            tds = [bview(BK + 592 + j * 512, nW, U16) for j in range(4)]
            w0 = bview(TAB_O + 0 * 512, nW, U16)
            w1 = bview(TAB_O + 1 * 512, nW, U16)
            E0 = bview(BK + 592, 2 * nW, U32)
            E1 = bview(BK + 592 + 2 * 512, 2 * nW, U32)
            nat32 = bview(TAB_O, 2 * nW, U32)
            tmp_nat = bview(BK + 592, nib)
            fin_nat = x_chb[:, bank * 2048:bank * 2048 + nib]
            scf = bview(BK + 512, 2 * ngrp, F32)
            sch = bview(BK + 512 + 64, ngrp, F16)
            pk = bview(BK, 2 * n32, I32)
            wu = pk.bitcast(U16)
            kp, ks, kc_, kt, kab = f"dk{bank}_pk", f"dk{bank}_sch", f"dk{bank}_sc", "dk_t", "dk_a"
            kd = [f"dk{bank}_d{j}" for j in range(4)]
            kdt = tuple(kd)
            kn, kf, kq = f"dk{bank}_nt", f"dk{bank}_F", f"dkQ{bank}"

            if phase in ("pre", "all"):
                P.op("sync", (lambda pk, pk_src: lambda e: e.dma_start(out=pk, in_=pk_src))(pk, pk_src),
                     reads=(kp,), writes=(kp,), dma=f"qpk{bank}", tag=tag)
                P.op("scalar", (lambda sch, sc_src: lambda e: e.dma_start(out=sch, in_=sc_src))(sch, sc_src),
                     reads=(ks,), writes=(ks,), dma=f"qsch{bank}", tag=tag)
                if phase == "pre":
                    return
            P.op("scalar", (lambda scf, sch, ngrp: lambda e: e.activation(
                out=scf[:, 0:ngrp], in_=sch, func=AF.Identity))(scf, sch, ngrp),
                reads=(ks,), writes=(kc_,), tag=tag)
            for j in range(4):
                td = tds[j]
                P.op("vector", (lambda j, tt, wu: lambda e: e.tensor_scalar(
                    out=tt, in0=wu, scalar1=4 * j, scalar2=7,
                    op0=ALU.logical_shift_right, op1=ALU.bitwise_and))(j, tt, wu),
                    reads=(kp,), writes=("dk_t",), tag=tag)
                P.op("vector", (lambda ta, tt: lambda e: e.tensor_scalar(
                    out=ta, in0=tt, scalar1=4, scalar2=48,
                    op0=ALU.mult, op1=ALU.add))(ta, tt),
                    reads=("dk_t",), writes=("dk_a",), tag=tag)
                P.op("vector", (lambda tb, tt: lambda e: e.tensor_scalar(
                    out=tb, in0=tt, scalar1=48, scalar2=None, op0=ALU.mult))(tb, tt),
                    reads=("dk_t",), writes=("dk_b",), tag=tag)
                P.op("vector", (lambda td, ta, tb: lambda e: e.tensor_tensor(
                    out=td, in0=ta, in1=tb, op=ALU.min))(td, ta, tb),
                    reads=("dk_a", "dk_b"), writes=(kd[j],), tag=tag)
                P.op("vector", (lambda j, tb, wu: lambda e: e.tensor_scalar(
                    out=tb, in0=wu,
                    scalar1=(4 if j == 0 else 4 * j - 4), scalar2=128,
                    op0=(ALU.logical_shift_left if j == 0 else ALU.logical_shift_right),
                    op1=ALU.bitwise_and))(j, tb, wu),
                    reads=(kp,), writes=("dk_b",), tag=tag)
                P.op("vector", (lambda td, tb: lambda e: e.tensor_tensor(
                    out=td, in0=td, in1=tb, op=ALU.add))(td, tb),
                    reads=(kd[j], "dk_b"), writes=(kd[j],), tag=tag)
            P.op("vector", (lambda tb, tds: lambda e: e.tensor_scalar(
                out=tb, in0=tds[1], scalar1=8, scalar2=None,
                op0=ALU.logical_shift_left))(tb, tds),
                reads=(kd[1], "dk_b"), writes=("dk_b",), tag=tag)
            P.op("vector", (lambda w0, tb, tds: lambda e: e.tensor_tensor(
                out=w0, in0=tb, in1=tds[0], op=ALU.bitwise_or))(w0, tb, tds),
                reads=("dk_b", kd[0], "dk_t"), writes=("dk_t",), tag=tag)
            P.op("vector", (lambda tb, tds: lambda e: e.tensor_scalar(
                out=tb, in0=tds[3], scalar1=8, scalar2=None,
                op0=ALU.logical_shift_left))(tb, tds),
                reads=(kd[3], "dk_b"), writes=("dk_b",), tag=tag)
            P.op("vector", (lambda w1, tb, tds: lambda e: e.tensor_tensor(
                out=w1, in0=tb, in1=tds[2], op=ALU.bitwise_or))(w1, tb, tds),
                reads=("dk_b", kd[2], "dk_a"), writes=("dk_a",), tag=tag)
            P.op("vector", (lambda E0, w0, nW: lambda e: e.tensor_copy(
                out=E0[:, 0:nW], in_=w0))(E0, w0, nW),
                reads=("dk_t", kd[0]), writes=(kd[0], kd[1]), tag=tag)
            P.op("vector", (lambda E1, w1, nW: lambda e: e.tensor_copy(
                out=E1[:, 0:nW], in_=w1))(E1, w1, nW),
                reads=("dk_a", kd[2]), writes=(kd[2], kd[3]), tag=tag)
            P.op("vector", (lambda E1, nW: lambda e: e.tensor_scalar(
                out=E1[:, 0:nW], in0=E1[:, 0:nW], scalar1=16, scalar2=None,
                op0=ALU.logical_shift_left))(E1, nW),
                reads=(kd[2], kd[3]), writes=(kd[2], kd[3]), tag=tag)
            P.op("vector", (lambda nat32, E0, E1, nW: lambda e: e.tensor_tensor(
                out=nat32[:, 0:nW], in0=E1[:, 0:nW], in1=E0[:, 0:nW],
                op=ALU.bitwise_or))(nat32, E0, E1, nW),
                reads=kdt + ("dk_t", "dk_a"),
                writes=("dk_t", "dk_a"), tag=tag)
            fp8all = nat32[:, 0:nW].bitcast(FP8)
            P.op("vector", (lambda tmp_nat, fp8all, nib: lambda e: e.tensor_copy(
                out=tmp_nat[:, 0:nib], in_=fp8all))(tmp_nat, fp8all, nib),
                reads=("dk_t", "dk_a") + kdt,
                writes=(kn,) + kdt, tag=tag)
            for g in range(ngrp):
                P.op("scalar", (lambda g, fin_nat, tmp_nat, scf: lambda e: e.activation(
                    out=fin_nat[:, g * 128:(g + 1) * 128],
                    in_=tmp_nat[:, g * 128:(g + 1) * 128],
                    func=AF.Identity, scale=scf[:, g:g + 1]))(g, fin_nat, tmp_nat, scf),
                    reads=(kn, kc_) + kdt,
                    writes=(kf,), tag=tag)
            if kind == "gu":
                up = ti >= 44
                g2 = (ti - 44) // 2 if up else ti // 2
                par = g2 % 2
                ic01 = ti % 2
                def mm_tr(e, fin_nat=fin_nat):
                    ins = None
                    for kc in range(KCH):
                        ins = e.matmul(out=p2b[:, kc * 128:(kc + 1) * 128],
                                       lhsT=fin_nat[:, kc * 128:(kc + 1) * 128],
                                       rhs=eye_s, start=True, stop=True)
                    return ins
                P.op("tensor", mm_tr, reads=(kf, "c_eye", "psB"), writes=("psB",),
                     tag=tag + "T")
                gdst = arB[:, GB_O + par * 8192 + (4096 if up else 0):
                           GB_O + par * 8192 + (4096 if up else 0) + 4096]
                gdst3 = gdst.rearrange("p (kc c) -> p kc c", c=256)[:, :, ic01 * 128:(ic01 + 1) * 128]
                psrc3 = p2b[:, 0:2048].rearrange("p (kc c) -> p kc c", c=128)
                gk = ("gu" if up else "gb")
                P.op("scalar", (lambda gdst3, psrc3: lambda e: e.activation(
                    out=gdst3, in_=psrc3, func=AF.Identity))(gdst3, psrc3),
                    reads=("psB",) + tuple(f"{gk}{par}_{kc}" for kc in range(KCH)),
                    writes=("psB",) + tuple(f"{gk}{par}_{kc}" for kc in range(KCH)),
                    tag=tag + "E")
            P.op("gpsimd", (lambda dec_dst, fin_nat, nib: lambda e: e.dma_start(
                out=dec_dst, in_=fin_nat[:, 0:nib]))(dec_dst, fin_nat, nib),
                reads=(kf,), writes=(fkey, kf, kq), dma=f"qdec{bank}", tag=tag)


        def emit_ffn(name, decode, epi_cb):
            LOOKAHEAD = 1
            dq = []
            _dn = 0
            for g in range(22):
                for t in (2 * g, 2 * g + 1):
                    dq.append((g, "gu", t, f"{name}G{t}"))
                    dq.append((g, "gu", 44 + t, f"{name}U{t}"))
                want = (g + 1) * 64 // 22
                while _dn < want:
                    t, q = divmod(_dn, 4)
                    dq.append((g, "dn", (t, q), f"{name}D{t}_{q}"))
                    _dn += 1
            dq_pos = [0]

            def emit_decode_upto(gmax):
                while dq_pos[0] < len(dq) and dq[dq_pos[0]][0] <= gmax:
                    i = dq_pos[0]
                    if i == 0:
                        decode_tile(dq[0][1], dq[0][2], dq[0][3], 0, phase="pre")
                    if i + 1 < len(dq):
                        _, k1, a1, t1 = dq[i + 1]
                        decode_tile(k1, a1, t1, (i + 1) % 2, phase="pre")
                    _, kind, args, tag = dq[i]
                    decode_tile(kind, args, tag, i % 2, phase="body")
                    dq_pos[0] += 1

            if decode:
                emit_decode_upto(0)
            for g2 in range(22):
                par = g2 % 2
                for kc in range(KCH):
                    dg = bview(GB_O + par * 8192 + kc * 256, 256)
                    du = bview(GB_O + par * 8192 + 4096 + kc * 256, 256)
                    if decode:
                        P.op("gpsimd", (lambda dg, kc, g2: lambda e: e.dma_start(
                            out=guT_d[kc * 128:(kc + 1) * 128, g2 * 256:(g2 + 1) * 256],
                            in_=dg))(dg, kc, g2),
                            reads=(f"gb{par}_{kc}",), writes=(f"guTw{g2}_{kc}",),
                            dma=f"qgw{kc % 4}", tag=f"{name}gw{g2}")
                        P.op("gpsimd", (lambda du, kc, g2: lambda e: e.dma_start(
                            out=guT_d[kc * 128:(kc + 1) * 128, FI + g2 * 256:FI + (g2 + 1) * 256],
                            in_=du))(du, kc, g2),
                            reads=(f"gu{par}_{kc}",), writes=(f"guTw{g2}_{kc}u",),
                            dma=f"qgw{(kc + 1) % 4}", tag=f"{name}uw{g2}")
                    else:
                        P.op(("scalar", "gpsimd")[kc % 2], (lambda dg, kc, g2: lambda e: e.dma_start(
                            out=dg, in_=guT_d[kc * 128:(kc + 1) * 128, g2 * 256:(g2 + 1) * 256]))(dg, kc, g2),
                            reads=(f"gb{par}_{kc}",),
                            writes=(f"gb{par}_{kc}",), dma=f"qgt{kc % 8}", tag=f"{name}gt{g2}")
                        P.op(("gpsimd", "scalar")[kc % 2], (lambda du, kc, g2: lambda e: e.dma_start(
                            out=du, in_=guT_d[kc * 128:(kc + 1) * 128, FI + g2 * 256:FI + (g2 + 1) * 256]))(du, kc, g2),
                            reads=(f"gu{par}_{kc}",),
                            writes=(f"gu{par}_{kc}",), dma=f"qut{kc % 8}", tag=f"{name}ut{g2}")
                for ic01 in (0, 1):
                    ic = 2 * g2 + ic01
                    pg = p2a[:, 0:1024]
                    pu = p2a[:, 1024:2048]
                    pgk, puk = "psg", "psu"

                    def mk_mm(par, ic01, which, pdst):
                        def f(e):
                            for kc in range(KCH):
                                base = GB_O + par * 8192 + which * 4096 + kc * 256 + ic01 * 128
                                for hn in range(2):
                                    ins = e.matmul(out=pdst[:, hn * 512:(hn + 1) * 512],
                                                   lhsT=arB[:, base:base + 128],
                                                   rhs=xlnT[:, kc * NTOK + hn * 512:kc * NTOK + (hn + 1) * 512],
                                                   start=(kc == 0), stop=(kc == KCH - 1))
                            return ins
                        return f
                    P.op("tensor", mk_mm(par, ic01, 0, pg),
                         reads=tuple(f"gb{par}_{kc}" for kc in range(KCH))
                               + tuple(f"xlnT{kc}" for kc in range(KCH)) + (pgk,),
                         writes=(pgk,), tag=f"{name}mg{ic}")
                    P.op("tensor", mk_mm(par, ic01, 1, pu),
                         reads=tuple(f"gu{par}_{kc}" for kc in range(KCH))
                               + tuple(f"xlnT{kc}" for kc in range(KCH)) + (puk,),
                         writes=(puk,), tag=f"{name}mu{ic}")
                    P.op("scalar", (lambda pg: lambda e: e.activation(
                        out=sg_f[:], in_=pg, func=AF.Silu))(pg),
                        reads=(pgk,), writes=("sg_f",), tag=f"{name}si{ic}")
                    P.op("vector", (lambda ic, pu: lambda e: e.tensor_tensor(
                        out=arA[:, ic * 1024:(ic + 1) * 1024],
                        in0=sg_f[:], in1=pu, op=ALU.mult))(ic, pu),
                        reads=("sg_f", puk), writes=(f"h{ic}",), tag=f"{name}h{ic}")
                if decode:
                    emit_decode_upto(g2 + LOOKAHEAD)

            if decode:
                P.op("gpsimd", lambda e: e.engine_nop(),
                     reads=("dkQ0", "dkQ1"), writes=("x_ch",), tag=f"{name}xfree")

            P.barrier()

            for tp in range(TCH // 2):
                for ic in range(ICH):
                    dsl = (ic % 2) * 2048
                    db = bview(DB_O + dsl, 2048)
                    if decode and tp == 0:
                        P.op("sync", (lambda db, ic: lambda e: e.dma_start(
                            out=db, in_=dn_dec[:, ic * 128:(ic + 1) * 128], transpose=True))(db, ic),
                            reads=tuple(f"decd{t}_{ic // 11}" for t in range(KCH))
                                  + (f"db{ic % 2}",),
                            writes=(f"db{ic % 2}",), dma=f"qdb{ic % 4}", tag=f"{name}dt{tp}_{ic}")
                        P.op("gpsimd", (lambda db, ic: lambda e: e.dma_start(
                            out=dnT_d[ic * 128:(ic + 1) * 128, :], in_=db))(db, ic),
                            reads=(f"db{ic % 2}",), writes=(f"dnTw{ic}",),
                            dma=f"qdw{ic % 4}", tag=f"{name}dw{ic}")
                    else:
                        P.op(("scalar", "gpsimd")[ic % 2], (lambda db, ic: lambda e: e.dma_start(
                            out=db, in_=dnT_d[ic * 128:(ic + 1) * 128, :]))(db, ic),
                            reads=((f"dnTw{ic}",) if decode else ()) + (f"db{ic % 2}",),
                            writes=(f"db{ic % 2}",), dma=f"qdb{ic % 4}", tag=f"{name}dt{tp}_{ic}")

                    def mk_dn(db, ic, tp):
                        def f(e):
                            for half in (0, 1):
                                pd_ = p2a if half == 0 else p2b
                                tc_ = 2 * tp + half
                                lt = arA[:, ic * 1024 + tc_ * 128: ic * 1024 + tc_ * 128 + 128]
                                for hn in range(4):
                                    ins = e.matmul(out=pd_[:, hn * 512:(hn + 1) * 512],
                                                   lhsT=lt, rhs=db[:, hn * 512:(hn + 1) * 512],
                                                   start=(ic == 0), stop=(ic == ICH - 1))
                            return ins
                        return f
                    P.op("tensor", mk_dn(db, ic, tp),
                         reads=(f"db{ic % 2}", f"h{ic}", "psd0", "psd1"),
                         writes=("psd0", "psd1"), tag=f"{name}md{tp}_{ic}")
                epi_cb(2 * tp, p2a, "psd0")
                epi_cb(2 * tp + 1, p2b, "psd1")

        def rms_prep(pd, pdk, tag):
            ssq = st_a[:, 4:5]; ms = st_b[:, 4:5]; rst = st_b[:, 5:6]; hr = st_b[:, 6:7]
            P.op("scalar", (lambda pd: lambda e: e.activation(
                out=xln_tm[:], in_=pd[:], func=AF.Square, accum_out=ssq))(pd),
                reads=(pdk,), writes=("xln_tm", "st_a"), tag=tag)
            P.op("vector", lambda e: e.tensor_scalar(out=ms, in0=ssq, scalar1=1.0 / C,
                                                     scalar2=RMS_EPS, op0=ALU.mult, op1=ALU.add),
                 reads=("st_a",), writes=("st_b",), tag=tag)
            newton_rsqrt(rst, ms, "st_b", tag)
            P.op("vector", lambda e: e.tensor_scalar(out=hr, in0=rst, scalar1=0.5,
                                                     scalar2=None, op0=ALU.mult),
                 reads=("st_b",), writes=("st_b",), tag=tag)
            return hr

        def ffn1_epi(tc, pd, pdk):
            hr = rms_prep(pd, pdk, f"f1r{tc}")
            P.op("sync", (lambda tc: lambda e: e.dma_start(
                out=x_ch[:], in_=x_in[tc * 128:(tc + 1) * 128, :]))(tc),
                reads=("x_ch",), writes=("x_ch",), dma="qx", tag=f"f1el{tc}")
            P.op("vector", (lambda pd: lambda e: e.scalar_tensor_tensor(
                out=x_ch[:], in0=pd[:], scalar=hr, in1=x_ch[:],
                op0=ALU.mult, op1=ALU.add))(pd),
                reads=(pdk, "st_b", "x_ch"), writes=("x_ch", pdk), tag=f"f1ex{tc}")
            P.op("gpsimd", (lambda tc: lambda e: e.dma_start(
                out=x1_d[tc * 128:(tc + 1) * 128, :], in_=x_ch[:]))(tc),
                reads=("x_ch",), writes=(f"x1w{tc}",), dma="qx1", tag=f"f1ew{tc}")
            if stage == 1:
                P.op("gpsimd", (lambda tc: lambda e: e.dma_start(
                    out=out_d[tc * 128:(tc + 1) * 128, :], in_=x_ch[:]))(tc),
                    reads=("x_ch",), writes=(f"outw{tc}",), dma="qout", tag=f"f1eo{tc}")
            else:
                ln_apply(x_ch[:], "x_ch", LN_EPS, f"ln2c{tc}")
                xlnT_build(tc, pd, (pdk,), f"ln2t{tc}")

        def ffn2_epi(tc, pd, pdk):
            hr = rms_prep(pd, pdk, f"f2r{tc}")
            P.op("sync", (lambda tc: lambda e: e.dma_start(
                out=x_ch[:], in_=x2_d[tc * 128:(tc + 1) * 128, :]))(tc),
                reads=("x_ch",), writes=("x_ch",), dma="qx", tag=f"f2el{tc}")
            P.op("vector", (lambda pd: lambda e: e.scalar_tensor_tensor(
                out=x_ch[:], in0=pd[:], scalar=hr, in1=x_ch[:],
                op0=ALU.mult, op1=ALU.add))(pd),
                reads=(pdk, "st_b", "x_ch"), writes=("x_ch", pdk), tag=f"f2ex{tc}")
            P.op("gpsimd", (lambda tc: lambda e: e.dma_start(
                out=out_d[tc * 128:(tc + 1) * 128, :], in_=x_ch[:]))(tc),
                reads=("x_ch",), writes=(f"outw{tc}",), dma="qout", tag=f"f2ew{tc}")

        if stage == 0:
            import os as _os
            _skip_dn = _os.environ.get("SKIP_DN", "0") == "1"
            dn_cnt0 = [0]
            bk0 = [0]
            for g2 in range(22):
                for t in (2 * g2, 2 * g2 + 1):
                    decode_tile("gu", t, f"zG{t}", bk0[0] % 2); bk0[0] += 1
                    decode_tile("gu", 44 + t, f"zU{t}", bk0[0] % 2); bk0[0] += 1
                want = 0 if _skip_dn else (g2 + 1) * 64 // 22
                while dn_cnt0[0] < want:
                    t, q = divmod(dn_cnt0[0], 4)
                    decode_tile("dn", (t, q), f"zD{t}_{q}", bk0[0] % 2); bk0[0] += 1
                    dn_cnt0[0] += 1
            for tc in range(TCH):
                P.op("sync", (lambda tc: lambda e: e.dma_start(
                    out=x_ch[:], in_=x_in[tc * 128:(tc + 1) * 128, :]))(tc),
                    reads=("x_ch",), writes=("x_ch",), dma="qx", tag=f"z{tc}")
                P.op("gpsimd", (lambda tc: lambda e: e.dma_start(
                    out=out_d[tc * 128:(tc + 1) * 128, :], in_=x_ch[:]))(tc),
                    reads=("x_ch",), writes=(f"outw{tc}",), dma="qout", tag=f"zo{tc}")
        dbstg = bview(DB_O, 4096, F32)
        for tc in range(TCH) if stage >= 1 else ():
            P.op("sync", (lambda tc: lambda e: e.dma_start(
                out=dbstg, in_=x_in[tc * 128:(tc + 1) * 128, :]))(tc),
                reads=("db0", "db1"), writes=("db0", "db1"), dma="qx", tag=f"ln1l{tc}")
            ln_apply(dbstg, ("db0", "db1"), LN_EPS, f"ln1c{tc}")
            xlnT_build(tc, (p2a, p2b)[tc % 2],
                       (("psg", "psu"), ("psB",))[tc % 2], f"ln1t{tc}")
        if stage >= 1:
            emit_ffn("f1", True, ffn1_epi)


        def emit_attention():
            QG_O, KG_O, VT_O, KT_O = 0, 8192, 16384, 24576
            YS_O, MS_O, SS_O, YSQ_O = 26624, 34816, 38912, 39936
            YN_O = 40960
            WQ_O = 0
            WSTG_O = 8192
            TB_O = 12288
            AT_O = 16384
            STF_O = 18432
            RG_O = 20480
            SF_O = 24576
            RB_O = 25600
            assert RB_O + 1024 <= 26720

            def aA(off, n, dt=BF16):
                sl = arA[:, off:off + n]
                return sl if dt == BF16 else sl.bitcast(dt)

            pQ = p2a[:, 0:1024]
            pRot = p2a[:, 1024:2048]
            pY = p2b[:, 0:1024]
            pT = p2b[:, 1024:2048]

            def mm_span(e, psum_ap, c0, c1, lhsT_ap, rhs_ap, rhs_off, start, stop):
                c = c0
                ins = None
                while c < c1:
                    ce = min(c1, (c // 512 + 1) * 512)
                    ins = e.matmul(out=psum_ap[:, c:ce], lhsT=lhsT_ap,
                                   rhs=rhs_ap[:, c - rhs_off:ce - rhs_off],
                                   start=start, stop=stop)
                    c = ce
                return ins

            def phase_A(g):
                par = g % 2
                for m_name, m_base in (("v", 2 * C), ("k", C), ("q", 0)):
                    for kc in range(KCH):
                        stg = bview(WSTG_O + (kc % 4) * 1024, 1024, F32)
                        src = w_qkv[kc * 128:(kc + 1) * 128,
                                    m_base + g * 512:m_base + (g + 1) * 512]
                        P.op("sync", (lambda stg, src: lambda e: e.dma_start(
                            out=stg, in_=src))(stg, src),
                            reads=(f"wstg{kc % 4}",), writes=(f"wstg{kc % 4}",),
                            dma=f"qwq{kc % 4}", tag=f"at{g}wl")
                        dst = bview(WQ_O + kc * 512, 512)
                        P.op("scalar", (lambda dst, stg: lambda e: e.activation(
                            out=dst, in_=stg, func=AF.Identity))(dst, stg),
                            reads=(f"wstg{kc % 4}", f"wq{kc}"), writes=(f"wq{kc}",),
                            tag=f"at{g}wc")
                    for hl in range(4):
                        h = 4 * g + hl
                        def mm_qkv(e, hl=hl):
                            ins = None
                            for kc in range(KCH):
                                lt = bview(WQ_O + kc * 512 + hl * 128, 128)
                                for sub in range(2):
                                    ins = e.matmul(out=pQ[:, sub * 512:(sub + 1) * 512],
                                                   lhsT=lt,
                                                   rhs=xlnT[:, kc * NTOK + sub * 512:kc * NTOK + (sub + 1) * 512],
                                                   start=(kc == 0), stop=(kc == KCH - 1))
                            return ins
                        P.op("tensor", mm_qkv,
                             reads=tuple(f"wq{kc}" for kc in range(KCH))
                                   + tuple(f"xlnT{kc}" for kc in range(KCH)) + ("psQ",),
                             writes=("psQ",), tag=f"at{g}m{m_name}{hl}")
                        if m_name == "v":
                            P.op("scalar", lambda e: e.activation(
                                out=xln_tm[:, 0:1024], in_=pQ, func=AF.Identity),
                                reads=("psQ", "vstg"), writes=("vstg", "psQ"), tag=f"at{g}ve{hl}")
                            def mm_vt(e):
                                ins = None
                                for j in range(TCH):
                                    ins = e.matmul(out=pRot[:, j * 128:(j + 1) * 128],
                                                   lhsT=xln_tm[:, j * 128:(j + 1) * 128],
                                                   rhs=eye_s, start=True, stop=True)
                                return ins
                            P.op("tensor", mm_vt, reads=("vstg", "c_eye", "psR"),
                                 writes=("psR",), tag=f"at{g}vT{hl}")
                            vt_dst = aA(VT_O + par * 4096 + hl * 1024, 1024)
                            P.op("scalar", (lambda vt_dst: lambda e: e.activation(
                                out=vt_dst, in_=pRot, func=AF.Identity))(vt_dst),
                                reads=("psR",) + tuple(f"vt{par}_{hl}_{j}" for j in range(TCH)),
                                writes=("psR",) + tuple(f"vt{par}_{hl}_{j}" for j in range(TCH)),
                                tag=f"at{g}vE{hl}")
                        else:
                            tb_c = bview(TB_O, 2048, F32)
                            tb_s = bview(TB_O + 2048, 2048, F32)
                            c_src = (cq_in if m_name == "q" else ck_in)
                            s_src = (sq_in if m_name == "q" else sk_in)
                            P.op("sync", (lambda c_src, h, tb_c: lambda e: e.dma_start(
                                out=tb_c[:, 0:1024], in_=c_src[h * 128:(h + 1) * 128, :]))(c_src, h, tb_c),
                                reads=("tbc",), writes=("tbc",), dma="qtabc", tag=f"at{g}tc")
                            P.op("sync", (lambda s_src, h, tb_s: lambda e: e.dma_start(
                                out=tb_s[:, 0:1024], in_=s_src[h * 128:(h + 1) * 128, :]))(s_src, h, tb_s),
                                reads=("tbs",), writes=("tbs",), dma="qtabs", tag=f"at{g}ts")
                            P.op("scalar", lambda e: e.activation(
                                out=xln_tm[:, 0:1024], in_=pQ, func=AF.Identity),
                                reads=("psQ", "qraw"), writes=("qraw",), tag=f"at{g}xe{m_name}{hl}")
                            def mm_rot(e):
                                e.matmul(out=pRot[:, 0:512], lhsT=pt_s, rhs=xln_tm[:, 0:512],
                                         start=True, stop=True)
                                return e.matmul(out=pRot[:, 512:1024], lhsT=pt_s,
                                                rhs=xln_tm[:, 512:1024], start=True, stop=True)
                            P.op("tensor", mm_rot,
                                 reads=("qraw", "c_pt", "psR"), writes=("psR",),
                                 tag=f"at{g}xr{m_name}{hl}")
                            xc0 = x_ch[:, 0:1024]
                            xc1 = x_ch[:, 1024:2048]
                            P.op("vector", (lambda tb_c: lambda e: e.tensor_tensor(
                                out=xc0, in0=xln_tm[:, 0:1024], in1=tb_c[:, 0:1024], op=ALU.mult))(tb_c),
                                reads=("qraw", "tbc", "xc0"), writes=("xc0", "psQ"), tag=f"at{g}x1{m_name}{hl}")
                            P.op("vector", (lambda tb_s: lambda e: e.tensor_tensor(
                                out=xc1, in0=pRot[:, 0:1024], in1=tb_s[:, 0:1024], op=ALU.mult))(tb_s),
                                reads=("psR", "tbs", "xc1"), writes=("xc1", "psR"), tag=f"at{g}x2{m_name}{hl}")
                            dstq = aA((QG_O if m_name == "q" else KG_O) + par * 4096 + hl * 1024, 1024)
                            P.op("vector", (lambda dstq: lambda e: e.tensor_tensor(
                                out=dstq, in0=xc0, in1=xc1, op=ALU.add))(dstq),
                                reads=("xc0", "xc1", f"{m_name}hat{par}_{hl}"),
                                writes=(f"{m_name}hat{par}_{hl}",),
                                tag=f"at{g}x3{m_name}{hl}")
                            if m_name == "k":
                                def mm_kt(e, dstq=dstq):
                                    ins = None
                                    for j in range(TCH):
                                        ins = e.matmul(out=pQ[:, j * 128:(j + 1) * 128],
                                                       lhsT=dstq[:, j * 128:(j + 1) * 128],
                                                       rhs=eye_s, start=True, stop=True)
                                    return ins
                                P.op("tensor", mm_kt,
                                     reads=(f"khat{par}_{hl}", "c_eye", "psQ"),
                                     writes=("psQ",), tag=f"at{g}kT{hl}")
                                kt_dst = aA(KT_O + par * 1024, 1024)
                                P.op("scalar", (lambda kt_dst: lambda e: e.activation(
                                    out=kt_dst, in_=pQ, func=AF.Identity))(kt_dst),
                                    reads=("psQ",) + tuple(f"kt{par}_{j}" for j in range(TCH)),
                                    writes=("psQ",) + tuple(f"kt{par}_{j}" for j in range(TCH)),
                                    tag=f"at{g}kE{hl}")
                                def mm_M(e, par=par, hl=hl):
                                    ins = None
                                    for j in range(TCH):
                                        ins = e.matmul(
                                            out=pRot[:, 0:128],
                                            lhsT=arA[:, KT_O + par * 1024 + j * 128:KT_O + par * 1024 + (j + 1) * 128],
                                            rhs=arA[:, VT_O + par * 4096 + hl * 1024 + j * 128:VT_O + par * 4096 + hl * 1024 + (j + 1) * 128],
                                            start=(j == 0), stop=(j == TCH - 1))
                                    return ins
                                P.op("tensor", mm_M,
                                     reads=tuple(f"kt{par}_{j}" for j in range(TCH))
                                           + tuple(f"vt{par}_{hl}_{j}" for j in range(TCH)) + ("psR",),
                                     writes=("psR",), tag=f"at{g}M{hl}")
                                msl = aA(MS_O + (4 * g + hl) * 256, 256, F32)
                                P.op("vector", (lambda msl, h: lambda e: e.tensor_scalar(
                                    out=msl, in0=pRot[:, 0:128], scalar1=GAMMA_P1024[h],
                                    scalar2=None, op0=ALU.mult))(msl, h),
                                    reads=("psR", f"ms{h}"), writes=(f"ms{h}", "psR"), tag=f"at{g}Me{hl}")

            def phase_B(g):
                msl_all = arA[:, MS_O + 4 * g * 256: MS_O + (4 * g + 4) * 256].bitcast(F32)
                P.op("gpsimd", (lambda msl_all, g: lambda e: e.dma_start(
                    out=cc_i[g].rearrange("(hl dk) dv -> dk hl dv", dk=128),
                    in_=msl_all.rearrange("dk (hl dv) -> dk hl dv", hl=4)))(msl_all, g),
                    reads=tuple(f"ms{4 * g + hl}" for hl in range(4)),
                    writes=(f"cci{g}",), dma="qcc", tag=f"cc{g}o")
                P.op("gpsimd", (lambda g: lambda e: e.collective_compute(
                    "AllGather", ALU.bypass,
                    replica_groups=[[0, 1], [2, 3], [4, 5], [6, 7]],
                    ins=[cc_i[g].opt()], outs=[cc_o[g].opt()]))(g),
                    reads=(f"cci{g}",), writes=(f"cco{g}",), inc=1, tag=f"cc{g}c")
                sf = bview(SF_O, 1024, F32)
                P.op("sync", (lambda sf, g: lambda e: e.dma_start(
                    out=sf.rearrange("dk (hl dv) -> dk hl dv", hl=4),
                    in_=cc_o[g][0:512, :].rearrange("(hl dk) dv -> dk hl dv", dk=128)))(sf, g),
                    reads=(f"cco{g}", "sfst"), writes=("sfst",), dma="qsf", tag=f"cc{g}l")
                ssl = aA(SS_O + (g % 2) * 512, 512)
                P.op("vector", (lambda ssl, sf: lambda e: e.tensor_scalar(
                    out=ssl, in0=sf, scalar1=alpha[:, 0:1], scalar2=None,
                    op0=ALU.mult))(ssl, sf),
                    reads=("sfst", "c_al", f"shat{g % 2}"), writes=(f"shat{g % 2}",), tag=f"cc{g}s")

            def phase_C(g):
                par = g % 2
                stf = bview(STF_O, 2048, F32)[0:8, :]
                P.op("vector", (lambda stf: lambda e: e.memset(stf, 0.0))(stf),
                     reads=("stf",), writes=("stf",), tag=f"at{g}z")
                for hl in range(4):
                    qh = aA(QG_O + par * 4096 + hl * 1024, 1024)
                    kh = aA(KG_O + par * 4096 + hl * 1024, 1024)
                    ysb = aA(YS_O + par * 4096 + hl * 1024, 1024)
                    ssl_ap = aA(SS_O + par * 512 + hl * 128, 128)
                    P.op("tensor", (lambda ssl_ap, qh: lambda e: mm_span(
                        e, pY, 0, 1024, ssl_ap, qh, 0, True, False))(ssl_ap, qh),
                        reads=(f"shat{par}", f"qhat{par}_{hl}", "psY"),
                        writes=("psY",), tag=f"at{g}cx{hl}")
                    for j in range(TCH):
                        Nj = 1024 - 128 * j
                        atb = bview(AT_O + (j % 2) * 1024, 1024)
                        P.op("tensor", (lambda kh, qh, j, Nj: lambda e: mm_span(
                            e, pT, 0, Nj,
                            kh[:, j * 128:(j + 1) * 128],
                            qh, -128 * j, True, True))(kh, qh, j, Nj),
                            reads=(f"khat{par}_{hl}", f"qhat{par}_{hl}", "psT"),
                            writes=("psT",), tag=f"at{g}sc{hl}_{j}")
                        P.op("scalar", (lambda atb, Nj: lambda e: e.activation(
                            out=atb[:, 0:Nj], in_=pT[:, 0:Nj], func=AF.Identity))(atb, Nj),
                            reads=("psT", f"atb{j % 2}"), writes=(f"atb{j % 2}", "psT"),
                            tag=f"at{g}er{hl}_{j}")
                        P.op("vector", (lambda atb: lambda e: e.tensor_tensor(
                            out=atb[:, 0:128], in0=atb[:, 0:128], in1=tri_s, op=ALU.mult))(atb),
                            reads=(f"atb{j % 2}", "c_tri"), writes=(f"atb{j % 2}",),
                            tag=f"at{g}ed{hl}_{j}")
                        P.op("tensor", (lambda atb, j, par, hl: lambda e: mm_span(
                            e, pY, 128 * j, 1024,
                            arA[:, VT_O + par * 4096 + hl * 1024 + j * 128:VT_O + par * 4096 + hl * 1024 + (j + 1) * 128],
                            atb, 128 * j, False, (j == TCH - 1)))(atb, j, par, hl),
                            reads=(f"atb{j % 2}", f"vt{par}_{hl}_{j}", "psY"),
                            writes=("psY",), tag=f"at{g}av{hl}_{j}")
                    P.op("scalar", (lambda ysb: lambda e: e.activation(
                        out=ysb, in_=pY, func=AF.Identity))(ysb),
                        reads=("psY", f"ysb{par}_{hl}"), writes=(f"ysb{par}_{hl}", "psY"),
                        tag=f"at{g}ye{hl}")
                    ysq = aA(YSQ_O, 1024)
                    P.op("scalar", (lambda ysq, ysb: lambda e: e.activation(
                        out=ysq, in_=ysb, func=AF.Square))(ysq, ysb),
                        reads=(f"ysb{par}_{hl}", "ysq"), writes=("ysq",), tag=f"at{g}yq{hl}")
                    def mm_st(e, hl=hl, ysb=ysb):
                        ysq_ = aA(YSQ_O, 1024)
                        ins = None
                        for sub in range(2):
                            ins = e.matmul(out=p2b[0:8, 1024 + sub * 512:1024 + (sub + 1) * 512],
                                           lhsT=sel_s[:, hl * 8:(hl + 1) * 8],
                                           rhs=ysb[:, sub * 512:(sub + 1) * 512],
                                           start=True, stop=False)
                        for sub in range(2):
                            ins = e.matmul(out=p2b[0:8, 1024 + sub * 512:1024 + (sub + 1) * 512],
                                           lhsT=sel_s[:, (4 + hl) * 8:(5 + hl) * 8],
                                           rhs=ysq_[:, sub * 512:(sub + 1) * 512],
                                           start=False, stop=True)
                        return ins
                    P.op("tensor", mm_st,
                         reads=(f"ysb{par}_{hl}", "ysq", "c_sel", "psT"),
                         writes=("psT",), tag=f"at{g}st{hl}")
                    P.op("vector", (lambda stf: lambda e: e.tensor_tensor(
                        out=stf, in0=stf, in1=p2b[0:8, 1024:2048], op=ALU.add))(stf),
                        reads=("stf", "psT"), writes=("stf", "psT"),
                        tag=f"at{g}sa{hl}")

            def phase_D(g):
                par = g % 2
                stf = bview(STF_O, 2048, F32)[0:8, :]
                RGv = bview(RG_O, 4096, F32)
                ex24 = RGv[0:4, 0:1024]
                varc = RGv[0:4, 1024:2048]
                mu = stf[0:4, :]
                t4 = arA[:, 41984:44032].bitcast(F32)[0:4, :]
                Rb = bview(RB_O, 1024)[0:4, :]
                Mb = arA[:, 44032:45056][0:4, :]
                P.op("vector", lambda e: e.tensor_scalar(out=stf, in0=stf, scalar1=1.0 / D,
                                                         scalar2=None, op0=ALU.mult),
                     reads=("stf",), writes=("stf",), tag=f"gn{g}")
                P.op("gpsimd", lambda e: e.dma_start(out=strow_d, in_=stf),
                     reads=("stf", "strd"), writes=("strd",), dma="qst", tag=f"gn{g}")
                P.op("sync", lambda e: e.dma_start(out=ex24, in_=strow_d[4:8, :]),
                     reads=("strd", "rg"), writes=("rg",), dma="qst2", tag=f"gn{g}")
                P.op("vector", lambda e: e.tensor_tensor(out=t4, in0=mu, in1=mu, op=ALU.mult),
                     reads=("stf", "t4"), writes=("t4",), tag=f"gn{g}")
                P.op("vector", lambda e: e.tensor_tensor(out=ex24, in0=ex24, in1=t4, op=ALU.subtract),
                     reads=("rg", "t4"), writes=("rg",), tag=f"gn{g}")
                P.op("vector", lambda e: e.tensor_scalar(out=ex24, in0=ex24, scalar1=1.0,
                                                         scalar2=GN_EPS, op0=ALU.mult, op1=ALU.add),
                     reads=("rg",), writes=("rg",), tag=f"gn{g}")
                P.op("vector", lambda e: e.tensor_copy(out=varc, in_=ex24),
                     reads=("rg",), writes=("rg",), tag=f"gn{g}")
                di = ex24.bitcast(U32)
                P.op("vector", lambda e: e.tensor_scalar(out=di, in0=varc.bitcast(U32), scalar1=1,
                                                         scalar2=None, op0=ALU.logical_shift_right),
                     reads=("rg",), writes=("rg",), tag=f"gn{g}")
                P.op("vector", lambda e: e.tensor_scalar(out=di, in0=di, scalar1=-1, scalar2=0x5F3759DF,
                                                         op0=ALU.mult, op1=ALU.add),
                     reads=("rg",), writes=("rg",), tag=f"gn{g}")
                for _ in range(3):
                    P.op("vector", lambda e: e.tensor_tensor(out=t4, in0=ex24, in1=ex24, op=ALU.mult),
                         reads=("rg", "t4"), writes=("t4",), tag=f"gn{g}")
                    P.op("vector", lambda e: e.tensor_tensor(out=t4, in0=t4, in1=varc, op=ALU.mult),
                         reads=("t4", "rg"), writes=("t4",), tag=f"gn{g}")
                    P.op("vector", lambda e: e.tensor_scalar(out=t4, in0=t4, scalar1=-0.5, scalar2=1.5,
                                                             op0=ALU.mult, op1=ALU.add),
                         reads=("t4",), writes=("t4",), tag=f"gn{g}")
                    P.op("vector", lambda e: e.tensor_tensor(out=ex24, in0=ex24, in1=t4, op=ALU.mult),
                         reads=("rg", "t4"), writes=("rg",), tag=f"gn{g}")
                P.op("vector", lambda e: e.tensor_tensor(out=mu, in0=mu, in1=ex24, op=ALU.mult),
                     reads=("stf", "rg"), writes=("stf",), tag=f"gn{g}")
                P.op("vector", lambda e: e.tensor_copy(out=Rb, in_=ex24),
                     reads=("rg", "rb"), writes=("rb",), tag=f"gn{g}")
                P.op("vector", lambda e: e.tensor_copy(out=Mb, in_=mu),
                     reads=("stf", "mb"), writes=("mb",), tag=f"gn{g}")
                for hl in range(4):
                    h = 4 * g + hl
                    ysb = aA(YS_O + par * 4096 + hl * 1024, 1024)
                    def mm_b(e, hl, rhs_tile, pdst):
                        ins = None
                        for sub in range(2):
                            ins = e.matmul(out=pdst[:, sub * 512:(sub + 1) * 512],
                                           lhsT=brc4_s[:, hl * 128:(hl + 1) * 128],
                                           rhs=rhs_tile[:, sub * 512:(sub + 1) * 512],
                                           start=True, stop=True)
                        return ins
                    P.op("tensor", (lambda hl: lambda e: mm_b(e, hl, Rb, pT))(hl),
                         reads=("rb", "c_b4", "psT"), writes=("psT",), tag=f"gn{g}bR{hl}")
                    P.op("tensor", (lambda hl: lambda e: mm_b(e, hl, Mb, pY))(hl),
                         reads=("mb", "c_b4", "psY"), writes=("psY",), tag=f"gn{g}bM{hl}")
                    xc0 = x_ch[:, 0:1024]
                    xc1 = x_ch[:, 1024:2048]
                    P.op("vector", (lambda ysb: lambda e: e.tensor_tensor(
                        out=xc0, in0=ysb, in1=pT, op=ALU.mult))(ysb),
                        reads=(f"ysb{par}_{hl}", "psT", "xc0"), writes=("xc0", "psT"),
                        tag=f"gn{g}a1{hl}")
                    P.op("vector", lambda e: e.tensor_tensor(
                        out=xc1, in0=xc0, in1=pY, op=ALU.subtract),
                        reads=("xc0", "psY", "xc1"), writes=("xc1", "psY"),
                        tag=f"gn{g}a2{hl}")
                    yn = aA(YN_O, 1024)
                    P.op("scalar", (lambda h, yn: lambda e: e.activation(
                        out=yn, in_=xc1, func=AF.Identity,
                        bias=gnwb[:, H + h:H + h + 1], scale=gnwb[:, h:h + 1]))(h, yn),
                        reads=("xc1", "c_gw", "c_gb", "yn"), writes=("yn",), tag=f"gn{g}a3{hl}")
                    for kc in range(KCH):
                        stg = bview(WSTG_O + (kc % 4) * 1024, 256, F32)
                        src = w_g[kc * 128:(kc + 1) * 128, h * 128:(h + 1) * 128]
                        P.op("sync", (lambda stg, src: lambda e: e.dma_start(
                            out=stg, in_=src))(stg, src),
                            reads=(f"wstg{kc % 4}",), writes=(f"wstg{kc % 4}",),
                            dma=f"qwq{kc % 4}", tag=f"gn{g}gl{hl}")
                        dst = bview(WQ_O + kc * 512, 128)
                        P.op("scalar", (lambda dst, stg: lambda e: e.activation(
                            out=dst, in_=stg, func=AF.Identity))(dst, stg),
                            reads=(f"wstg{kc % 4}", f"wq{kc}"), writes=(f"wq{kc}",),
                            tag=f"gn{g}gc{hl}")
                    def mm_g(e):
                        ins = None
                        for kc in range(KCH):
                            for sub in range(2):
                                ins = e.matmul(out=pQ[:, sub * 512:(sub + 1) * 512],
                                               lhsT=bview(WQ_O + kc * 512, 128),
                                               rhs=xlnT[:, kc * NTOK + sub * 512:kc * NTOK + (sub + 1) * 512],
                                               start=(kc == 0), stop=(kc == KCH - 1))
                        return ins
                    P.op("tensor", mm_g,
                         reads=tuple(f"wq{kc}" for kc in range(KCH))
                               + tuple(f"xlnT{kc}" for kc in range(KCH)) + ("psQ",),
                         writes=("psQ",), tag=f"gn{g}mg{hl}")
                    P.op("scalar", lambda e: e.activation(out=sg_f[:], in_=pQ, func=AF.Silu),
                         reads=("psQ", "sg_f"), writes=("sg_f", "psQ"), tag=f"gn{g}si{hl}")
                    ogb = aA(YSQ_O, 1024)
                    P.op("vector", (lambda ogb, yn: lambda e: e.tensor_tensor(
                        out=ogb, in0=sg_f[:], in1=yn, op=ALU.mult))(ogb, yn),
                        reads=("sg_f", "yn", "ysq"), writes=("ysq",), tag=f"gn{g}og{hl}")
                    P.op("gpsimd", (lambda ogb, h: lambda e: e.dma_start(
                        out=og_d[h * 128:(h + 1) * 128, :], in_=ogb))(ogb, h),
                        reads=("ysq",), writes=(f"ogw{h}", "ysq"), dma="qog", tag=f"gn{g}ow{hl}")

            phase_A(0); phase_B(0)
            phase_A(1)
            phase_C(0); phase_D(0)
            phase_B(1)
            phase_A(2)
            phase_C(1); phase_D(1)
            phase_B(2)
            phase_A(3)
            phase_C(2); phase_D(2)
            phase_B(3)
            phase_C(3); phase_D(3)

            P.barrier()
            OGT_O = 0
            WP_O = 8192
            WPB_O = 12288
            for tp in range(4):
                for kc in range(KCH):
                    ogt = bview(OGT_O + (tp % 2) * 4096 + kc * 256, 256)
                    P.op("sync", (lambda ogt, kc, tp: lambda e: e.dma_start(
                        out=ogt, in_=og_d[kc * 128:(kc + 1) * 128, tp * 256:(tp + 1) * 256]))(ogt, kc, tp),
                        reads=tuple(f"ogw{h}" for h in range(H)) + (f"ogt{tp % 2}_{kc}",),
                        writes=(f"ogt{tp % 2}_{kc}",), dma=f"qot{kc % 4}", tag=f"pj{tp}o")
                for kc in range(KCH):
                    wpb = bview(WPB_O + (kc % 2) * 2048, 2048)
                    if tp == 0:
                        for h2 in (0, 1):
                            stgp = bview(WSTG_O + h2 * 2048, 2048, F32)
                            skeys = (f"wstg{2 * h2}", f"wstg{2 * h2 + 1}")
                            P.op("sync", (lambda stgp, kc, h2: lambda e: e.dma_start(
                                out=stgp, in_=w_proj[kc * 128:(kc + 1) * 128,
                                                     h2 * 1024:(h2 + 1) * 1024]))(stgp, kc, h2),
                                reads=skeys, writes=skeys,
                                dma=("qwp", "qwp2")[h2], tag=f"pj0l{kc}")
                            P.op("scalar", (lambda wpb, stgp, h2: lambda e: e.activation(
                                out=wpb[:, h2 * 1024:(h2 + 1) * 1024], in_=stgp,
                                func=AF.Identity))(wpb, stgp, h2),
                                reads=skeys + (f"wpb{kc % 2}",), writes=(f"wpb{kc % 2}",),
                                tag=f"pj0c{kc}")
                        P.op("gpsimd", (lambda wpb, kc: lambda e: e.dma_start(
                            out=wproj_bf[kc * 128:(kc + 1) * 128, :], in_=wpb))(wpb, kc),
                            reads=(f"wpb{kc % 2}",), writes=(f"wpbd{kc}",),
                            dma="qwpc", tag=f"pj0s{kc}")
                    else:
                        P.op("sync", (lambda wpb, kc: lambda e: e.dma_start(
                            out=wpb, in_=wproj_bf[kc * 128:(kc + 1) * 128, :]))(wpb, kc),
                            reads=(f"wpbd{kc}", f"wpb{kc % 2}"), writes=(f"wpb{kc % 2}",),
                            dma=("qwp", "qwp2")[kc % 2], tag=f"pj{tp}l{kc}")
                    def mm_p(e, kc=kc, tp=tp, wpb=wpb):
                        ins = None
                        for half in (0, 1):
                            pdst = p2a if half == 0 else p2b
                            for sub in range(4):
                                ins = e.matmul(
                                    out=pdst[:, sub * 512:(sub + 1) * 512],
                                    lhsT=bview(OGT_O + (tp % 2) * 4096 + kc * 256 + half * 128, 128),
                                    rhs=wpb[:, sub * 512:(sub + 1) * 512],
                                    start=(kc == 0), stop=(kc == KCH - 1))
                        return ins
                    P.op("tensor", mm_p,
                         reads=(f"ogt{tp % 2}_{kc}", f"wpb{kc % 2}", "psP0", "psP1"),
                         writes=("psP0", "psP1"), tag=f"pj{tp}m{kc}")
                for half in (0, 1):
                    tc = 2 * tp + half
                    pdst, pk_ = (p2a, "psP0") if half == 0 else (p2b, "psP1")
                    P.op("sync", (lambda tc: lambda e: e.dma_start(
                        out=x_ch[:], in_=x1_d[tc * 128:(tc + 1) * 128, :]))(tc),
                        reads=("x_ch",), writes=("x_ch",), dma="qx", tag=f"pj{tp}x{half}")
                    P.op("vector", (lambda pdst: lambda e: e.tensor_tensor(
                        out=x_ch[:], in0=x_ch[:], in1=pdst[:, 0:2048], op=ALU.add))(pdst),
                        reads=(pk_, "x_ch"), writes=("x_ch", pk_), tag=f"pj{tp}a{half}")
                    P.op("gpsimd", (lambda tc: lambda e: e.dma_start(
                        out=x2_d[tc * 128:(tc + 1) * 128, :], in_=x_ch[:]))(tc),
                        reads=("x_ch",), writes=(f"x2w{tc}",), dma="qx1", tag=f"pj{tp}w{half}")
                    if stage == 2:
                        P.op("gpsimd", (lambda tc: lambda e: e.dma_start(
                            out=out_d[tc * 128:(tc + 1) * 128, :], in_=x_ch[:]))(tc),
                            reads=("x_ch",), writes=(f"outw{tc}",), dma="qout", tag=f"pj{tp}q{half}")
                    else:
                        ln_apply(x_ch[:], "x_ch", LN_EPS, f"ln3c{tc}")
                        xlnT_build(tc, pdst, (pk_,), f"ln3t{tc}")

        if stage >= 2:
            P.barrier()
            emit_attention()


        if stage >= 3:
            P.barrier()
            emit_ffn("f2", False, ffn2_epi)

        if _DEBUG:
            d_xln = nc.dram_tensor("d_xln", [NTOK, C], BF16, kind="ExternalOutput").ap()
            d_w = nc.dram_tensor("d_w", [256, C], BF16, kind="ExternalOutput").ap()
            d_h = nc.dram_tensor("d_h", [128, 2048], BF16, kind="ExternalOutput").ap()
            P.barrier()
            P.op("gpsimd", lambda e: e.dma_start(out=d_xln, in_=xln_d), writes=("dbg1",), dma="qdbg")
            P.op("gpsimd", lambda e: e.dma_start(out=d_w[0:128, :], in_=gu_dec[128:256, :]), writes=("dbg2",), dma="qdbg")
            P.op("gpsimd", lambda e: e.dma_start(out=d_w[128:256, :], in_=gu_dec[FI + 128:FI + 256, :]), writes=("dbg3",), dma="qdbg")
            P.op("gpsimd", lambda e: e.dma_start(out=d_h, in_=arA[:, 0:2048]), reads=("h0", "h1"), writes=("dbg4",), dma="qdbg")
        P.barrier()
        P.op("gpsimd", lambda e: e.engine_nop(),
             reads=tuple(f"outw{tc}" for tc in range(TCH)) +
                   (("dbg1", "dbg2", "dbg3", "dbg4") if _DEBUG else ()),
             writes=(), tag="fin")

        for st_name in sorted(P.streams):
            sems[st_name] = es.enter_context(nc.semaphore(f"sd_{st_name}"))
        P.schedule()
        n, w = P.stats()
        build_program.last_stats = (n, w)

        with nc.Block() as block:
            @block.sync
            def _(eng):
                P.emit_stream("sync", eng, sems)

            @block.vector
            def _(eng):
                P.emit_stream("vector", eng, sems)

            @block.scalar
            def _(eng):
                P.emit_stream("scalar", eng, sems)

            @block.tensor
            def _(eng):
                P.emit_stream("tensor", eng, sems)

            @block.gpsimd
            def _(eng):
                P.emit_stream("gpsimd", eng, sems)

    return nc





def kernel(L0_attn_x, L0_attn_w_qkv, L0_attn_w_g, L0_attn_w_proj,
           L0_attn_gn_weight, L0_attn_gn_bias,
           L0_ffn_gate_up_packed, L0_ffn_gate_up_scales,
           L0_ffn_down_packed, L0_ffn_down_scales):
    x = np.asarray(L0_attn_x, dtype=np.float32)
    wqkv = np.ascontiguousarray(np.asarray(L0_attn_w_qkv, dtype=np.float32))
    wg = np.ascontiguousarray(np.asarray(L0_attn_w_g, dtype=np.float32))
    wproj = np.ascontiguousarray(np.asarray(L0_attn_w_proj, dtype=np.float32))
    gnw = np.ascontiguousarray(np.asarray(L0_attn_gn_weight, np.float32).reshape(H, 128).T)
    gnb = np.ascontiguousarray(np.asarray(L0_attn_gn_bias, np.float32).reshape(H, 128).T)
    gu_pk = np.ascontiguousarray(np.asarray(L0_ffn_gate_up_packed, np.int32))
    gu_sc = np.ascontiguousarray(np.asarray(L0_ffn_gate_up_scales).T)
    dn_pk = np.ascontiguousarray(np.asarray(L0_ffn_down_packed, np.int32))
    dn_sc = np.ascontiguousarray(np.asarray(L0_ffn_down_scales).T)
    PT, tri, sel, brc4, eye = _host_consts()
    tabs = {0: _host_tables(0), 1: _host_tables(1)}

    nc = build_program(_STAGE)
    in_maps = []
    for c in range(8):
        b, half = c // 2, c % 2
        cq, sq, ck, sk = tabs[half]
        in_maps.append({
            "x_own": np.ascontiguousarray(x[b, half * NTOK:(half + 1) * NTOK, :]),
            "w_qkv": wqkv, "w_g": wg, "w_proj": wproj,
            "gnw": gnw, "gnb": gnb,
            "gu_pk": gu_pk, "gu_sc": gu_sc, "dn_pk": dn_pk, "dn_sc": dn_sc,
            "cq": cq, "sq": sq, "ck": ck, "sk": sk,
            "pt": PT, "tri": tri, "sel": sel, "eye": eye, "brc4": brc4,
            "alpha": np.full((128, 1), float(half), np.float32),
        })
    res = run_bass_kernel_spmd(nc, in_maps, core_ids=list(range(8)))
    out = np.empty((B, T, C), np.float32)
    for c in range(8):
        b, half = c // 2, c % 2
        out[b, half * NTOK:(half + 1) * NTOK, :] = res.results[c]["out"]
    return out
